# version 06
# baseline (speedup 1.0000x reference)
import contextlib
import numpy as np
import ml_dtypes
import concourse.bass as bass
import concourse.mybir as mybir
from concourse.bass_utils import run_bass_kernel_spmd

F32 = mybir.dt.float32
BF16 = mybir.dt.bfloat16
AF = mybir.ActivationFunctionType
ALU = mybir.AluOpType
AX = mybir.AxisListType

T, D, NT, KD = 2048, 1024, 16, 8
NEG = -1.0e30
N_CORES = 8
DBG = {}



_CHAINED = {
    "tensor_copy", "tensor_tensor", "tensor_add", "tensor_sub", "tensor_mul",
    "tensor_max", "tensor_scalar", "tensor_single_scalar", "tensor_scalar_add",
    "tensor_scalar_sub", "tensor_scalar_mul", "tensor_scalar_max",
    "tensor_scalar_min", "tensor_relu", "tensor_reduce", "reduce_max",
    "reduce_sum", "scalar_tensor_tensor", "tensor_tensor_scan", "memset",
    "memzero", "reciprocal", "select", "copy_predicated", "bn_stats", "bn_aggr",
    "pool", "pool_avg", "pool_max", "stream_shuffle", "match_replace",
    "activation", "sign", "square", "sqrt", "mul", "copy", "add", "iota",
    "tensor_tensor_reduce", "max", "max_index",
}


class EngProxy:
    def __init__(self, p, ename, raw):
        self._p = p
        self._ename = ename
        self._raw = raw

    def __getattr__(self, name):
        real = getattr(self._raw, name)
        if name not in _CHAINED:
            return real
        p, en = self._p, self._ename

        def wrapped(*a, **k):
            p.need(en, p.tip(en))
            ins = real(*a, **k)
            p.mark(ins, en)
            return ins
        return wrapped


class Prog:
    ENGS = ("pe", "v", "a", "g", "s")

    def __init__(self):
        self.nc = bass.Bass("TRN2", target_bir_lowering=False, debug=False)
        raw = {"pe": self.nc.tensor, "v": self.nc.vector, "a": self.nc.scalar,
               "g": self.nc.gpsimd, "s": self.nc.sync}
        self.eng = {k: EngProxy(self, k, v) for k, v in raw.items()}
        self.top = contextlib.ExitStack()
        self.sems = {}
        self.cnt = {}
        self.hwm = {}
        self.stage_stack = None
        self._uid = 0
        self._consts = {}
        self._stage_id = 0
        self._marked = {}
        self._keep = []

    def constf(self, val, dtype=F32):
        key = (self._stage_id, float(val), dtype)
        if key not in self._consts:
            stack = self.stage_stack if self.stage_stack is not None else self.top
            tsb = stack.enter_context(
                self.nc.sbuf_tensor(f"cst{self.uid()}", [128, 1], dtype))
            d = self.mark(self.eng["v"].memset(tsb[:, :], float(val)), "v")
            for en in self.ENGS:
                self.need(en, d)
            self._consts[key] = tsb
        return self._consts[key][:, 0:1]

    def uid(self):
        self._uid += 1
        return self._uid

    def sem(self, name):
        if name not in self.sems:
            self.sems[name] = self.top.enter_context(self.nc.semaphore(name))
            self.cnt[name] = 0
        return self.sems[name]

    def mark(self, ins, ename, n=1):
        key = id(ins)
        if key in self._marked:
            return self._marked[key]
        sname = "m_" + ename
        ins.then_inc(self.sem(sname), n)
        self.cnt[sname] += n
        dep = (sname, self.cnt[sname])
        self._marked[key] = dep
        self._keep.append(ins)
        return dep

    def tip(self, ename):
        sname = "m_" + ename
        self.sem(sname)
        return (sname, self.cnt[sname])

    def need(self, ename, *deps):
        stack = list(deps)
        while stack:
            dep = stack.pop()
            if dep is None or len(dep) == 0:
                continue
            if not isinstance(dep[0], str):
                stack.extend(dep)
                continue
            sname, v = dep
            if v <= 0:
                continue
            key = (ename, sname)
            if self.hwm.get(key, 0) < v:
                self.eng[ename].wait_ge(self.sems[sname], v)
                self.hwm[key] = v

    def dma(self, stream, out, in_, deps=(), **kw):
        ename, suffix = stream
        sname = "d_" + suffix
        self.sem(sname)
        if self.cnt[sname] > 0:
            self.need(ename, (sname, self.cnt[sname]))
        self.need(ename, *deps)
        ins = self.eng[ename].dma_start(out, in_, **kw)
        ins.then_inc(self.sems[sname], 16)
        self.cnt[sname] += 16
        return (sname, self.cnt[sname])

    def bar(self):
        s = self.sem("bar")
        for en in self.ENGS:
            self.need(en, self.tip(en))
            self.eng[en].sem_inc(s, 1)
        self.cnt["bar"] += len(self.ENGS)
        tgt = self.cnt["bar"]
        dma_deps = [(sn, v) for sn, v in self.cnt.items() if sn.startswith("d_") and v > 0]
        for en in self.ENGS:
            self.need(en, ("bar", tgt), *dma_deps)

    @contextlib.contextmanager
    def stage(self, name):
        st = contextlib.ExitStack()
        prev = self.stage_stack
        self.stage_stack = st
        self._stage_id += 1
        try:
            yield st
        finally:
            self.bar()
            st.close()
            self.stage_stack = prev
            self._stage_id += 1

    def sb(self, name, shape, dtype, stack=None, side=None):
        right = stack is not None
        stack = stack or (self.stage_stack if self.stage_stack is not None else self.top)
        return stack.enter_context(
            self.nc.sbuf_tensor(f"{name}_{self.uid()}", list(shape), dtype,
                                side=("right" if right else None) if side is None else side))

    def ps(self, name, shape, dtype=F32):
        stack = self.stage_stack if self.stage_stack is not None else self.top
        esz = 2 if dtype == BF16 else 4
        free = 1
        for s in shape[1:]:
            free *= s
        pad_free = ((free * esz + 2047) // 2048) * 2048 // esz
        if pad_free != free:
            shape = [shape[0], pad_free]
        return stack.enter_context(
            self.nc.psum_tensor(f"{name}_{self.uid()}", list(shape), dtype))



class Ctx:

    def __init__(self, p: Prog, host_arrays: dict, fused: bool):
        self.p = p
        self.host = host_arrays
        self.fused = fused
        self.decl = {}
        self.kind = {}
        self.buf_shape = {}
        self.buf_mark = {}
        self.cst = {}

    def W(self, name):
        if name not in self.decl:
            arr = self.host[name]
            self.decl[name] = self.p.nc.dram_tensor(
                name, list(arr.shape), F32, kind="ExternalInput").ap()
            self.kind[name] = "ExternalInput"
        return self.decl[name]

    def buf(self, name, shape=None, dtype=F32, write=False, final=False, external_in=False):
        if name not in self.decl:
            assert shape is not None, f"first access to {name} needs shape"
            if self.fused:
                kind = ("ExternalOutput" if final else
                        ("ExternalInput" if external_in else "Internal"))
            else:
                kind = "ExternalOutput" if write else "ExternalInput"
            self.decl[name] = self.p.nc.dram_tensor(
                name, list(shape), dtype, kind=kind).ap()
            self.kind[name] = kind
            self.buf_shape[name] = (tuple(shape), dtype)
        return self.decl[name]

    def wrote(self, name, dep):
        self.buf_mark[name] = dep

    def rdep(self, name):
        return self.buf_mark.get(name)



def host_constants():
    c = {}
    ii = np.arange(128)
    c["cst_ident"] = np.eye(128, dtype=np.float32)
    c["cst_causal_add"] = np.where(ii[None, :] <= ii[:, None], 0.0, NEG).astype(np.float32)
    c["cst_lower_ji"] = (ii[None, :] >= ii[:, None]).astype(np.float32)
    c["cst_ones_row"] = np.ones((1, 128), np.float32)
    c["cst_ones_col"] = np.ones((128, 1), np.float32)
    dh = 64
    inv = 1.0 / 10000.0 ** (np.arange(0, dh, 2, dtype=np.float64) / dh)
    ang = np.arange(T, dtype=np.float64)[:, None] * inv[None, :]
    c["cst_rope0_sin"] = np.sin(ang).astype(np.float32)
    c["cst_rope0_cos"] = np.cos(ang).astype(np.float32)
    rd = 32
    inv2 = 1.0 / 10000.0 ** (np.arange(0, rd, 2, dtype=np.float64) / rd)
    f2 = np.arange(T, dtype=np.float64)[:, None] * inv2[None, :]
    emb = np.concatenate([f2, f2], axis=1)
    c["cst_rope2_sin"] = np.sin(emb).astype(np.float32)
    c["cst_rope2_cos"] = np.cos(emb).astype(np.float32)
    c["cst_egroup"] = (np.arange(8)[None, :] // 4).astype(np.float32)
    return c


def load_constants(ctx: Ctx):
    p = ctx.p
    cst = ctx.cst
    cst["ident_bf"] = p.sb("ident_bf", (128, 128), BF16)
    cst["causal_add"] = p.sb("causal_add", (128, 128), F32)
    cst["lower_ji_bf"] = p.sb("lower_ji_bf", (128, 128), BF16)
    cst["lower_ji_f32"] = p.sb("lower_ji_f32", (128, 128), F32)
    cst["ones_row_f32"] = p.sb("ones_row_f32", (1, 128), F32)
    cst["ones_row_bf"] = p.sb("ones_row_bf", (1, 128), BF16)
    cst["ones_col_bf"] = p.sb("ones_col_bf", (128, 1), BF16)
    cst["ones_col_f32"] = p.sb("ones_col_f32", (128, 1), F32)
    deps = [
        p.dma(("g", "cload"), cst["ident_bf"][:, :], ctx.W("cst_ident")[:, :]),
        p.dma(("g", "cload"), cst["causal_add"][:, :], ctx.W("cst_causal_add")[:, :]),
        p.dma(("g", "cload"), cst["lower_ji_bf"][:, :], ctx.W("cst_lower_ji")[:, :]),
        p.dma(("g", "cload"), cst["lower_ji_f32"][:, :], ctx.W("cst_lower_ji")[:, :]),
        p.dma(("g", "cload"), cst["ones_row_f32"][:, :], ctx.W("cst_ones_row")[:, :]),
        p.dma(("g", "cload"), cst["ones_row_bf"][:, :], ctx.W("cst_ones_row")[:, :]),
        p.dma(("g", "cload"), cst["ones_col_bf"][:, :], ctx.W("cst_ones_col")[:, :]),
        p.dma(("g", "cload"), cst["ones_col_f32"][:, :], ctx.W("cst_ones_col")[:, :]),
    ]
    for en in Prog.ENGS:
        p.need(en, deps[-1])
    p.bar()



def row_bcast(ctx, dst, row_ap, n, psum, deps=(), dtype_bf=False):
    p = ctx.p
    ones = ctx.cst["ones_row_f32"]
    mm = None
    for off in range(0, n, 512):
        w = min(512, n - off)
        first = off == 0
        p.need("pe", *deps)
        mm = p.mark(p.eng["pe"].matmul(psum[:, off:off + w], ones[0:1, :],
                                       row_ap[0:1, off:off + w],
                                       start=True, stop=True), "pe")
    dep = None
    p.need("v", mm)
    dep = p.mark(p.eng["v"].tensor_copy(dst[:, 0:n], psum[:, 0:n]), "v")
    return dep


def transpose_to(ctx, dst_ap, src_ap, psum_bf, deps=(), engine="v"):
    p = ctx.p
    p.need("pe", *deps)
    mm = p.mark(p.eng["pe"].transpose(psum_bf[:, :], src_ap, ctx.cst["ident_bf"][:, :]), "pe")
    p.need(engine, mm)
    return p.mark(p.eng[engine].tensor_copy(dst_ap, psum_bf[:, :]), engine)


def ln_stats(ctx, x_tile, mean, rstd, scratch, eps, deps=()):
    p = ctx.p
    v, a = p.eng["v"], p.eng["a"]
    n = x_tile.shape[-1]
    p.need("a", *deps)
    d_sq = p.mark(a.activation(scratch[:, 0:n], x_tile, AF.Square,
                               accum_out=mean[:, 1:2]), "a")
    p.need("v", *deps)
    d = p.mark(v.reduce_sum(mean[:, 0:1], x_tile, axis=AX.X), "v")
    p.need("v", d, d_sq)
    d = p.mark(v.tensor_scalar_mul(mean[:, 0:1], mean[:, 0:1], 1.0 / n), "v")
    p.need("v", d)
    d2 = p.mark(v.tensor_tensor(rstd[:, 1:2], mean[:, 0:1], mean[:, 0:1], ALU.mult), "v")
    d3 = p.mark(v.tensor_scalar(rstd[:, 0:1], mean[:, 1:2], 1.0 / n, None, ALU.mult), "v")
    p.need("v", d2, d3)
    d = p.mark(v.tensor_tensor(rstd[:, 0:1], rstd[:, 0:1], rstd[:, 1:2], ALU.subtract), "v")
    p.need("a", d)
    a.activation(rstd[:, 1:2], rstd[:, 0:1], AF.Ln, bias=p.constf(eps))
    d_a = p.mark(a.activation(rstd[:, 0:1], rstd[:, 1:2], AF.Exp, scale=p.constf(-0.5)), "a")
    return (d_a, d)


def softplus(ctx, dst, src, tmp, deps=(), mark_last=True):
    p = ctx.p
    a, v = p.eng["a"], p.eng["v"]
    p.need("a", *deps)
    a.activation(tmp, src, AF.Abs)
    a.activation(tmp, tmp, AF.Exp, scale=p.constf(-1.0))
    d_l = p.mark(a.activation(tmp, tmp, AF.Ln, bias=p.constf(1.0)), "a")
    p.need("v", *deps)
    v.tensor_relu(dst, src)
    p.need("v", d_l)
    d = p.mark(v.tensor_tensor(dst, dst, tmp, ALU.add), "v")
    return d


def load_w(ctx, name, M, stream=("g", "wload"), deps=(), dtype=BF16, stack=None):
    p = ctx.p
    ap = ctx.W(name)
    Kdim = ap.shape[0]
    assert ap.shape[-1] == M and Kdim % 128 == 0
    wt = p.sb("w_" + name, (128, Kdim // 128, M), dtype, stack=stack)
    dep = p.dma(stream, wt[:, :, :], ap.rearrange("(k p) m -> p k m", p=128), deps=deps)
    return wt, dep


def load_row(ctx, name, stream=("s", "rload"), deps=(), stack=None):
    p = ctx.p
    ap = ctx.W(name)
    n = ap.shape[-1]
    rt = p.sb("r_" + name, (1, n), F32, stack=stack)
    dep = p.dma(stream, rt[0:1, :], ap[0:1, :], deps=deps)
    return rt, dep


def bcast_row(ctx, name, psum, deps=(), stack=None, dtype=F32):
    p = ctx.p
    rt, d = load_row(ctx, name, deps=deps, stack=stack)
    n = rt.shape[-1]
    bc = p.sb("bc_" + name, (128, n), dtype, stack=stack)
    dep = row_bcast(ctx, bc, rt, n, psum, deps=(d,) + tuple(deps))
    return bc, dep


def acc_mm(p, ps_ap, pairs, deps=(), tpos=None):
    n = len(pairs)
    p.need("pe", *deps)
    ins = None
    kw = {} if tpos is None else {"tile_position": tpos}
    for i, (lh, rh) in enumerate(pairs):
        ins = p.eng["pe"].matmul(ps_ap, lh, rh, start=(i == 0), stop=(i == n - 1),
                                 **kw)
    return p.mark(ins, "pe")



def prep_weights(inputs: dict) -> dict:
    w = {}
    tr = lambda x: np.ascontiguousarray(np.asarray(x).T)
    idn = lambda x: np.ascontiguousarray(np.asarray(x))
    row = lambda x: np.ascontiguousarray(np.asarray(x).reshape(1, -1))

    for n in ("Wq", "Wk", "Wv", "Wfc"):
        w[f"L0_attn_{n}"] = idn(inputs[f"L0_attn_{n}"])
    w["L0_attn_bfc"] = row(inputs["L0_attn_bfc"])
    w["L0_attn_gamma"] = row(inputs["L0_attn_gamma"])
    w["L0_attn_beta"] = row(inputs["L0_attn_beta"])
    for n in ("Wq", "Wk_c", "Wv_c", "Wk", "Wv", "Wo", "W_shared_in", "W_shared_out",
              "W1_shared", "W2_experts", "Wg_gate", "We_gate"):
        w[f"L0_ffn_{n}"] = idn(inputs[f"L0_ffn_{n}"])
    w["L0_ffn_group_bias"] = row(inputs["L0_ffn_group_bias"])
    w["L0_ffn_expert_bias"] = row(inputs["L0_ffn_expert_bias"])
    w["L0_ffn_norm1_w"] = row(inputs["L0_ffn_norm1_w"])
    w["L0_ffn_norm2_w"] = row(inputs["L0_ffn_norm2_w"])
    w["L1_attn_w_qkv"] = idn(inputs["L1_attn_w_qkv"])
    w["L1_attn_w_out"] = idn(inputs["L1_attn_w_out"])
    w["L1_attn_b_out"] = row(inputs["L1_attn_b_out"])
    for n in ("k_w", "v_w", "router_w", "w1", "w2"):
        w[f"L1_ffn_{n}"] = idn(inputs[f"L1_ffn_{n}"])
    for n in ("o_b", "router_b"):
        w[f"L1_ffn_{n}"] = row(inputs[f"L1_ffn_{n}"])
    qw = np.asarray(inputs["L1_ffn_q_w"])
    w["L1_ffn_q_w_c"] = np.ascontiguousarray(np.concatenate(
        [qw[:, 64 * h:64 * h + 32] for h in range(16)], axis=1))
    qb = np.asarray(inputs["L1_ffn_q_b"])
    qbc = np.concatenate([qb[64 * h:64 * h + 32] for h in range(16)])
    w["L1_ffn_q_b_f"] = np.ascontiguousarray(qbc.reshape(4, 128).T)
    w["L1_ffn_k_b_f"] = np.ascontiguousarray(
        np.asarray(inputs["L1_ffn_k_b"]).reshape(4, 128).T)
    w["L1_ffn_v_b"] = row(inputs["L1_ffn_v_b"])
    w["L1_ffn_o_w_c"] = idn(np.asarray(inputs["L1_ffn_o_w"])[0:512, :])
    b1 = np.asarray(inputs["L1_ffn_b1"])
    w["L1_ffn_b1_f"] = np.ascontiguousarray(
        b1.reshape(8, 16, 128).transpose(0, 2, 1))
    w["L1_ffn_b2"] = idn(inputs["L1_ffn_b2"])
    w["L2_attn_norm_w"] = row(inputs["L2_attn_norm_w"])
    w["L2_attn_norm_b"] = row(inputs["L2_attn_norm_b"])
    w["L2_attn_W_in"] = tr(inputs["L2_attn_W_in"])
    cw = np.asarray(inputs["L2_attn_conv_w"]).reshape(16, 128, 4)
    w["L2_attn_conv_w"] = np.ascontiguousarray(cw.transpose(1, 0, 2))
    w["L2_attn_conv_b"] = np.ascontiguousarray(
        np.asarray(inputs["L2_attn_conv_b"]).reshape(16, 128).T)
    w["L2_attn_W_dt"] = tr(inputs["L2_attn_W_dt"])
    w["L2_attn_b_dt"] = np.ascontiguousarray(
        np.asarray(inputs["L2_attn_b_dt"]).reshape(16, 128).T)
    w["L2_attn_W_B"] = tr(inputs["L2_attn_W_B"])
    w["L2_attn_W_C"] = tr(inputs["L2_attn_W_C"])
    w["L2_attn_D"] = np.ascontiguousarray(
        np.asarray(inputs["L2_attn_D"]).reshape(16, 128).T)
    al = np.asarray(inputs["L2_attn_A_log"]).reshape(16, 128, 16)
    w["L2_attn_A_log"] = np.ascontiguousarray(al.transpose(1, 0, 2))
    w["L2_attn_W_out"] = tr(inputs["L2_attn_W_out"])
    w["L2_ffn_ln1_w"] = row(inputs["L2_ffn_ln1_w"])
    w["L2_ffn_ln2_w"] = row(inputs["L2_ffn_ln2_w"])
    for n in ("kv_d", "q_d", "k_u", "q_u", "v_u", "rope_q_w", "rope_k_w", "o_w"):
        w[f"L2_ffn_{n}"] = tr(inputs[f"L2_ffn_{n}"])
    w["L2_ffn_router_w"] = tr(inputs["L2_ffn_router_w"])
    w["L2_ffn_routing_bias"] = row(inputs["L2_ffn_routing_bias"])
    w["L2_ffn_shared_gate"] = tr(np.asarray(inputs["L2_ffn_shared_gate"])[0])
    w["L2_ffn_shared_up"] = tr(np.asarray(inputs["L2_ffn_shared_up"])[0])
    w["L2_ffn_shared_down"] = tr(np.asarray(inputs["L2_ffn_shared_down"])[0])
    w["L2_ffn_routed_gate"] = np.ascontiguousarray(
        np.asarray(inputs["L2_ffn_routed_gate"]).transpose(0, 2, 1))
    w["L2_ffn_routed_up"] = np.ascontiguousarray(
        np.asarray(inputs["L2_ffn_routed_up"]).transpose(0, 2, 1))
    w["L2_ffn_routed_down"] = np.ascontiguousarray(
        np.asarray(inputs["L2_ffn_routed_down"]).transpose(0, 2, 1))
    for n in ("Wq", "Wk", "Wv", "Wg1", "Wg2", "Wg", "Wo"):
        w[f"L3_attn_{n}"] = idn(inputs[f"L3_attn_{n}"])
    w["L3_attn_bg2"] = row(inputs["L3_attn_bg2"])
    w["L3_attn_g_norm_w"] = row(inputs["L3_attn_g_norm_w"])
    w["L3_ffn_router_w"] = tr(inputs["L3_ffn_router_w"])
    for n in ("expert_gate", "expert_up", "expert_down", "shared_gate",
              "shared_up", "shared_down"):
        w[f"L3_ffn_{n}"] = idn(inputs[f"L3_ffn_{n}"])
    return w



A_STAGES = {}
F_STAGES = {}

def s_ln(ctx: Ctx, l: int):
    p = ctx.p
    X = ctx.buf(f"X{l}", (T, D), F32, external_in=(l == 0 and ctx.fused))
    xn_f = ctx.buf(f"xn{l}_f", (T, D), F32, write=True)
    xn_b = ctx.buf(f"xn{l}_b", (T, D), BF16, write=True)
    xn_T = ctx.buf(f"xn{l}_T", (D, T), BF16, write=True)
    xn_T3 = xn_T.rearrange("(k p) t -> p k t", p=128)
    want_lo = (l == 3)
    if want_lo:
        xn_lo = ctx.buf(f"xn{l}_loT", (D, T), BF16, write=True)
        xn_lo3 = xn_lo.rearrange("(k p) t -> p k t", p=128)
    with p.stage(f"ln{l}"):
        v = p.eng["v"]
        xt = [p.sb("xt", (128, D), F32) for _ in range(2)]
        nf = [p.sb("nf", (128, D), F32) for _ in range(2)]
        nb = [p.sb("nb", (128, D), BF16) for _ in range(2)]
        tb = [p.sb("tb", (128, KD, 128), BF16) for _ in range(2)]
        lo = [p.sb("lo", (128, D), BF16) for _ in range(2)]
        tl = [p.sb("tl", (128, KD, 128), BF16) for _ in range(2)]
        sq = p.sb("sq", (128, D), F32)
        st = p.sb("st", (128, 4), F32)
        ps_t = [p.ps("ps_t", (128, 128), BF16) for _ in range(2)]
        xt_free = [None, None]
        nb_free = [None, None]
        tb_free = [None, None]
        cp_prev = [None, None]
        nf_st = [None, None]
        nb_st = [None, None]
        lo_free = [None, None]
        tl_free = [None, None]
        prev_nf = None
        for q in range(NT):
            c = q % 2
            s = slice(q * 128, (q + 1) * 128)
            dl = p.dma(("s", f"a{c}"), xt[c][:, :], X[s, :], deps=(xt_free[c],))
            d_st = ln_stats(ctx, xt[c][:, :], st[:, 0:2], st[:, 2:4], sq, 1e-5,
                            deps=(dl, prev_nf))
            p.need("v", *d_st, nb_free[c], nf_st[c], nb_st[c])
            d_nf = p.mark(v.tensor_scalar(nf[c][:, :], xt[c][:, :], st[:, 0:1], st[:, 2:3],
                                          ALU.subtract, ALU.mult), "v")
            xt_free[c] = d_nf
            prev_nf = d_nf
            d_nb = p.mark(v.tensor_copy(nb[c][:, :], nf[c][:, :]), "v")
            nf_st[c] = p.dma(("s", f"sn{c}"), xn_f[s, :], nf[c][:, :], deps=(d_nf,))
            nb_st[c] = p.dma(("s", f"sb{c}"), xn_b[s, :], nb[c][:, :], deps=(d_nb,))
            if want_lo:
                p.need("v", lo_free[c])
                d_lo = p.mark(v.tensor_tensor(lo[c][:, :], nf[c][:, :], nb[c][:, :],
                                              ALU.subtract), "v")
            last_cp = None
            for k in range(KD):
                pc = k % 2
                p.need("pe", d_nb, cp_prev[pc])
                mm = p.mark(p.eng["pe"].transpose(ps_t[pc][:, 0:128],
                                                  nb[c][:, k * 128:(k + 1) * 128],
                                                  ctx.cst["ident_bf"][:, :]), "pe")
                if k == KD - 1:
                    nb_free[c] = mm
                p.need("a", mm, tb_free[c])
                last_cp = p.mark(p.eng["a"].copy(tb[c][:, k, :], ps_t[pc][:, 0:128]), "a")
                cp_prev[pc] = last_cp
            tb_free[c] = p.dma(("s", f"tc{c}"), xn_T3[:, :, s], tb[c][:, :, :],
                               deps=(last_cp,))
            if want_lo:
                last_cp2 = None
                for k in range(KD):
                    pc = k % 2
                    p.need("pe", d_lo, cp_prev[pc])
                    mm = p.mark(p.eng["pe"].transpose(ps_t[pc][:, 0:128],
                                                      lo[c][:, k * 128:(k + 1) * 128],
                                                      ctx.cst["ident_bf"][:, :]), "pe")
                    if k == KD - 1:
                        lo_free[c] = mm
                    p.need("a", mm, tl_free[c])
                    last_cp2 = p.mark(p.eng["a"].copy(tl[c][:, k, :], ps_t[pc][:, 0:128]), "a")
                    cp_prev[pc] = last_cp2
                tl_free[c] = p.dma(("s", f"tl{c}"), xn_lo3[:, :, s], tl[c][:, :, :],
                                   deps=(last_cp2,))



def s_res(ctx: Ctx, l: int):
    p = ctx.p
    X = ctx.buf(f"X{l}", (T, D), F32)
    A = ctx.buf(f"brA{l}", (T, D), F32)
    B = ctx.buf(f"brB{l}", (T, D), F32)
    Xn = ctx.buf(f"X{l + 1}", (T, D), F32, write=True,
                 final=(l == 3 and ctx.fused))
    deps = ()
    with p.stage(f"res{l}"):
        v = p.eng["v"]
        xt = [p.sb(f"x{i}", (128, D), F32) for i in range(2)]
        at = [p.sb(f"a{i}", (128, D), F32) for i in range(2)]
        bt = [p.sb(f"b{i}", (128, D), F32) for i in range(2)]
        ot = [p.sb(f"o{i}", (128, D), F32) for i in range(2)]
        last_use = [None, None]
        st_dep = [None, None]
        for q in range(NT):
            c = q % 2
            s = slice(q * 128, (q + 1) * 128)
            d1 = p.dma(("s", f"a{c}"), xt[c][:, :], X[s, :], deps=deps + (last_use[c],))
            d2 = p.dma(("s", f"b{c}"), at[c][:, :], A[s, :], deps=deps + (last_use[c],))
            d3 = p.dma(("g", f"c{c}"), bt[c][:, :], B[s, :], deps=deps + (last_use[c],))
            p.need("v", d1, d2, d3, st_dep[c])
            v.tensor_tensor(ot[c][:, :], xt[c][:, :], at[c][:, :], ALU.add)
            du = p.mark(v.tensor_tensor(ot[c][:, :], ot[c][:, :], bt[c][:, :], ALU.add), "v")
            last_use[c] = du
            st_dep[c] = p.dma(("s", f"o{c}"), Xn[s, :], ot[c][:, :], deps=(du,))


def evac(p, eng, out_ap, ps_ap, deps=()):
    p.need(eng, *deps)
    if eng == "v":
        return p.mark(p.eng["v"].tensor_copy(out_ap, ps_ap), "v")
    return p.mark(p.eng["a"].copy(out_ap, ps_ap), "a")


def proj_fm(ctx, out, wt, wcol0, xnf, deps, psA, psA_free, bias_fm=None,
            act=None):
    p = ctx.p
    nm = out.shape[1]
    nk = wt.shape[1]
    last = [None, None]
    for m in range(nm):
        for tsl in range(T // 512):
            t0 = tsl * 512
            b = tsl % 2
            pairs = [(wt[:, k, wcol0 + m * 128: wcol0 + (m + 1) * 128],
                      xnf[:, k, t0:t0 + 512]) for k in range(nk)]
            mm = acc_mm(p, psA[b][:, :], pairs, deps=tuple(deps) + tuple(psA_free[b]))
            if bias_fm is None and act is None:
                ev = evac(p, "v" if b == 0 else "a", out[:, m, t0:t0 + 512],
                          psA[b][:, :], deps=(mm,))
            elif act is None:
                p.need("v", mm)
                ev = p.mark(p.eng["v"].tensor_scalar_add(
                    out[:, m, t0:t0 + 512], psA[b][:, :], bias_fm[:, m:m + 1]), "v")
            else:
                p.need("a", mm)
                kw = {}
                if bias_fm is not None:
                    kw["bias"] = bias_fm[:, m:m + 1]
                ev = p.mark(p.eng["a"].activation(
                    out[:, m, t0:t0 + 512], psA[b][:, :], act, **kw), "a")
            psA_free[b] = (ev,)
            last[b] = ev
    return tuple(d for d in last if d is not None)


def proj_tm(ctx, out, wt, wcol0, N, xnf, deps, psA, psA_free, bias_bc=None):
    p = ctx.p
    v = p.eng["v"]
    nk = wt.shape[1]
    last = [None, None]
    for q in range(NT):
        qs = slice(q * 128, (q + 1) * 128)
        for h0 in range(0, N, 512):
            w_ = min(512, N - h0)
            b = (h0 // 512) % 2
            pairs = [(xnf[:, k, qs], wt[:, k, wcol0 + h0: wcol0 + h0 + w_])
                     for k in range(nk)]
            mm = acc_mm(p, psA[b][:, 0:w_], pairs, deps=tuple(deps) + tuple(psA_free[b]))
            if bias_bc is None:
                ev = evac(p, "v" if b == 0 else "a", out[:, q, h0:h0 + w_],
                          psA[b][:, 0:w_], deps=(mm,))
            else:
                p.need("v", mm)
                ev = p.mark(v.tensor_tensor(out[:, q, h0:h0 + w_], psA[b][:, 0:w_],
                                            bias_bc[:, h0:h0 + w_], ALU.add), "v")
            psA_free[b] = (ev,)
            last[b] = ev
    return tuple(d for d in last if d is not None)


def proj_fm_dram(ctx, dram3, wt, xnf, deps, psA, psA_free, stream):
    p = ctx.p
    nm = dram3.shape[1]
    nk = wt.shape[1]
    stg = [p.sb("stg", (128, 512), BF16) for _ in range(2)]
    stg_free = [None, None]
    for m in range(nm):
        for tsl in range(T // 512):
            t0 = tsl * 512
            b = tsl % 2
            pairs = [(wt[:, k, m * 128:(m + 1) * 128], xnf[:, k, t0:t0 + 512])
                     for k in range(nk)]
            mm = acc_mm(p, psA[b][:, :], pairs, deps=tuple(deps) + tuple(psA_free[b]))
            ev = evac(p, "v" if b == 0 else "a", stg[b][:, :], psA[b][:, :],
                      deps=(mm, stg_free[b]))
            psA_free[b] = (ev,)
            stg_free[b] = p.dma((stream[0], stream[1] + str(b)),
                                dram3[:, m, t0:t0 + 512], stg[b][:, :], deps=(ev,))
    return stg_free


def mha_pools(p):
    return {
        "psA": [p.ps("psA", (128, 512), F32) for _ in range(2)],
        "psT": [p.ps("psT", (128, 512), BF16) for _ in range(2)],
        "psO": [p.ps("psO", (128, 512), F32) for _ in range(2)],
        "psA_free": [(), ()],
        "psT_free": [None, None],
    }


def mha_core(ctx, qf, kf, vb, pools, *, H, dk, dv, causal, scale, in_deps,
             emit_out, mask_extra=None, q_dram=None, score_pairs=None,
             arena=None):
    p = ctx.p
    v, a, pe = p.eng["v"], p.eng["a"], p.eng["pe"]
    psA, psT, psO = pools["psA"], pools["psT"], pools["psO"]
    S = [p.sb("S", (128, T), F32, stack=arena) for _ in range(2)]
    P = [p.sb("P", (128, T), BF16, stack=arena) for _ in range(2)]
    PT = [p.sb("PT", (128, T), BF16, stack=arena) for _ in range(2)]
    stt = [p.sb("stt", (128, 4), F32, stack=arena) for _ in range(2)]
    ot = [p.sb("ot", (128, H * dv), F32, stack=arena) for _ in range(2)]
    if q_dram is not None:
        if not isinstance(q_dram, (list, tuple)):
            q_dram = [q_dram]
        nkqs = [ap.shape[0] // 128 for ap in q_dram]
        nkq = sum(nkqs)
        qtl = [p.sb("qtl", (128, nkq, 128), BF16, stack=arena) for _ in range(2)]
        qtl_free = [None, None]
    sc_ap = p.constf(scale)
    exp_prev = [None, None]
    rm_prev = [None, None]
    pt_mm_prev = [None, None]
    pt_rd_prev = [None, None]
    ot_free = [None, None]
    psO_free = [None, None]
    for q in range(NT):
        c = q % 2
        qs = slice(q * 128, (q + 1) * 128)
        if q_dram is not None:
            d_qt = ()
            k0 = 0
            for ap3, nk_i in zip(q_dram3 := [ap.rearrange("(k p) t -> p k t", p=128)
                                             for ap in q_dram], nkqs)                     if False else zip([ap.rearrange("(k p) t -> p k t", p=128)
                                       for ap in q_dram], nkqs):
                d_qt = d_qt + (p.dma(("s", f"q{c}{k0}"), qtl[c][:, k0:k0 + nk_i, :],
                                     ap3[:, :, qs], deps=(qtl_free[c],)),)
                k0 += nk_i
        kw = (q + 1) * 128 if causal else T
        nkt = kw // 128
        d_o = None
        for h in range(H):
            hc = h % 2
            bq, cq = (h * dk) % 128, (h * dk) // 128
            ev_S = None
            for g0 in range(0, kw, 512):
                w_ = min(512, kw - g0)
                b = (g0 // 512) % 2
                if score_pairs is not None:
                    prs = score_pairs(qtl[c] if q_dram is not None else qf, h,
                                      qs, g0, w_)
                else:
                    q_ap = (qtl[c][bq:bq + dk, cq, :] if q_dram is not None
                            else qf[bq:bq + dk, cq, qs])
                    prs = [(q_ap, kf[bq:bq + dk, cq, g0:g0 + w_], (bq, 0))]
                p.need("pe", *(tuple(in_deps) + tuple(pools["psA_free"][b]) +
                               (d_qt if q_dram is not None else ())))
                mm = None
                for i, (lh, rh, tp) in enumerate(prs):
                    mm = p.mark(pe.matmul(psA[b][:, 0:w_], lh, rh,
                                          start=(i == 0), stop=(i == len(prs) - 1),
                                          tile_position=tp), "pe")
                if q_dram is not None:
                    qtl_free[c] = mm
                ev_S = evac(p, "v" if b == 0 else "a", S[hc][:, g0:g0 + w_],
                            psA[b][:, 0:w_], deps=(mm, exp_prev[hc]))
                pools["psA_free"][b] = (ev_S,)
            p.need("v", ev_S, p.tip("a") if kw > 512 else None)
            if causal:
                v.tensor_tensor(S[hc][:, qs], S[hc][:, qs],
                                ctx.cst["causal_add"][:, :], ALU.add)
            if mask_extra is not None:
                mask_extra(q, h, S[hc], kw)
            p.need("v", rm_prev[hc])
            d_rm = p.mark(v.reduce_max(stt[hc][:, 0:1], S[hc][:, 0:kw], axis=AX.X,
                                       negate=True), "v")
            p.need("v", d_rm)
            d_nm = p.mark(v.tensor_scalar_mul(stt[hc][:, 1:2], stt[hc][:, 0:1],
                                              scale), "v")
            p.need("a", d_nm, pt_mm_prev[hc])
            d_exp = p.mark(a.activation(P[hc][:, 0:kw], S[hc][:, 0:kw], AF.Exp,
                                        bias=stt[hc][:, 1:2], scale=sc_ap,
                                        accum_out=stt[hc][:, 2:3]), "a")
            exp_prev[hc] = d_exp
            p.need("v", d_exp)
            d_rd = p.mark(v.reciprocal(stt[hc][:, 3:4], stt[hc][:, 2:3]), "v")
            rm_prev[hc] = d_rd
            ev_pt = []
            for g0 in range(0, nkt, 4):
                gn = min(4, nkt - g0)
                tc = (g0 // 4) % 2
                mmt = None
                for j in range(gn):
                    p.need("pe", d_exp, pools["psT_free"][tc], pt_rd_prev[hc])
                    mmt = p.mark(pe.transpose(psT[tc][:, j * 128:(j + 1) * 128],
                                              P[hc][:, (g0 + j) * 128:(g0 + j + 1) * 128],
                                              ctx.cst["ident_bf"][:, :]), "pe")
                pt_mm_prev[hc] = mmt
                evp = evac(p, "v" if tc == 0 else "a",
                           PT[hc][:, g0 * 128:(g0 + gn) * 128],
                           psT[tc][:, 0:gn * 128], deps=(mmt,))
                pools["psT_free"][tc] = evp
                ev_pt.append(evp)
            pairs = [(PT[hc][:, kt * 128:(kt + 1) * 128],
                      vb[:, kt, h * dv:(h + 1) * dv]) for kt in range(nkt)]
            mmo = acc_mm(p, psO[hc][:, 0:dv], pairs,
                         deps=tuple(ev_pt) + (psO_free[hc],) + tuple(in_deps))
            pt_rd_prev[hc] = mmo
            p.need("v", mmo, d_rd, ot_free[c] if h < 2 else None)
            d_o = p.mark(v.tensor_scalar_mul(ot[c][:, h * dv:(h + 1) * dv],
                                             psO[hc][:, 0:dv], stt[hc][:, 3:4]), "v")
            psO_free[hc] = d_o
        ot_free[c] = emit_out(q, c, ot[c], d_o)


def tm_to_fm_out(ctx, pools, wo, dwo, bias_bc, dbias, psC, otb, lnfm, ob, dst,
                 state, q, c, ot, d_o, nko=KD, ncol=D, extra_add=None, store=True):
    p = ctx.p
    v, a, pe = p.eng["v"], p.eng["a"], p.eng["pe"]
    s = slice(q * 128, (q + 1) * 128)
    p.need("a", d_o, state["fm_mm_prev"])
    d_cast = p.mark(a.copy(otb[:, 0:nko * 128], ot[:, 0:nko * 128]), "a")
    ev_fm = [None, None]
    for k in range(nko):
        tc = k % 2
        p.need("pe", d_cast, pools["psT_free"][tc])
        mmt = p.mark(pe.transpose(pools["psT"][tc][:, 0:128],
                                  otb[:, k * 128:(k + 1) * 128],
                                  ctx.cst["ident_bf"][:, :]), "pe")
        ev_fm[tc] = evac(p, "v" if tc == 0 else "a", lnfm[:, k, :],
                         pools["psT"][tc][:, 0:128], deps=(mmt, state["fm_mm_prev"]))
        pools["psT_free"][tc] = ev_fm[tc]
    mm_f = None
    for h0 in range(0, ncol, 512):
        w_ = min(512, ncol - h0)
        pairs = [(lnfm[:, k, :], wo[:, k, h0:h0 + w_]) for k in range(nko)]
        mm_f = acc_mm(p, psC[:, h0:h0 + w_], pairs,
                      deps=(ev_fm[0], ev_fm[1], dwo, state["psC_free"]))
    state["fm_mm_prev"] = mm_f
    p.need("v", mm_f, dbias, state["ob_free"][c])
    if bias_bc is not None:
        d_ob = p.mark(v.tensor_tensor(ob[c][:, 0:ncol], psC[:, 0:ncol],
                                      bias_bc[:, 0:ncol], ALU.add), "v")
    else:
        d_ob = p.mark(v.tensor_copy(ob[c][:, 0:ncol], psC[:, 0:ncol]), "v")
    if extra_add is not None:
        d_ob = extra_add(q, c, d_ob)
    state["psC_free"] = d_ob
    if store:
        dep = p.dma(("s", f"o{c}"), dst[s, :], ob[c][:, 0:ncol], deps=(d_ob,))
        state["ob_free"][c] = dep
    else:
        state["ob_free"][c] = d_ob
    return d_cast


def s_attn1(ctx: Ctx, l: int):
    p = ctx.p
    xn_T = ctx.buf("xn1_T", (D, T), BF16)
    brA = ctx.buf("brA1", (T, D), F32, write=True)
    with p.stage("attn1"):
        pools = mha_pools(p)
        psC = p.ps("psC", (128, 1024), F32)
        qf = p.sb("qf", (128, KD, T), BF16)
        kf = p.sb("kf", (128, KD, T), BF16)
        vb = p.sb("vb", (128, NT, D), BF16)
        subA = contextlib.ExitStack()
        wqkv_ap = ctx.W("L1_attn_w_qkv").rearrange("(k p) m -> p k m", p=128)
        wq = p.sb("wq", (128, KD, D), BF16, stack=subA)
        wk = p.sb("wk", (128, KD, D), BF16, stack=subA)
        wv = p.sb("wv", (128, KD, D), BF16, stack=subA)
        dwq = p.dma(("g", "w0"), wq[:, :, :], wqkv_ap[:, :, 0:D])
        dwk = p.dma(("g", "w1"), wk[:, :, :], wqkv_ap[:, :, D:2 * D])
        dwv = p.dma(("g", "w2"), wv[:, :, :], wqkv_ap[:, :, 2 * D:3 * D])
        xnf = p.sb("xnf", (128, KD, T), BF16, stack=subA)
        dxn = p.dma(("s", "ax"), xnf[:, :, :],
                    xn_T.rearrange("(k p) t -> p k t", p=128))
        dq = proj_fm(ctx, qf, wq, 0, xnf, (dwq, dxn), pools["psA"], pools["psA_free"])
        dk_ = proj_fm(ctx, kf, wk, 0, xnf, (dwk, dxn), pools["psA"], pools["psA_free"])
        dv_ = proj_tm(ctx, vb, wv, 0, D, xnf, (dwv, dxn), pools["psA"],
                      pools["psA_free"])
        p.bar()
        subA.close()
        subB = contextlib.ExitStack()
        wo, dwo = load_w(ctx, "L1_attn_w_out", D, stream=("g", "w0"), stack=subB)
        bo, dbo = bcast_row(ctx, "L1_attn_b_out", psC, stack=subB)
        ob = [p.sb("ob", (128, D), F32) for _ in range(2)]
        otb = p.sb("otb", (128, D), BF16)
        lnfm = p.sb("lnfm", (128, KD, 128), BF16)
        state = {"psC_free": dbo, "fm_mm_prev": None, "cp_prev": None,
                 "ob_free": [None, None]}

        def emit_out(q, c, ot, d_o):
            return tm_to_fm_out(ctx, pools, wo, dwo, bo, dbo, psC, otb, lnfm,
                                ob, brA, state, q, c, ot, d_o)

        subM = contextlib.ExitStack()
        mha_core(ctx, qf, kf, vb, pools, H=16, dk=64, dv=64, causal=True,
                 scale=0.125, in_deps=dq + dk_ + dv_, emit_out=emit_out,
                 arena=subM)
        p.bar()
        subM.close()
        subB.close()


A_STAGES[1] = s_attn1

def proj_tm_dram(ctx, dram, wt, xnf, deps, psA, psA_free, stream, bias_bc=None):
    p = ctx.p
    N = wt.shape[2]
    nk = wt.shape[1]
    stg = [p.sb("stg", (128, N), BF16) for _ in range(2)]
    stg_free = [None, None]
    for q in range(NT):
        c = q % 2
        qs = slice(q * 128, (q + 1) * 128)
        evs = []
        for h0 in range(0, N, 512):
            w_ = min(512, N - h0)
            b = (h0 // 512) % 2
            pairs = [(xnf[:, k, qs], wt[:, k, h0:h0 + w_]) for k in range(nk)]
            mm = acc_mm(p, psA[b][:, 0:w_], pairs, deps=tuple(deps) + tuple(psA_free[b]))
            ev = evac(p, "v" if b == 0 else "a", stg[c][:, h0:h0 + w_],
                      psA[b][:, 0:w_], deps=(mm, stg_free[c]))
            psA_free[b] = (ev,)
            evs.append(ev)
        stg_free[c] = p.dma((stream[0], stream[1] + str(c)), dram[qs, :],
                            stg[c][:, :], deps=tuple(evs))
    return stg_free


def s_attn3(ctx: Ctx, l: int):
    p = ctx.p
    v, a, pe = p.eng["v"], p.eng["a"], p.eng["pe"]
    xn_T = ctx.buf("xn3_T", (D, T), BF16)
    brA = ctx.buf("brA3", (T, D), F32, write=True)
    qd = ctx.buf("l3a_q", (T, D), BF16, write=True)
    rowd = ctx.buf("l3a_row", (1, D), F32, write=True)
    kd = ctx.buf("l3a_k", (T, D), BF16, write=True)
    vd = ctx.buf("l3a_v", (T, D), BF16, write=True)
    gd = ctx.buf("l3a_g", (T, D), BF16, write=True)
    H, DH = 16, 64
    SCALE = DH ** -0.5
    with p.stage("attn3"):
        psA = [p.ps("psA", (128, 512), F32) for _ in range(2)]
        psT = [p.ps("psT", (128, 512), BF16) for _ in range(2)]
        psO = p.ps("psO", (128, 512), F32)
        psC = p.ps("psC", (128, 1024), F32)
        pools = {"psA": psA, "psT": psT, "psA_free": [(), ()],
                 "psT_free": [None, None]}
        subA = contextlib.ExitStack()
        xnf = p.sb("xnf", (128, KD, T), BF16, stack=subA)
        dxn = p.dma(("s", "ax"), xnf[:, :, :],
                    xn_T.rearrange("(k p) t -> p k t", p=128))
        u_fm = p.sb("u_fm", (16, T), BF16)
        for (nm, dst) in (("Wq", qd), ("Wk", kd), ("Wv", vd), ("Wg", gd)):
            subW = contextlib.ExitStack()
            wt, dwt = load_w(ctx, f"L3_attn_{nm}", D, stream=("g", "w0"), stack=subW)
            proj_tm_dram(ctx, dst, wt, xnf, (dwt, dxn), psA, pools["psA_free"],
                         ("s", "t" + nm))
            p.bar()
            subW.close()
        subW = contextlib.ExitStack()
        wg1, dwg1 = load_w(ctx, "L3_attn_Wg1", 16, stream=("g", "w0"), stack=subW)
        ut = p.sb("ut", (128, 16), BF16, stack=subW)
        uf = p.sb("uf", (128, 16), F32, stack=subW)
        ev_prev = None
        for q in range(NT):
            qs = slice(q * 128, (q + 1) * 128)
            pairs = [(xnf[:, k, qs], wg1[:, k, :]) for k in range(KD)]
            mm = acc_mm(p, psA[0][:, 0:16], pairs,
                        deps=(dwg1, dxn, ev_prev) + tuple(pools["psA_free"][0]))
            p.need("v", mm)
            d_u = p.mark(v.tensor_copy(ut[:, :], psA[0][:, 0:16]), "v")
            p.need("pe", d_u, ev_prev)
            mmt = p.mark(pe.transpose(psT[0][0:16, 0:128], ut[:, :],
                                      ctx.cst["ident_bf"][:, :]), "pe")
            ev_prev = evac(p, "a", u_fm[0:16, qs], psT[0][0:16, 0:128], deps=(mmt,))
            pools["psA_free"][0] = (d_u,)
            pools["psT_free"][0] = ev_prev
        p.bar()
        subW.close()
        subA.close()
        wg2 = p.sb("wg2", (16, 1, D), BF16)
        dwg2 = p.dma(("g", "w0"), wg2[:, 0, :], ctx.W("L3_attn_Wg2")[:, :])
        wo, dwo = load_w(ctx, "L3_attn_Wo", D, stream=("g", "w1"))
        bg2bc, dbg = bcast_row(ctx, "L3_attn_bg2", psC)
        gnwbc, dgn = bcast_row(ctx, "L3_attn_g_norm_w", psC, deps=(dbg,))
        Sst = p.sb("Sst", (128, KD, DH), F32)
        Ssb = p.sb("Ssb", (128, KD, DH), BF16)
        v.memset(Sst[:, :, :], 0.0)
        d_z0 = p.mark(v.memset(Ssb[:, :, :], 0.0), "v")
        qt = [p.sb("qt", (128, D), BF16) for _ in range(2)]
        kt = [p.sb("kt", (128, D), BF16) for _ in range(2)]
        vt = [p.sb("vt", (128, D), BF16) for _ in range(2)]
        gt = [p.sb("gt", (128, D), BF16) for _ in range(2)]
        zf = p.sb("zf", (128, D), F32)
        scr1 = p.sb("scr1", (128, D), F32)
        gkf = p.sb("gkf", (128, D), F32)
        Af = p.sb("Af", (128, D), F32)
        rbc = p.sb("rbc", (128, D), F32)
        e1 = p.sb("e1", (128, D), F32)
        e2 = p.sb("e2", (128, D), F32)
        e3 = p.sb("e3", (128, D), F32)
        at_row = p.sb("at_row", (1, D), F32)
        rcol = p.sb("rcol", (128, KD), F32)
        ercol = p.sb("ercol", (128, KD), F32)
        dtcol = p.sb("dtcol", (128, KD), F32)
        qp = p.sb("qp", (128, D), BF16)
        kp = p.sb("kp", (128, D), BF16)
        kpp = p.sb("kpp", (128, D), BF16)
        qp_fm = p.sb("qp_fm", (128, KD, 128), BF16)
        kp_fm = p.sb("kp_fm", (128, KD, 128), BF16)
        qpp_fm = p.sb("qpp_fm", (128, KD, 128), BF16)
        St_sb = p.sb("St_sb", (128, 128), BF16)
        ot = p.sb("ot", (128, D), F32)
        ssq = p.sb("ssq", (128, 16), F32)
        obt = p.sb("obt", (128, D), BF16)
        obfm = p.sb("obfm", (128, KD, 128), BF16)
        ob = [p.sb("ob", (128, D), F32) for _ in range(2)]
        ob_free = [None, None]
        ld_free = [None, None]
        lnS = p.constf(float(np.log(SCALE)))
        NCH = T // 128
        prev = {"psC": (dgn,), "fm": None, "St": None, "ost": None, "kv": None,
                "o_ev": None, "gk": None}
        for ch in range(NCH):
            c = ch % 2
            qs = slice(ch * 128, (ch + 1) * 128)
            dq_ = p.dma(("s", f"a{c}"), qt[c][:, :], qd[qs, :], deps=(ld_free[c],))
            dk_ = p.dma(("s", f"b{c}"), kt[c][:, :], kd[qs, :], deps=(ld_free[c],))
            dv_ = p.dma(("s", f"c{c}"), vt[c][:, :], vd[qs, :], deps=(ld_free[c],))
            dg_ = p.dma(("s", f"d{c}"), gt[c][:, :], gd[qs, :], deps=(ld_free[c],))
            mmz = None
            for h0 in range(2):
                mmz = acc_mm(p, psC[:, h0 * 512:(h0 + 1) * 512],
                             [(u_fm[0:16, qs], wg2[0:16, 0, h0 * 512:(h0 + 1) * 512])],
                             deps=(dwg2, p.tip("v")) + tuple(prev["psC"]))
            p.need("v", mmz, dbg)
            d_z = p.mark(v.tensor_tensor(zf[:, :], psC[:, :], bg2bc[:, :],
                                         ALU.add), "v")
            p.need("a", d_z)
            a.activation(scr1[:, :], zf[:, :], AF.Abs)
            a.activation(scr1[:, :], scr1[:, :], AF.Exp, scale=p.constf(-1.0))
            d_l = p.mark(a.activation(scr1[:, :], scr1[:, :], AF.Ln,
                                      bias=p.constf(1.0)), "a")
            p.need("v", d_l)
            v.tensor_scalar_min(zf[:, :], zf[:, :], 0.0)
            v.tensor_tensor(zf[:, :], zf[:, :], scr1[:, :], ALU.subtract)
            d_gk = p.mark(v.tensor_scalar_mul(gkf[:, :], zf[:, :], 1.0 / 16.0), "v")
            mmA = None
            for h0 in range(2):
                mmA = acc_mm(p, psC[:, h0 * 512:(h0 + 1) * 512],
                             [(ctx.cst["lower_ji_f32"][:, :],
                               gkf[:, h0 * 512:(h0 + 1) * 512])],
                             deps=(d_gk, d_z))
            mmT = None
            for h0 in range(2):
                mmT = acc_mm(p, psA[0][0:1, 0:512],
                             [(ctx.cst["ones_col_f32"][:, :],
                               gkf[:, h0 * 512:(h0 + 1) * 512])],
                             deps=(d_gk,) + tuple(pools["psA_free"][0]))                     if h0 == 0 else                     acc_mm(p, psA[1][0:1, 0:512],
                           [(ctx.cst["ones_col_f32"][:, :],
                             gkf[:, h0 * 512:(h0 + 1) * 512])],
                           deps=(d_gk,) + tuple(pools["psA_free"][1]))
            p.need("v", mmA, mmT)
            d_A = p.mark(v.tensor_copy(Af[:, :], psC[:, :]), "v")
            prev["psC"] = (d_A,)
            v.tensor_scalar_mul(at_row[0:1, 0:512], psA[0][0:1, 0:512], 0.5)
            d_at = p.mark(v.tensor_scalar_mul(at_row[0:1, 512:1024],
                                              psA[1][0:1, 0:512], 0.5), "v")
            pools["psA_free"][0] = (d_at,)
            pools["psA_free"][1] = (d_at,)
            mmb = None
            for h0 in range(2):
                mmb = acc_mm(p, psC[:, h0 * 512:(h0 + 1) * 512],
                             [(ctx.cst["ones_row_f32"][0:1, :],
                               at_row[0:1, h0 * 512:(h0 + 1) * 512])],
                             deps=(d_at, d_A))
            p.need("v", mmb)
            d_rb = p.mark(v.tensor_copy(rbc[:, :], psC[:, :]), "v")
            prev["psC"] = (d_rb,)
            d_r1 = p.dma(("s", "rc"), rowd[0:1, :], at_row[0:1, :],
                         deps=(d_at,))
            d_rc = p.dma(("s", "rc2"), rcol[:, :],
                         rowd.rearrange("o (k p) -> p (k o)", p=128),
                         deps=(d_r1, p.tip("a")),
                         allow_slow_non_contiguous=True)
            p.need("a", d_rc, d_rb)
            a.activation(ercol[:, :], rcol[:, :], AF.Exp)
            d_dt = p.mark(a.activation(dtcol[:, :], rcol[:, :], AF.Exp,
                                       scale=p.constf(2.0)), "a")
            p.need("v", d_rb)
            d_s1 = p.mark(v.tensor_tensor(e1[:, :], Af[:, :], rbc[:, :],
                                          ALU.subtract), "v")
            p.need("a", d_s1)
            d_e1 = p.mark(a.activation(e1[:, :], e1[:, :], AF.Exp, bias=lnS), "a")
            d_s2 = p.mark(v.tensor_tensor(e2[:, :], rbc[:, :], Af[:, :],
                                          ALU.subtract), "v")
            p.need("a", d_s2)
            d_e2 = p.mark(a.activation(e2[:, :], e2[:, :], AF.Exp), "a")
            v.tensor_scalar_mul(e3[:, :], rbc[:, :], 2.0)
            d_s3 = p.mark(v.tensor_tensor(e3[:, :], e3[:, :], Af[:, :],
                                          ALU.subtract), "v")
            p.need("a", d_s3)
            d_e3 = p.mark(a.activation(e3[:, :], e3[:, :], AF.Exp), "a")
            p.need("v", d_e1, dq_, prev["fm"])
            d_qp = p.mark(v.tensor_tensor(qp[:, :], qt[c][:, :], e1[:, :],
                                          ALU.mult), "v")
            p.need("v", d_e2, dk_)
            d_kp = p.mark(v.tensor_tensor(kp[:, :], kt[c][:, :], e2[:, :],
                                          ALU.mult), "v")
            p.need("v", d_e3, dk_)
            d_kpp = p.mark(v.tensor_tensor(kpp[:, :], kt[c][:, :], e3[:, :],
                                          ALU.mult), "v")
            p.need("v", d_dt)
            evq = [None, None]
            for (src_t, dst_t) in ((qp, qp_fm), (kp, kp_fm)):
                dep_rp = d_qp if src_t is qp else d_kp
                for k in range(KD):
                    tc = k % 2
                    p.need("pe", dep_rp, pools["psT_free"][tc])
                    mmt = p.mark(pe.transpose(psT[tc][:, 0:128],
                                              src_t[:, k * 128:(k + 1) * 128],
                                              ctx.cst["ident_bf"][:, :]), "pe")
                    ev = evac(p, "v" if tc == 0 else "a", dst_t[:, k, :],
                              psT[tc][:, 0:128], deps=(mmt,))
                    pools["psT_free"][tc] = ev
                    evq[tc] = ev
            d_qpp = None
            for k in range(KD):
                p.need("v", evq[0], evq[1])
                d_qpp = p.mark(v.tensor_scalar_mul(qpp_fm[:, k, :], qp_fm[:, k, :],
                                                   ercol[:, k:k + 1]), "v")
            d_o = None
            for h in range(H):
                rows = (h * DH) % 128
                chk = (h * DH) // 128
                hs = slice(h * DH, (h + 1) * DH)
                mmS = acc_mm(p, psA[0][:, 0:128],
                             [(kp_fm[rows:rows + DH, chk, :],
                               qp_fm[rows:rows + DH, chk, :], )],
                             deps=(evq[0], evq[1], d_qpp) + tuple(pools["psA_free"][0]),
                             tpos=(rows, 0))
                p.need("v", mmS)
                d_St = p.mark(v.tensor_tensor(St_sb[:, :], psA[0][:, 0:128],
                                              ctx.cst["lower_ji_bf"][:, :],
                                              ALU.mult), "v")
                pools["psA_free"][0] = (d_St,)
                mmo = acc_mm(p, psO[:, 0:DH],
                             [(St_sb[:, :], vt[c][:, hs])],
                             deps=(d_St, dv_, prev["o_ev"]))
                mst = acc_mm(p, psA[1][:, 0:DH],
                             [(qpp_fm[rows:rows + DH, chk, :],
                               Ssb[rows:rows + DH, chk, :])],
                             deps=(d_qpp, d_z0, prev["o_ev"], prev["St"]),
                             tpos=(rows, 0))
                p.need("v", mmo, mst)
                v.tensor_copy(ot[:, hs], psO[:, 0:DH])
                d_o = p.mark(v.tensor_tensor(ot[:, hs], ot[:, hs], psA[1][:, 0:DH],
                                             ALU.add), "v")
                prev["o_ev"] = d_o
                pools["psA_free"][1] = (d_o,)
                mkv = acc_mm(p, psC[rows:rows + DH, 0:DH],
                             [(kt[c][:, hs] if False else kpp[:, hs], vt[c][:, hs])],
                             deps=(d_kpp, dv_) + tuple(prev["psC"]),
                             tpos=(0, rows))
                p.need("v", mkv, d_dt)
                v.tensor_scalar_mul(Sst[rows:rows + DH, chk, :],
                                    Sst[rows:rows + DH, chk, :],
                                    dtcol[rows:rows + DH, chk:chk + 1])
                v.tensor_tensor(Sst[rows:rows + DH, chk, :],
                                Sst[rows:rows + DH, chk, :],
                                psC[rows:rows + DH, 0:DH], ALU.add)
                d_up = p.mark(v.tensor_copy(Ssb[rows:rows + DH, chk, :],
                                            Sst[rows:rows + DH, chk, :]), "v")
                prev["St"] = d_up
                prev["psC"] = (d_up,)
            ot3 = ot[:, :].rearrange("p (h f) -> p h f", h=H)
            p.need("v", d_o)
            v.tensor_tensor(scr1[:, :], ot[:, :], ot[:, :], ALU.mult)
            d_ss = p.mark(v.reduce_sum(ssq[:, :].unsqueeze(2),
                                       scr1[:, :].rearrange("p (h f) -> p h f", h=H),
                                       axis=AX.X), "v")
            p.need("a", d_ss)
            a.activation(ssq[:, :], ssq[:, :], AF.Ln, scale=p.constf(1.0 / DH),
                         bias=p.constf(1e-5))
            d_rs = p.mark(a.activation(ssq[:, :], ssq[:, :], AF.Exp,
                                       scale=p.constf(-0.5)), "a")
            p.need("v", d_rs, dg_, dgn)
            v.tensor_tensor(ot3, ot3,
                            ssq[:, :].unsqueeze(2).broadcast_to((128, H, DH)),
                            ALU.mult)
            d_on = p.mark(v.tensor_tensor(
                ot3, ot3, gnwbc[:, 0:DH].unsqueeze(1).broadcast_to((128, H, DH)),
                ALU.mult), "v")
            p.need("a", d_on, dg_)
            d_gs = p.mark(a.activation(scr1[:, :], gt[c][:, :], AF.Silu), "a")
            p.need("v", d_gs, prev["gk"])
            d_og = p.mark(v.tensor_tensor(obt[:, :], ot[:, :], scr1[:, :],
                                          ALU.mult), "v")
            ld_free[c] = (d_og, prev["St"])
            ev_fm = [None, None]
            for k in range(KD):
                tc = k % 2
                p.need("pe", d_og, pools["psT_free"][tc])
                mmt = p.mark(pe.transpose(psT[tc][:, 0:128],
                                          obt[:, k * 128:(k + 1) * 128],
                                          ctx.cst["ident_bf"][:, :]), "pe")
                if k == KD - 1:
                    prev["gk"] = mmt
                ev_fm[tc] = evac(p, "v" if tc == 0 else "a", obfm[:, k, :],
                                 psT[tc][:, 0:128], deps=(mmt, prev["fm"]))
                pools["psT_free"][tc] = ev_fm[tc]
            mmf = None
            for h0 in range(2):
                pairs = [(obfm[:, k, :], wo[:, k, h0 * 512:(h0 + 1) * 512])
                         for k in range(KD)]
                mmf = acc_mm(p, psC[:, h0 * 512:(h0 + 1) * 512], pairs,
                             deps=(dwo, ev_fm[0], ev_fm[1]) + tuple(prev["psC"]))
            prev["fm"] = mmf
            p.need("v", mmf, ob_free[c])
            d_ob = p.mark(v.tensor_copy(ob[c][:, :], psC[:, :]), "v")
            prev["psC"] = (d_ob,)
            ob_free[c] = p.dma(("s", f"o{c}"), brA[qs, :], ob[c][:, :],
                               deps=(d_ob,))


A_STAGES[3] = s_attn3

def s_attn2(ctx: Ctx, l: int):
    p = ctx.p
    v, a, pe = p.eng["v"], p.eng["a"], p.eng["pe"]
    xn_T = ctx.buf("xn2_T", (D, T), BF16)
    xn_f = ctx.buf("xn2_f", (T, D), F32)
    brA = ctx.buf("brA2", (T, D), F32, write=True)
    xcd = ctx.buf("l2a_xc", (2048, T), BF16, write=True)
    zd = ctx.buf("l2a_z", (2048, T), BF16, write=True)
    DI, NS = 2048, 16
    with p.stage("attn2"):
        psA = [p.ps("psA", (128, 512), F32) for _ in range(2)]
        psT = [p.ps("psT", (128, 512), BF16) for _ in range(2)]
        psC = p.ps("psC", (128, 1024), F32)
        pools = {"psA_free": [(), ()], "psT_free": [None, None]}
        subX = contextlib.ExitStack()
        xn2f = p.sb("xn2f", (128, KD, T), BF16, stack=subX)
        subX2 = contextlib.ExitStack()
        nwbc, dnw = bcast_row(ctx, "L2_attn_norm_w", psC, stack=subX2)
        nbbc, dnb = bcast_row(ctx, "L2_attn_norm_b", psC, deps=(dnw,), stack=subX2)
        xt = [p.sb("xt", (128, D), F32, stack=subX2) for _ in range(2)]
        xb_ = p.sb("xb_", (128, D), BF16, stack=subX2)
        st4 = p.sb("st4", (128, 4), F32, stack=subX2)
        sqs = p.sb("sqs", (128, D), F32, stack=subX2)
        xt_free = [None, None]
        cast_prev = None
        stat_prev = None
        for q in range(NT):
            c = q % 2
            qs = slice(q * 128, (q + 1) * 128)
            dxl = p.dma(("s", f"a{c}"), xt[c][:, :], xn_f[qs, :], deps=(xt_free[c],))
            d_st = ln_stats(ctx, xt[c][:, :], st4[:, 0:2], st4[:, 2:4], sqs, 1e-5,
                            deps=(dxl, stat_prev))
            p.need("v", *d_st, dnw, dnb, cast_prev)
            d_n = p.mark(v.tensor_scalar(xt[c][:, :], xt[c][:, :], st4[:, 0:1],
                                         st4[:, 2:3], ALU.subtract, ALU.mult), "v")
            stat_prev = d_n
            v.tensor_tensor(xt[c][:, :], xt[c][:, :], nwbc[:, :], ALU.mult)
            d_n2 = p.mark(v.tensor_tensor(xt[c][:, :], xt[c][:, :], nbbc[:, :],
                                          ALU.add), "v")
            p.need("a", d_n2, cast_prev)
            d_cb = p.mark(a.copy(xb_[:, :], xt[c][:, :]), "a")
            xt_free[c] = d_cb
            for k in range(KD):
                tc = k % 2
                p.need("pe", d_cb, pools["psT_free"][tc])
                mmt = p.mark(pe.transpose(psT[tc][:, 0:128],
                                          xb_[:, k * 128:(k + 1) * 128],
                                          ctx.cst["ident_bf"][:, :]), "pe")
                if k == KD - 1:
                    cast_prev = mmt
                ev = evac(p, "v" if tc == 0 else "a", xn2f[:, k, qs],
                          psT[tc][:, 0:128], deps=(mmt,))
                pools["psT_free"][tc] = ev
        p.bar()
        subX2.close()
        cw = p.sb("cw", (128, 16, 4), F32)
        cb = p.sb("cb", (128, 16), F32)
        dcw = p.dma(("s", "r0"), cw[:, :, :], ctx.W("L2_attn_conv_w")[:, :, :])
        dcb = p.dma(("s", "r1"), cb[:, :], ctx.W("L2_attn_conv_b")[:, :])
        win_ap = ctx.W("L2_attn_W_in").rearrange("(k p) m -> p k m", p=128)
        for half in range(2):
            subW = contextlib.ExitStack()
            win = p.sb("win", (128, KD, 2048), BF16, stack=subW)
            dwi = p.dma(("g", "w0"), win[:, :, :],
                        win_ap[:, :, half * 2048:(half + 1) * 2048])
            xrow = p.sb("xrow", (128, T + 4), F32, stack=subW)
            crow = p.sb("crow", (128, T), F32, stack=subW)
            stg = [p.sb("stg", (128, T), BF16, stack=subW) for _ in range(2)]
            stg_free = [None, None]
            d_z0 = p.mark(v.memset(xrow[:, 0:4], 0.0), "v")
            ev_prev = None
            for m in range(16):
                c = m % 2
                evs = []
                for tsl in range(4):
                    t0 = tsl * 512
                    b = tsl % 2
                    pairs = [(win[:, k, m * 128:(m + 1) * 128],
                              xn2f[:, k, t0:t0 + 512]) for k in range(KD)]
                    mm = acc_mm(p, psA[b][:, :], pairs,
                                deps=(dwi,) + tuple(pools["psA_free"][b]))
                    if half == 0:
                        ev = evac(p, "v", xrow[:, 4 + t0:4 + t0 + 512],
                                  psA[b][:, :], deps=(mm, ev_prev))
                    else:
                        ev = evac(p, "v" if b == 0 else "a",
                                  stg[c][:, t0:t0 + 512], psA[b][:, :],
                                  deps=(mm, stg_free[c]))
                    pools["psA_free"][b] = (ev,)
                    evs.append(ev)
                if half == 0:
                    p.need("v", *(evs + [d_z0, dcw, dcb, ev_prev]))
                    v.tensor_scalar(crow[:, :], xrow[:, 1:1 + T], cw[:, m, 0:1],
                                    cb[:, m:m + 1], ALU.mult, ALU.add)
                    for j in range(1, 4):
                        v.scalar_tensor_tensor(crow[:, :], xrow[:, 1 + j:1 + j + T],
                                               cw[:, m, j:j + 1], crow[:, :],
                                               ALU.mult, ALU.add)
                    d_cv = p.mark(v.tensor_copy(crow[:, :], crow[:, :]), "v") \
                        if False else p.tip("v")
                    p.need("a", d_cv, stg_free[c])
                    d_si = p.mark(a.activation(stg[c][:, :], crow[:, :],
                                               AF.Silu), "a")
                    ev_prev = d_si
                    stg_free[c] = p.dma(("s", f"tc{c}"),
                                        xcd[m * 128:(m + 1) * 128, :],
                                        stg[c][:, :], deps=(d_si,))
                else:
                    stg_free[c] = p.dma(("s", f"tc{c}"),
                                        zd[m * 128:(m + 1) * 128, :],
                                        stg[c][:, :], deps=tuple(evs))
            p.bar()
            subW.close()
        p.bar()
        subX.close()
        TC = 256
        NTC = T // TC
        af = p.sb("af", (128, 16, NS), F32)
        dal = p.dma(("s", "r0"), af[:, :, :], ctx.W("L2_attn_A_log")[:, :, :])
        p.need("a", dal)
        d_af = p.mark(a.activation(af[:, :, :], af[:, :, :], AF.Exp), "a")
        p.need("v", d_af)
        d_af = p.mark(v.tensor_scalar_mul(af[:, :, :], af[:, :, :], -1.0), "v")
        dcol = p.sb("dcol", (128, 16), F32)
        ddc = p.dma(("s", "r1"), dcol[:, :], ctx.W("L2_attn_D")[:, :])
        bdt = p.sb("bdt", (128, 16), F32)
        dbd = p.dma(("s", "r2"), bdt[:, :], ctx.W("L2_attn_b_dt")[:, :])
        wbc_ = p.sb("wbc_", (128, 2, 16, NS), BF16)
        dwb = p.dma(("g", "w0"), wbc_[:, 0, :, :],
                    ctx.W("L2_attn_W_B").rearrange("(k p) m -> p k m", p=128))
        dwc = p.dma(("g", "w1"), wbc_[:, 1, :, :],
                    ctx.W("L2_attn_W_C").rearrange("(k p) m -> p k m", p=128))
        wout, dwo = load_w(ctx, "L2_attn_W_out", D, stream=("g", "w2"))
        hcarry = p.sb("hcarry", (128, 16 * NS), F32)
        d_hc0 = p.mark(v.memset(hcarry[:, :], 0.0), "v")
        xcc = p.sb("xcc", (128, 16, TC), BF16)
        zcc = p.sb("zcc", (128, 16, TC), BF16)
        dtc = p.sb("dtc", (128, 16, TC), BF16)
        abuf = p.sb("abuf", (128, 16, TC), BF16)
        bbuf = p.sb("bbuf", (128, 16, TC), BF16)
        hbuf = p.sb("hbuf", (128, 16, TC), BF16)
        ybuf = p.sb("ybuf", (128, 16, TC), F32)
        ubuf = p.sb("ubuf", (128, 16, TC), BF16)
        ygb = p.sb("ygb", (128, 16, TC), BF16)
        Bt = p.sb("Bt", (128, 2 * NS), F32)
        Bf = p.sb("Bf", (1, 32, TC), BF16)
        stg32 = p.sb("stg32", (32, 128), BF16)
        Bbc = p.sb("Bbc", (128, TC), F32)
        Cbc = p.sb("Cbc", (128, TC), F32)
        wdt_ap = ctx.W("L2_attn_W_dt").rearrange("(k p) m -> p k m", p=128)
        wdtm = [p.sb("wdtm", (128, 16, 128), BF16) for _ in range(2)]
        xnt = [p.sb("xnt", (128, D), F32) for _ in range(2)]
        obx = [p.sb("obx", (128, D), F32) for _ in range(2)]
        scn = p.sb("scn", (128, 3 * TC), F32)
        prev = {"xc": None, "z": None, "dt": None, "ab": None, "h": None,
                "y": None, "Bt": None, "Bf": None, "bc": (), "u": None,
                "yg": None, "ob": [None, None], "xn": [None, None],
                "wd": [None, None], "sc": None, "pcread": None}
        for tci in range(NTC):
            ts0 = tci * TC
            tsl = slice(ts0, ts0 + TC)
            dxc = p.dma(("s", "a0"), xcc[:, :, :],
                        xcd.rearrange("(k p) t -> p k t", p=128)[:, :, tsl],
                        deps=(prev["u"], prev["yg"]))
            for m in range(16):
                c = m % 2
                b = m % 2
                dwm = p.dma(("g", f"wd{c}"), wdtm[c][:, :, :],
                            wdt_ap[:, :, m * 128:(m + 1) * 128],
                            deps=(prev["wd"][c],))
                pairs = [(wdtm[c][:, k, :], xcc[:, k, :]) for k in range(16)]
                mm = acc_mm(p, psA[b][:, 0:TC], pairs,
                            deps=(dwm, dxc) + tuple(pools["psA_free"][b]))
                prev["wd"][c] = mm
                p.need("v", mm, dbd, prev["sc"])
                d_t = p.mark(v.tensor_scalar_add(scn[:, 0:TC], psA[b][:, 0:TC],
                                                 bdt[:, m:m + 1]), "v")
                pools["psA_free"][b] = (d_t,)
                d_sp = softplus(ctx, scn[:, TC:2 * TC], scn[:, 0:TC],
                                scn[:, 2 * TC:3 * TC], deps=(d_t,))
                p.need("v", d_sp)
                d_dt = p.mark(v.tensor_copy(dtc[:, m, :], scn[:, TC:2 * TC]), "v")
                prev["sc"] = d_dt
            for b4 in range(TC // 128):
                t4 = slice(b4 * 128, (b4 + 1) * 128)
                mm = None
                for ci in range(2):
                    pairs = [(xcc[:, k, t4], wbc_[:, ci, k, :]) for k in range(16)]
                    mm = acc_mm(p, psA[ci][:, 0:NS], pairs,
                                deps=(dwb, dwc, dxc) + tuple(pools["psA_free"][ci]))
                    ev = evac(p, "v" if ci == 0 else "a",
                              Bt[:, ci * NS:(ci + 1) * NS], psA[ci][:, 0:NS],
                              deps=(mm, prev["Bt"]))
                    pools["psA_free"][ci] = (ev,)
                p.need("v", p.tip("a"))
                d_bt = p.mark(v.tensor_copy(scn.bitcast(BF16)[:, 0:32],
                                            Bt[:, :]), "v")
                p.need("pe", d_bt, pools["psT_free"][0])
                mmt = p.mark(pe.transpose(psT[0][0:32, 0:128],
                                          scn.bitcast(BF16)[:, 0:32],
                                          ctx.cst["ident_bf"][:, :]), "pe")
                prev["Bt"] = mmt
                ev = evac(p, "a", stg32[:, :], psT[0][0:32, 0:128],
                          deps=(mmt, prev["Bf"]))
                pools["psT_free"][0] = ev
                prev["Bf"] = p.dma(("s", "bf"), Bf[0:1, :, t4], stg32[:, :],
                                   deps=(ev,))
            p.need("v", d_hc0)
            dum = None
            p.need("v", dxc, prev["yg"])
            d_u = p.mark(v.tensor_tensor(ubuf[:, :, :], xcc[:, :, :],
                                         dtc[:, :, :], ALU.mult), "v")
            prev["u"] = d_u
            d_y0 = p.mark(v.memset(ybuf[:, :, :], 0.0), "v")
            for n in range(NS):
                d_ab = None
                for k in range(16):
                    p.need("a", prev["ab"], prev["h"],
                           p.tip("v") if k == 0 and n == 0 else None)
                    d_ab = p.mark(a.activation(abuf[:, k, :], dtc[:, k, :], AF.Exp,
                                               scale=af[:, k, n:n + 1]), "a")
                prev["ab"] = d_ab
                mmb = acc_mm(p, psC[:, 0:TC],
                             [(ctx.cst["ones_row_bf"][0:1, :],
                               Bf[0:1, n, :])],
                             deps=(prev["Bf"], prev["bc"]) + tuple(
                                 d for d in prev["ob"] if d is not None))
                mmc = acc_mm(p, psC[:, 512:512 + TC],
                             [(ctx.cst["ones_row_bf"][0:1, :],
                               Bf[0:1, NS + n, :])],
                             deps=(prev["Bf"],) + tuple(
                                 d for d in prev["ob"] if d is not None))
                p.need("v", mmb, mmc)
                d_bb = p.mark(v.tensor_copy(Bbc[:, :], psC[:, 0:TC]), "v")
                d_cc = p.mark(v.tensor_copy(Cbc[:, :], psC[:, 512:512 + TC]), "v")
                prev["bc"] = (d_bb, d_cc)
                p.need("v", d_u, d_bb)
                d_b = p.mark(v.tensor_tensor(
                    bbuf[:, :, :], ubuf[:, :, :],
                    Bbc[:, :].unsqueeze(1).broadcast_to((128, 16, TC)),
                    ALU.mult), "v")
                d_h = None
                p.need("v", d_ab, d_b, prev["h"])
                for k in range(16):
                    d_h = p.mark(v.tensor_tensor_scan(
                        hbuf[:, k, :], abuf[:, k, :], bbuf[:, k, :],
                        hcarry[:, k * NS + n:k * NS + n + 1],
                        ALU.mult, ALU.add), "v")
                p.need("v", d_h)
                for k in range(16):
                    v.tensor_copy(hcarry[:, k * NS + n:k * NS + n + 1],
                                  hbuf[:, k, TC - 1:TC])
                d_hm = p.mark(v.tensor_tensor(
                    hbuf[:, :, :], hbuf[:, :, :],
                    Cbc[:, :].unsqueeze(1).broadcast_to((128, 16, TC)),
                    ALU.mult), "v")
                p.need("v", d_hm, d_y0)
                d_y = p.mark(v.tensor_tensor(ybuf[:, :, :], ybuf[:, :, :],
                                             hbuf[:, :, :], ALU.add), "v")
                prev["h"] = d_y
            dzc = p.dma(("s", "b0"), zcc[:, :, :],
                        zd.rearrange("(k p) t -> p k t", p=128)[:, :, tsl],
                        deps=(prev["z"],))
            p.need("v", ddc)
            d_g = None
            for k in range(16):
                d_g = p.mark(v.scalar_tensor_tensor(ybuf[:, k, :], xcc[:, k, :],
                                                    dcol[:, k:k + 1], ybuf[:, k, :],
                                                    ALU.mult, ALU.add), "v")
            p.need("a", dzc, prev["yg"])
            d_sz = p.mark(a.activation(ygb[:, :, :], zcc[:, :, :], AF.Silu), "a")
            prev["z"] = d_sz
            p.need("v", d_g, d_sz)
            d_yg = p.mark(v.tensor_tensor(ygb[:, :, :], ybuf[:, :, :],
                                          ygb[:, :, :], ALU.mult), "v")
            prev["yg"] = d_yg
            for b4 in range(TC // 128):
                c = b4 % 2
                q = tci * (TC // 128) + b4
                qs = slice(q * 128, (q + 1) * 128)
                t4 = slice(b4 * 128, (b4 + 1) * 128)
                dxn_ = p.dma(("s", f"c{c}"), xnt[c][:, :], xn_f[qs, :],
                             deps=(prev["xn"][c],))
                mmf = None
                for h0 in range(2):
                    pairs = [(ygb[:, k2, t4], wout[:, k2, h0 * 512:(h0 + 1) * 512])
                             for k2 in range(16)]
                    mmf = acc_mm(p, psC[:, h0 * 512:(h0 + 1) * 512], pairs,
                                 deps=(dwo, d_yg, prev["bc"], prev["pcread"]) +
                                      (() if prev["ob"][c] is None
                                       else (prev["ob"][c],)))
                prev["bc"] = ()
                p.need("v", mmf, dxn_, prev["ob"][c])
                v.tensor_tensor(obx[c][:, 0:512], psC[:, 0:512],
                                xnt[c][:, 0:512], ALU.add)
                d_ob = p.mark(v.tensor_tensor(obx[c][:, 512:1024], psC[:, 512:1024],
                                              xnt[c][:, 512:1024], ALU.add), "v")
                prev["xn"][c] = d_ob
                prev["pcread"] = d_ob
                prev["ob"][c] = p.dma(("s", f"o{c}"), brA[qs, :], obx[c][:, :],
                                      deps=(d_ob,))


A_STAGES[2] = s_attn2






def s_attn0(ctx: Ctx, l: int):
    p = ctx.p
    v, a, pe = p.eng["v"], p.eng["a"], p.eng["pe"]
    xn_T = ctx.buf("xn0_T", (D, T), BF16)
    brA = ctx.buf("brA0", (T, D), F32, write=True)
    vfd = ctx.buf("l0a_vf", (T, D), F32, write=True)
    SC = 1.0 / 32.0
    with p.stage("attn0"):
        psA = [p.ps("psA", (128, 512), F32) for _ in range(2)]
        psB = p.ps("psB", (128, 1024), F32)
        psC = p.ps("psC", (128, 1024), F32)
        psT = [p.ps("psT", (128, 128), BF16) for _ in range(2)]
        qf = p.sb("qf", (128, KD, T), BF16)
        kf = p.sb("kf", (128, KD, T), BF16)
        vb = p.sb("vb", (128, NT, D), BF16)
        vtm = p.sb("vtm", (128, D), F32)
        subA = contextlib.ExitStack()
        wq, dwq = load_w(ctx, "L0_attn_Wq", D, stack=subA)
        wk, dwk = load_w(ctx, "L0_attn_Wk", D, stack=subA)
        wv, dwv = load_w(ctx, "L0_attn_Wv", D, stack=subA)
        xnf = p.sb("xnf", (128, KD, T), BF16, stack=subA)
        dxn = p.dma(("s", "ax"), xnf[:, :, :],
                    xn_T.rearrange("(k p) t -> p k t", p=128))
        psA_free = [(), ()]
        vf_saved = None
        vb_done = None
        for q in range(DBG.get("a0_v", NT)):
            pairs = [[], []]
            for h in range(2):
                for k in range(KD):
                    pairs[h].append((xnf[:, k, q * 128:(q + 1) * 128],
                                     wv[:, k, h * 512:(h + 1) * 512]))
            mms = []
            for h in range(2):
                mms.append(acc_mm(p, psA[h][:, :], pairs[h],
                                  deps=(dxn, dwv) + tuple(psA_free[h])))
            p.need("v", *mms, vf_saved)
            v.tensor_copy(vb[:, q, 0:512], psA[0][:, :])
            dvb = p.mark(v.tensor_copy(vtm[:, 0:512], psA[0][:, :]), "v")
            p.need("a", *mms)
            p.need("a", vf_saved)
            a.copy(vb[:, q, 512:1024], psA[1][:, :])
            dvt = p.mark(a.copy(vtm[:, 512:1024], psA[1][:, :]), "a")
            vf_saved = p.dma(("s", "vf"), vfd[q * 128:(q + 1) * 128, :], vtm[:, :],
                             deps=(dvb, dvt))
            psA_free = [(dvb,), (dvt,)]
            vb_done = (dvb, dvt)
        last_ev = {"q": None, "k": None}
        for (buf, wt, dw, nm) in ((qf, wq, dwq, "q"), (kf, wk, dwk, "k"))[:DBG.get("a0_qk", 2)]:
            for m in range(KD):
                for tsl in range(4):
                    t0 = tsl * 512
                    h = tsl % 2
                    pairs = [(wt[:, k, m * 128:(m + 1) * 128], xnf[:, k, t0:t0 + 512])
                             for k in range(KD)]
                    mm = acc_mm(p, psA[h][:, :], pairs,
                                deps=(dw, dxn) + tuple(psA_free[h]))
                    p.need("v", mm)
                    ev = p.mark(v.tensor_copy(buf[:, m, t0:t0 + 512], psA[h][:, :]), "v")
                    psA_free[h] = (ev,)
                    last_ev[nm] = ev
        p.bar()
        subA.close()
        subB = contextlib.ExitStack()
        if DBG.get("a0_skipB"):
            p.bar()
            subB.close()
            return
        wfc, dwfc = load_w(ctx, "L0_attn_Wfc", D, stack=subB)
        gm, dgm = bcast_row(ctx, "L0_attn_gamma", psC, stack=subB)
        bt, dbt = bcast_row(ctx, "L0_attn_beta", psC, deps=(dgm,), stack=subB)
        bf, dbf = bcast_row(ctx, "L0_attn_bfc", psC, deps=(dbt,), stack=subB)
        S = p.sb("S", (128, T), F32)
        P = p.sb("P", (128, T), BF16)
        PT = p.sb("PT", (128, T), BF16)
        o1 = p.sb("o1", (128, D), F32)
        lnf = p.sb("lnf", (128, D), F32)
        lnb = p.sb("lnb", (128, D), BF16)
        lnfm = p.sb("lnfm", (128, KD, 128), BF16)
        vft = [p.sb("vft", (128, D), F32) for _ in range(2)]
        ob = [p.sb("ob", (128, D), F32) for _ in range(2)]
        stt_ = p.sb("st2", (128, 4), F32)
        sq2 = p.sb("sq2", (128, D), F32)
        rden = p.sb("rden", (128, 2), F32)
        vft_free = [None, None]
        ob_free = [None, None]
        exp_prev = None
        pt_mm_prev = None
        pv_prev = None
        psB_free = None
        psC_free = dbf
        psT_free = [None, None]
        o1_sq_prev = None
        lnfm_mm_prev = None
        prev_nf = None
        for q in range(DBG.get("a0_q", NT)):
            c = q % 2
            kw = (q + 1) * 128
            s = slice(q * 128, (q + 1) * 128)
            dvf = p.dma(("s", f"a{c}"), vft[c][:, :], vfd[s, :],
                        deps=(vf_saved, vft_free[c]))
            ev_S = None
            for g0 in range(0, kw, 512):
                w_ = min(512, kw - g0)
                h = (g0 // 512) % 2
                pairs = [(qf[:, k, s], kf[:, k, g0:g0 + w_]) for k in range(KD)]
                mm = acc_mm(p, psA[h][:, 0:w_], pairs,
                            deps=(last_ev["q"], last_ev["k"]) + tuple(psA_free[h]))
                p.need("v", mm, exp_prev)
                ev_S = p.mark(v.tensor_copy(S[:, g0:g0 + w_], psA[h][:, 0:w_]), "v")
                psA_free[h] = (ev_S,)
            d_msk = p.mark(v.tensor_tensor(S[:, s], S[:, s], ctx.cst["causal_add"][:, :],
                                           ALU.add), "v")
            d_rm = p.mark(v.reduce_max(stt_[:, 0:1], S[:, 0:kw], axis=AX.X, negate=True), "v")
            p.need("v", d_rm)
            d_nm = p.mark(v.tensor_scalar_mul(stt_[:, 1:2], stt_[:, 0:1], SC), "v")
            p.need("a", d_nm, pt_mm_prev)
            d_exp = p.mark(a.activation(P[:, 0:kw], S[:, 0:kw], AF.Exp,
                                        bias=stt_[:, 1:2], scale=p.constf(SC),
                                        accum_out=rden[:, 1:2]), "a")
            exp_prev = d_exp
            p.need("v", d_exp)
            d_rd = p.mark(v.reciprocal(rden[:, 0:1], rden[:, 1:2]), "v")
            ev_pt = None
            for kt in range(q + 1):
                tc = kt % 2
                p.need("pe", d_exp, psT_free[tc], pv_prev)
                mm = p.mark(pe.transpose(psT[tc][:, 0:128], P[:, kt * 128:(kt + 1) * 128],
                                         ctx.cst["ident_bf"][:, :]), "pe")
                pt_mm_prev = mm
                p.need("a", mm)
                ev_pt = p.mark(a.copy(PT[:, kt * 128:(kt + 1) * 128], psT[tc][:, 0:128]), "a")
                psT_free[tc] = ev_pt
            for h in range(2):
                pairs = [(PT[:, kt * 128:(kt + 1) * 128], vb[:, kt, h * 512:(h + 1) * 512])
                         for kt in range(q + 1)]
                mm = acc_mm(p, psB[:, h * 512:(h + 1) * 512], pairs,
                            deps=(ev_pt, psB_free) + tuple(vb_done or ()))
                pv_prev = mm
            p.need("v", pv_prev, d_rd, dvf, o1_sq_prev)
            d_o1 = p.mark(v.scalar_tensor_tensor(o1[:, :], psB[:, :], rden[:, 0:1],
                                                 vft[c][:, :], ALU.mult, ALU.add), "v")
            psB_free = d_o1
            vft_free[c] = d_o1
            d_st = ln_stats(ctx, o1[:, :], stt_[:, 0:2], stt_[:, 2:4], sq2, 1e-5,
                            deps=(d_o1, prev_nf))
            o1_sq_prev = d_st[0]
            p.need("v", *d_st)
            d_nf = p.mark(v.tensor_scalar(lnf[:, :], o1[:, :], stt_[:, 0:1], stt_[:, 2:3],
                                          ALU.subtract, ALU.mult), "v")
            prev_nf = d_nf
            p.need("v", dgm, dbt)
            v.tensor_tensor(lnf[:, :], lnf[:, :], gm[:, :], ALU.mult)
            d_ln = p.mark(v.tensor_tensor(lnf[:, :], lnf[:, :], bt[:, :], ALU.add), "v")
            p.need("a", d_ln, lnfm_mm_prev)
            d_lb = p.mark(a.copy(lnb[:, :], lnf[:, :]), "a")
            ev_fm = None
            for k in range(KD):
                tc = k % 2
                p.need("pe", d_lb, psT_free[tc])
                mm = p.mark(pe.transpose(psT[tc][:, 0:128], lnb[:, k * 128:(k + 1) * 128],
                                         ctx.cst["ident_bf"][:, :]), "pe")
                p.need("a", mm)
                ev_fm = p.mark(a.copy(lnfm[:, k, :], psT[tc][:, 0:128]), "a")
                psT_free[tc] = ev_fm
            mm_f = None
            for h in range(2):
                pairs = [(lnfm[:, k, :], wfc[:, k, h * 512:(h + 1) * 512])
                         for k in range(KD)]
                mm_f = acc_mm(p, psC[:, h * 512:(h + 1) * 512], pairs,
                              deps=(ev_fm, dwfc, psC_free))
            lnfm_mm_prev = mm_f
            p.need("v", mm_f, dbf, ob_free[c])
            d_ob = p.mark(v.tensor_tensor(ob[c][:, :], psC[:, :], bf[:, :], ALU.add), "v")
            psC_free = d_ob
            ob_free[c] = p.dma(("s", f"o{c}"), brA[s, :], ob[c][:, :], deps=(d_ob,))
        p.bar()
        subB.close()


A_STAGES[0] = s_attn0


def swiglu_fm(ctx, h_out, wg, wu, xf, deps, psA, psA_free, state,
              bias_g=None, bias_u=None, act=AF.Silu, tlen=T):
    p = ctx.p
    v, a = p.eng["v"], p.eng["a"]
    nm = h_out.shape[1]
    nk = wg.shape[1]
    if "sg" not in state:
        state["sg"] = p.sb("sg", (128, 512), F32)
        state["sg_prev"] = None
    sg = state["sg"]
    last = None
    for m in range(nm):
        for t0 in range(0, tlen, 512):
            w_ = min(512, tlen - t0)
            pairs_g = [(wg[:, k, m * 128:(m + 1) * 128], xf[:, k, t0:t0 + w_])
                       for k in range(nk)]
            mm_g = acc_mm(p, psA[0][:, 0:w_], pairs_g,
                          deps=tuple(deps) + tuple(psA_free[0]))
            p.need("a", mm_g, state["sg_prev"])
            kw = {} if bias_g is None else {"bias": bias_g[:, m:m + 1]}
            d_sg = p.mark(a.activation(sg[:, 0:w_], psA[0][:, 0:w_], act, **kw), "a")
            psA_free[0] = (d_sg,)
            pairs_u = [(wu[:, k, m * 128:(m + 1) * 128], xf[:, k, t0:t0 + w_])
                       for k in range(nk)]
            mm_u = acc_mm(p, psA[1][:, 0:w_], pairs_u,
                          deps=tuple(deps) + tuple(psA_free[1]))
            p.need("v", mm_u, d_sg)
            if bias_u is None:
                d_h = p.mark(v.tensor_tensor(h_out[:, m, t0:t0 + w_], sg[:, 0:w_],
                                             psA[1][:, 0:w_], ALU.mult), "v")
            else:
                d_h = p.mark(v.scalar_tensor_tensor(
                    h_out[:, m, t0:t0 + w_], psA[1][:, 0:w_], bias_u[:, m:m + 1],
                    sg[:, 0:w_], ALU.add, ALU.mult), "v")
            psA_free[1] = (d_h,)
            state["sg_prev"] = d_h
            last = d_h
    return last


def rms_tile(ctx, x_ap, wbc, st4, scr, eps, deps=(), n=D):
    p = ctx.p
    v, a = p.eng["v"], p.eng["a"]
    p.need("a", *deps)
    d_sq = p.mark(a.activation(scr[:, 0:n], x_ap, AF.Square,
                               accum_out=st4[:, 0:1]), "a")
    a.activation(st4[:, 1:2], st4[:, 0:1], AF.Ln, bias=p.constf(eps * n))
    d_r = p.mark(a.activation(st4[:, 2:3], st4[:, 1:2], AF.Exp,
                              scale=p.constf(-0.5)), "a")
    p.need("v", d_r, *deps)
    d = p.mark(v.tensor_scalar(x_ap, x_ap, st4[:, 2:3], float(np.sqrt(n)),
                               ALU.mult, ALU.mult), "v")
    if wbc is not None:
        d = p.mark(v.tensor_tensor(x_ap, x_ap, wbc[:, 0:n], ALU.mult), "v")
    return d


def rope_interleaved_tm(ctx, out_bf, xin, cosb, sinb, scrA, scrB, H, deps=()):
    p = ctx.p
    v = p.eng["v"]
    n = H * 32
    xr = xin.rearrange("p (h f two) -> p h f two", h=H, two=2)
    orr = out_bf.rearrange("p (h f two) -> p h f two", h=H, two=2)
    x1 = xr[:, :, :, 0]
    x2 = xr[:, :, :, 1]
    cb = cosb.unsqueeze(1).broadcast_to((128, H, 32))
    sb_ = sinb.unsqueeze(1).broadcast_to((128, H, 32))
    sA = scrA[:, 0:n].rearrange("p (h f) -> p h f", h=H)
    sB = scrB[:, 0:n].rearrange("p (h f) -> p h f", h=H)
    p.need("v", *deps)
    v.tensor_tensor(sA, x1, cb, ALU.mult)
    v.tensor_tensor(sB, x2, sb_, ALU.mult)
    d1 = p.mark(v.tensor_tensor(orr[:, :, :, 0], sA, sB, ALU.subtract), "v")
    v.tensor_tensor(sA, x1, sb_, ALU.mult)
    v.tensor_tensor(sB, x2, cb, ALU.mult)
    d2 = p.mark(v.tensor_tensor(orr[:, :, :, 1], sA, sB, ALU.add), "v")
    return d2


def s_ffn0(ctx: Ctx, l: int):
    p = ctx.p
    v, a, pe = p.eng["v"], p.eng["a"], p.eng["pe"]
    xn_T = ctx.buf("xn0_T", (D, T), BF16)
    xn_f = ctx.buf("xn0_f", (T, D), F32)
    brB = ctx.buf("brB0", (T, D), F32, write=True)
    x1d = ctx.buf("l0f_x1", (T, D), F32, write=True)
    accd = ctx.buf("l0f_acc", (T, D), F32, write=True)
    with p.stage("ffn0"):
        pools = mha_pools(p)
        psC = p.ps("psC", (128, 1024), F32)
        x1f = p.sb("x1f", (128, KD, T), BF16)
        rope_c = p.sb("rope_c", (128, NT, 32), F32)
        rope_s = p.sb("rope_s", (128, NT, 32), F32)
        drc = p.dma(("s", "r0"), rope_c[:, :, :],
                    ctx.W("cst_rope0_cos").rearrange("(q p) f -> p q f", p=128))
        drs = p.dma(("s", "r1"), rope_s[:, :, :],
                    ctx.W("cst_rope0_sin").rearrange("(q p) f -> p q f", p=128))
        qTd = ctx.buf("l0f_qT", (D, T), BF16, write=True)
        subQ = contextlib.ExitStack()
        kf = p.sb("kf", (128, KD, T), BF16, stack=subQ)
        vb = p.sb("vb", (128, NT, D), BF16, stack=subQ)
        subQ2 = contextlib.ExitStack()
        kcf = p.sb("kcf", (128, 2, T), BF16, stack=subQ2)
        vcf = p.sb("vcf", (128, 2, T), BF16, stack=subQ2)
        subX = contextlib.ExitStack()
        xnf = p.sb("xnf", (128, KD, T), BF16, stack=subX)
        dxn = p.dma(("s", "ax"), xnf[:, :, :],
                    xn_T.rearrange("(k p) t -> p k t", p=128))
        rope_c = p.sb("rope_c", (128, NT, 32), F32, stack=subX)
        rope_s = p.sb("rope_s", (128, NT, 32), F32, stack=subX)
        drc = p.dma(("s", "r0"), rope_c[:, :, :],
                    ctx.W("cst_rope0_cos").rearrange("(q p) f -> p q f", p=128))
        drs = p.dma(("s", "r1"), rope_s[:, :, :],
                    ctx.W("cst_rope0_sin").rearrange("(q p) f -> p q f", p=128))
        qtm = p.sb("qtm", (128, D), F32, stack=subX)
        qbt = p.sb("qbt", (128, D), BF16, stack=subX)
        scrA = p.sb("scrA", (128, 512), F32, stack=subX)
        scrB = p.sb("scrB", (128, 512), F32, stack=subX)
        tbq = [p.sb("tbq", (128, KD, 128), BF16, stack=subX) for _ in range(2)]
        qT3 = qTd.rearrange("(k p) t -> p k t", p=128)
        subW = contextlib.ExitStack()
        wkc, dwkc = load_w(ctx, "L0_ffn_Wk_c", 256, stream=("g", "w1"), stack=subW)
        wvc, dwvc = load_w(ctx, "L0_ffn_Wv_c", 256, stream=("g", "w2"), stack=subW)
        d_kc = proj_fm(ctx, kcf, wkc, 0, xnf, (dwkc, dxn), pools["psA"],
                       pools["psA_free"])
        d_vc = proj_fm(ctx, vcf, wvc, 0, xnf, (dwvc, dxn), pools["psA"],
                       pools["psA_free"])
        p.bar()
        subW.close()
        subW = contextlib.ExitStack()
        wv_, dwv = load_w(ctx, "L0_ffn_Wv", D, stream=("g", "w2"), stack=subW)
        d_vb = proj_tm(ctx, vb, wv_, 0, D, vcf, (dwv,) + d_vc, pools["psA"],
                       pools["psA_free"])
        p.bar()
        subW.close()
        for pass_i in range(2):
            subW = contextlib.ExitStack()
            if pass_i == 0:
                wt, dwt = load_w(ctx, "L0_ffn_Wq", D, stream=("g", "w0"), stack=subW)
                srcf, nk = xnf, KD
            else:
                wt, dwt = load_w(ctx, "L0_ffn_Wk", D, stream=("g", "w1"), stack=subW)
                srcf, nk = kcf, 2
            rp_prev = None
            q_prev = None
            tb_free = [None, None]
            for q in range(NT):
                c = q % 2
                qs = slice(q * 128, (q + 1) * 128)
                mmq = None
                for h0 in range(2):
                    pairs = [(srcf[:, k, qs], wt[:, k, h0 * 512:(h0 + 1) * 512])
                             for k in range(nk)]
                    mmq = acc_mm(p, psC[:, h0 * 512:(h0 + 1) * 512], pairs,
                                 deps=(dwt, dxn, q_prev) + d_kc)
                p.need("v", mmq, drc, drs)
                d_q = p.mark(v.tensor_copy(qtm[:, :], psC[:, :]), "v")
                q_prev = d_q
                d_rp = rope_interleaved_tm(ctx, qbt, qtm[:, :], rope_c[:, q, :],
                                           rope_s[:, q, :], scrA, scrB, 16,
                                           deps=(d_q, rp_prev))
                last_cp = None
                for k in range(KD):
                    tc = k % 2
                    p.need("pe", d_rp, pools["psT_free"][tc])
                    mmt = p.mark(pe.transpose(pools["psT"][tc][:, 0:128],
                                              qbt[:, k * 128:(k + 1) * 128],
                                              ctx.cst["ident_bf"][:, :]), "pe")
                    if k == KD - 1:
                        rp_prev = mmt
                    if pass_i == 0:
                        ev = evac(p, "a", tbq[c][:, k, :], pools["psT"][tc][:, 0:128],
                                  deps=(mmt, tb_free[c]))
                        last_cp = ev
                    else:
                        ev = evac(p, "v" if tc == 0 else "a", kf[:, k, qs],
                                  pools["psT"][tc][:, 0:128], deps=(mmt,))
                    pools["psT_free"][tc] = ev
                if pass_i == 0:
                    tb_free[c] = p.dma(("s", f"tc{c}"), qT3[:, :, qs],
                                       tbq[c][:, :, :], deps=(last_cp,))
            p.bar()
            subW.close()
        p.bar()
        subX.close()
        subQ2.close()
        subB = contextlib.ExitStack()
        wo, dwo = load_w(ctx, "L0_ffn_Wo", D, stream=("g", "w0"), stack=subB)
        n1bc, dn1 = bcast_row(ctx, "L0_ffn_norm1_w", psC, stack=subB)
        ob = [p.sb("ob", (128, D), F32, stack=subB) for _ in range(2)]
        xnt = [p.sb("xnt", (128, D), F32, stack=subB) for _ in range(2)]
        otb = p.sb("otb", (128, D), BF16, stack=subB)
        x1b = p.sb("x1b", (128, D), BF16, stack=subB)
        lnfm = p.sb("lnfm", (128, KD, 128), BF16, stack=subB)
        st4 = p.sb("st4", (128, 4), F32, stack=subB)
        sqs = p.sb("sqs", (128, D), F32, stack=subB)
        state = {"psC_free": dn1, "fm_mm_prev": None, "cp_prev": None,
                 "ob_free": [None, None]}
        xnt_free = [None, None]
        x1t_prev = [None]

        def x1_hook(q, c, d_ob):
            s = slice(q * 128, (q + 1) * 128)
            dxl = p.dma(("s", f"b{c}"), xnt[c][:, :], xn_f[s, :],
                        deps=(xnt_free[c],))
            p.need("v", d_ob, dxl)
            d1 = p.mark(v.tensor_tensor(ob[c][:, :], ob[c][:, :], xnt[c][:, :],
                                        ALU.add), "v")
            xnt_free[c] = d1
            d_x1 = rms_tile(ctx, ob[c][:, :], n1bc, st4, sqs, 1e-6,
                            deps=(d1, x1t_prev[0], dn1))
            x1t_prev[0] = d_x1
            p.need("a", d_x1)
            d_cb = p.mark(a.copy(x1b[:, :], ob[c][:, :]), "a")
            for k in range(KD):
                tc = k % 2
                p.need("pe", d_cb, pools["psT_free"][tc])
                mmt = p.mark(pe.transpose(pools["psT"][tc][:, 0:128],
                                          x1b[:, k * 128:(k + 1) * 128],
                                          ctx.cst["ident_bf"][:, :]), "pe")
                ev = evac(p, "v" if tc == 0 else "a", x1f[:, k, s],
                          pools["psT"][tc][:, 0:128], deps=(mmt,))
                pools["psT_free"][tc] = ev
            return (d_x1, d_cb)

        def emit_out(q, c, ot, d_o):
            return tm_to_fm_out(ctx, pools, wo, dwo, None, dn1, psC, otb, lnfm,
                                ob, x1d, state, q, c, ot, d_o,
                                extra_add=x1_hook)

        subM = contextlib.ExitStack()
        mha_core(ctx, None, kf, vb, pools, H=16, dk=64, dv=64, causal=False,
                 scale=0.125, in_deps=(p.tip("v"), p.tip("a")),
                 emit_out=emit_out, q_dram=qTd, arena=subM)
        p.bar()
        subM.close()
        subB.close()
        subQ.close()
        wsi_ap = ctx.W("L0_ffn_W_shared_in").rearrange("(k p) m -> p k m", p=128)
        w1s_ap = ctx.W("L0_ffn_W1_shared").rearrange("(k p) m -> p k m", p=128)
        sw_state = {}
        subH = contextlib.ExitStack()
        hbuf = p.sb("hbuf", (128, 16, T), BF16, stack=subH)
        ob2 = [p.sb("ob2", (128, D), F32, stack=subH) for _ in range(2)]
        ob2_free = [None, None]
        for phase in range(2):
            subW = contextlib.ExitStack()
            wg = p.sb("wg", (128, KD, 2048), BF16, stack=subW)
            wu = p.sb("wu", (128, KD, 2048), BF16, stack=subW)
            src_ap = wsi_ap if phase == 0 else w1s_ap
            dwg = p.dma(("g", "w1"), wg[:, :, :], src_ap[:, :, 0:2048])
            dwu = p.dma(("g", "w2"), wu[:, :, :], src_ap[:, :, 2048:4096])
            swiglu_fm(ctx, hbuf, wg, wu, x1f, (dwg, dwu, p.tip("v"), p.tip("a")),
                      pools["psA"], pools["psA_free"], sw_state)
            p.bar()
            subW.close()
            if phase == 0:
                subW2 = contextlib.ExitStack()
                wso, dwso = load_w(ctx, "L0_ffn_W_shared_out", D,
                                   stream=("g", "w1"), stack=subW2)
                psB_free = None
                for q in range(NT):
                    c = q % 2
                    qs = slice(q * 128, (q + 1) * 128)
                    mmd = None
                    for h0 in range(2):
                        pairs = [(hbuf[:, k2, qs],
                                  wso[:, k2, h0 * 512:(h0 + 1) * 512])
                                 for k2 in range(16)]
                        mmd = acc_mm(p, psC[:, h0 * 512:(h0 + 1) * 512], pairs,
                                     deps=(dwso, psB_free, p.tip("v")))
                    p.need("v", mmd, ob2_free[c])
                    d_ob = p.mark(v.tensor_copy(ob2[c][:, :], psC[:, :]), "v")
                    psB_free = d_ob
                    ob2_free[c] = p.dma(("s", f"o{c}"), accd[qs, :], ob2[c][:, :],
                                        deps=(d_ob,))
                p.bar()
                subW2.close()
        subR = contextlib.ExitStack()
        wgg = p.sb("wgg", (128, KD, 2), BF16, stack=subR)
        weg = p.sb("weg", (128, KD, 8), BF16, stack=subR)
        dgg = p.dma(("g", "w1"), wgg[:, :, :],
                    ctx.W("L0_ffn_Wg_gate").rearrange("(k p) m -> p k m", p=128))
        deg = p.dma(("g", "w2"), weg[:, :, :],
                    ctx.W("L0_ffn_We_gate").rearrange("(k p) m -> p k m", p=128))
        gbias, dgb = bcast_row(ctx, "L0_ffn_group_bias", psC, stack=subR)
        ebias, deb = bcast_row(ctx, "L0_ffn_expert_bias", psC, deps=(dgb,), stack=subR)
        egrp, dge = bcast_row(ctx, "cst_egroup", psC, deps=(deb,), stack=subR)
        cmb = p.sb("cmb", (128, NT, 8), F32)
        lg = p.sb("lg", (128, 16), F32)
        ev_prev = dge
        for q in range(NT):
            qs = slice(q * 128, (q + 1) * 128)
            pairs = [(x1f[:, k, qs], wgg[:, k, :]) for k in range(KD)]
            mm1 = acc_mm(p, psC[:, 0:2], pairs, deps=(dgg, ev_prev, p.tip("a")))
            pairs = [(x1f[:, k, qs], weg[:, k, :]) for k in range(KD)]
            mm2 = acc_mm(p, psC[:, 512:520], pairs, deps=(deg,))
            p.need("v", mm1, mm2, dgb, deb, dge)
            v.tensor_tensor(lg[:, 0:2], psC[:, 0:2], gbias[:, 0:2], ALU.add)
            d_lg = p.mark(v.tensor_tensor(lg[:, 2:10], psC[:, 512:520],
                                          ebias[:, 0:8], ALU.add), "v")
            ev_prev = d_lg
            d_gm = p.mark(v.reduce_max(lg[:, 10:11], lg[:, 0:2], axis=AX.X,
                                       negate=True), "v")
            p.need("a", d_gm)
            d_eg = p.mark(a.activation(lg[:, 12:14], lg[:, 0:2], AF.Exp,
                                       bias=lg[:, 10:11],
                                       accum_out=lg[:, 11:12]), "a")
            p.need("v", d_eg)
            v.reciprocal(lg[:, 14:15], lg[:, 11:12])
            v.reduce_max(lg[:, 15:16], lg[:, 12:14], axis=AX.X)
            v.tensor_tensor(lg[:, 10:11], lg[:, 15:16], lg[:, 14:15], ALU.mult)
            v.tensor_scalar(lg[:, 11:12], lg[:, 1:2], lg[:, 0:1], None, ALU.is_gt)
            v.tensor_scalar(cmb[:, q, :], egrp[:, 0:8], lg[:, 11:12], None,
                            ALU.is_equal)
            v.tensor_scalar(cmb[:, q, :], cmb[:, q, :], 1.0e30, -1.0e30,
                            ALU.mult, ALU.add)
            v.tensor_tensor(lg[:, 2:10], lg[:, 2:10], cmb[:, q, :], ALU.add)
            d_m8 = p.mark(v.reduce_max(lg[:, 12:13], lg[:, 2:10], axis=AX.X,
                                       negate=True), "v")
            p.need("a", d_m8)
            d_e8 = p.mark(a.activation(lg[:, 2:10], lg[:, 2:10], AF.Exp,
                                       bias=lg[:, 12:13],
                                       accum_out=lg[:, 13:14]), "a")
            p.need("v", d_e8)
            v.reciprocal(lg[:, 14:15], lg[:, 13:14])
            v.tensor_scalar(lg[:, 2:10], lg[:, 2:10], lg[:, 14:15], lg[:, 10:11],
                            ALU.mult, ALU.mult)
            v.reduce_max(lg[:, 12:13], lg[:, 2:10], axis=AX.X)
            v.tensor_scalar(cmb[:, q, :], lg[:, 2:10], lg[:, 12:13], None,
                            ALU.is_ge)
            v.scalar_tensor_tensor(lg[:, 2:10], cmb[:, q, :], -1.0e30, lg[:, 2:10],
                                   ALU.mult, ALU.add)
            v.reduce_max(lg[:, 13:14], lg[:, 2:10], axis=AX.X)
            v.tensor_scalar(lg[:, 2:10], lg[:, 2:10], lg[:, 13:14], None,
                            ALU.is_ge)
            v.tensor_scalar_mul(lg[:, 2:10], lg[:, 2:10], lg[:, 13:14])
            p.mark(v.scalar_tensor_tensor(cmb[:, q, :], cmb[:, q, :],
                                          lg[:, 12:13], lg[:, 2:10],
                                          ALU.mult, ALU.add), "v")
        if DBG.get("dump_cmb"):
            dcmb = ctx.buf("l0f_cmb", (128, NT * 8), F32, write=True)
            p.dma(("s", "o0"), dcmb[:, :], cmb[:, :, :].rearrange("p q e -> p (q e)"),
                  deps=(p.tip("v"),))
        p.bar()
        subR.close()
        w2_ap = ctx.W("L0_ffn_W2_experts").rearrange("e (k p) m -> e p k m", p=128)
        for e in range(8):
            subW2 = contextlib.ExitStack()
            w2 = p.sb("w2", (128, 16, D), BF16, stack=subW2)
            dw2 = p.dma(("g", "w1"), w2[:, :, :], w2_ap[e])
            psB_free = None
            for q in range(NT):
                c = q % 2
                qs = slice(q * 128, (q + 1) * 128)
                mmd = None
                for h0 in range(2):
                    pairs = [(hbuf[:, k2, qs], w2[:, k2, h0 * 512:(h0 + 1) * 512])
                             for k2 in range(16)]
                    mmd = acc_mm(p, psC[:, h0 * 512:(h0 + 1) * 512], pairs,
                                 deps=(dw2, psB_free, p.tip("v")))
                p.need("v", mmd, ob2_free[c])
                d_ob = p.mark(v.tensor_scalar_mul(ob2[c][:, :], psC[:, :],
                                                  cmb[:, q, e:e + 1]), "v")
                psB_free = d_ob
                ob2_free[c] = p.dma(("g", f"x{c}"), accd[qs, :], ob2[c][:, :],
                                    deps=(d_ob,), accum_op=ALU.add)
            p.bar()
            subW2.close()
        p.bar()
        subH.close()
        n2bc, dn2 = bcast_row(ctx, "L0_ffn_norm2_w", psC)
        xt = [p.sb("xt", (128, D), F32) for _ in range(2)]
        at = [p.sb("at", (128, D), F32) for _ in range(2)]
        st4b = p.sb("st4b", (128, 4), F32)
        sqb = p.sb("sqb", (128, D), F32)
        fr = [None, None]
        rprev = None
        for q in range(NT):
            c = q % 2
            qs = slice(q * 128, (q + 1) * 128)
            d1 = p.dma(("s", f"a{c}"), xt[c][:, :], x1d[qs, :], deps=(fr[c],))
            d2 = p.dma(("s", f"b{c}"), at[c][:, :], accd[qs, :], deps=(fr[c],))
            p.need("v", d1, d2)
            d3 = p.mark(v.tensor_tensor(xt[c][:, :], xt[c][:, :], at[c][:, :],
                                        ALU.add), "v")
            d4 = rms_tile(ctx, xt[c][:, :], n2bc, st4b, sqb, 1e-6,
                          deps=(d3, rprev, dn2))
            rprev = d4
            fr[c] = p.dma(("s", f"o{c}"), brB[qs, :], xt[c][:, :], deps=(d4,))


F_STAGES[0] = s_ffn0

def s_ffn1(ctx: Ctx, l: int):
    p = ctx.p
    v, a, pe = p.eng["v"], p.eng["a"], p.eng["pe"]
    xn_T = ctx.buf("xn1_T", (D, T), BF16)
    xn_f = ctx.buf("xn1_f", (T, D), F32)
    brB = ctx.buf("brB1", (T, D), F32, write=True)
    accd = ctx.buf("l1f_acc", (T, D), F32, write=True)
    with p.stage("ffn1"):
        pools = mha_pools(p)
        psC = p.ps("psC", (128, 1024), F32)
        mlaf = p.sb("mlaf", (128, KD, T), BF16)
        cmb = p.sb("cmb", (128, NT, 8), F32)
        cmbT = p.sb("cmbT", (8, NT * 128), BF16)
        cmb_b = p.sb("cmb_b", (128, 8), BF16)
        subQ = contextlib.ExitStack()
        qf = p.sb("qf", (128, 4, T), BF16, stack=subQ)
        kf = p.sb("kf", (128, 4, T), BF16, stack=subQ)
        vb = p.sb("vb", (128, NT, 512), BF16, stack=subQ)
        subX = contextlib.ExitStack()
        xnf = p.sb("xnf", (128, KD, T), BF16, stack=subX)
        dxn = p.dma(("s", "ax"), xnf[:, :, :],
                    xn_T.rearrange("(k p) t -> p k t", p=128))
        wq, dwq = load_w(ctx, "L1_ffn_q_w_c", 512, stream=("g", "w0"), stack=subX)
        wk, dwk = load_w(ctx, "L1_ffn_k_w", 512, stream=("g", "w1"), stack=subX)
        wv, dwv = load_w(ctx, "L1_ffn_v_w", 512, stream=("g", "w2"), stack=subX)
        qbf = p.sb("qbf", (128, 4), F32, stack=subX)
        kbf = p.sb("kbf", (128, 4), F32, stack=subX)
        dqb = p.dma(("s", "r0"), qbf[:, :], ctx.W("L1_ffn_q_b_f")[:, :])
        dkb = p.dma(("s", "r1"), kbf[:, :], ctx.W("L1_ffn_k_b_f")[:, :])
        vbc, dvb0 = bcast_row(ctx, "L1_ffn_v_b", psC, stack=subX)
        dq = proj_fm(ctx, qf, wq, 0, xnf, (dwq, dxn, dqb), pools["psA"],
                     pools["psA_free"], bias_fm=qbf)
        dk_ = proj_fm(ctx, kf, wk, 0, xnf, (dwk, dxn, dkb), pools["psA"],
                      pools["psA_free"], bias_fm=kbf)
        dv_ = proj_tm(ctx, vb, wv, 0, 512, xnf, (dwv, dxn, dvb0), pools["psA"],
                      pools["psA_free"], bias_bc=vbc)
        p.bar()
        subX.close()
        subB = contextlib.ExitStack()
        wo, dwo = load_w(ctx, "L1_ffn_o_w_c", D, stream=("g", "w0"), stack=subB)
        obc, dob = bcast_row(ctx, "L1_ffn_o_b", psC, stack=subB)
        ob = [p.sb("ob", (128, D), F32, stack=subB) for _ in range(2)]
        otb = p.sb("otb", (128, 512), BF16, stack=subB)
        mlb = p.sb("mlb", (128, D), BF16, stack=subB)
        lnfm = p.sb("lnfm", (128, 4, 128), BF16, stack=subB)
        state = {"psC_free": dob, "fm_mm_prev": None, "cp_prev": None,
                 "ob_free": [None, None]}

        def mla_hook(q, c, d_ob):
            s = slice(q * 128, (q + 1) * 128)
            p.need("a", d_ob)
            d_cb = p.mark(a.copy(mlb[:, :], ob[c][:, :]), "a")
            for k in range(KD):
                tc = k % 2
                p.need("pe", d_cb, pools["psT_free"][tc])
                mmt = p.mark(pe.transpose(pools["psT"][tc][:, 0:128],
                                          mlb[:, k * 128:(k + 1) * 128],
                                          ctx.cst["ident_bf"][:, :]), "pe")
                ev = evac(p, "v" if tc == 0 else "a", mlaf[:, k, s],
                          pools["psT"][tc][:, 0:128], deps=(mmt,))
                pools["psT_free"][tc] = ev
            return (d_ob, d_cb)

        def emit_out(q, c, ot, d_o):
            return tm_to_fm_out(ctx, pools, wo, dwo, obc, dob, psC, otb, lnfm,
                                ob, None, state, q, c, ot, d_o, nko=4, ncol=D,
                                extra_add=mla_hook, store=False)

        subM = contextlib.ExitStack()
        mha_core(ctx, qf, kf, vb, pools, H=16, dk=32, dv=32, causal=False,
                 scale=1.0 / float(np.sqrt(32.0)),
                 in_deps=dq + dk_ + dv_, emit_out=emit_out, arena=subM)
        p.bar()
        subM.close()
        subB.close()
        subQ.close()
        subR = contextlib.ExitStack()
        rw = p.sb("rw", (128, KD, 8), BF16, stack=subR)
        drw = p.dma(("g", "w0"), rw[:, :, :],
                    ctx.W("L1_ffn_router_w").rearrange("(k p) m -> p k m", p=128))
        rbc, drb = bcast_row(ctx, "L1_ffn_router_b", psC, stack=subR)
        lg = p.sb("lg", (128, 16), F32, stack=subR)
        ev_prev = drb
        cT_prev = None
        for q in range(NT):
            qs = slice(q * 128, (q + 1) * 128)
            pairs = [(mlaf[:, k, qs], rw[:, k, :]) for k in range(KD)]
            mm = acc_mm(p, psC[:, 0:8], pairs, deps=(drw, ev_prev, p.tip("a")))
            p.need("v", mm, drb)
            d_lg = p.mark(v.tensor_tensor(lg[:, 0:8], psC[:, 0:8], rbc[:, 0:8],
                                          ALU.add), "v")
            ev_prev = d_lg
            d_m = p.mark(v.reduce_max(lg[:, 8:9], lg[:, 0:8], axis=AX.X,
                                      negate=True), "v")
            p.need("a", d_m)
            d_e = p.mark(a.activation(lg[:, 0:8], lg[:, 0:8], AF.Exp,
                                      bias=lg[:, 8:9], accum_out=lg[:, 9:10]), "a")
            p.need("v", d_e)
            v.reciprocal(lg[:, 10:11], lg[:, 9:10])
            v.tensor_scalar_mul(lg[:, 0:8], lg[:, 0:8], lg[:, 10:11])
            v.reduce_max(lg[:, 8:9], lg[:, 0:8], axis=AX.X)
            v.tensor_scalar(cmb[:, q, :], lg[:, 0:8], lg[:, 8:9], None, ALU.is_ge)
            v.scalar_tensor_tensor(lg[:, 0:8], cmb[:, q, :], -1.0e30, lg[:, 0:8],
                                   ALU.mult, ALU.add)
            v.reduce_max(lg[:, 9:10], lg[:, 0:8], axis=AX.X)
            v.tensor_scalar(lg[:, 0:8], lg[:, 0:8], lg[:, 9:10], None, ALU.is_ge)
            v.tensor_scalar_mul(lg[:, 0:8], lg[:, 0:8], lg[:, 9:10])
            v.scalar_tensor_tensor(cmb[:, q, :], cmb[:, q, :], lg[:, 8:9],
                                   lg[:, 0:8], ALU.mult, ALU.add)
            v.tensor_tensor(lg[:, 11:12], lg[:, 8:9], lg[:, 9:10], ALU.add)
            v.tensor_scalar_add(lg[:, 11:12], lg[:, 11:12], 1.0e-8)
            v.reciprocal(lg[:, 12:13], lg[:, 11:12])
            d_w = p.mark(v.tensor_scalar_mul(cmb[:, q, :], cmb[:, q, :],
                                             lg[:, 12:13]), "v")
            p.need("v", d_w)
            d_cb = p.mark(v.tensor_copy(cmb_b[:, :], cmb[:, q, :]), "v")
            p.need("pe", d_cb, cT_prev)
            mmt = p.mark(pe.transpose(pools["psT"][0][0:8, 0:128], cmb_b[:, :],
                                      ctx.cst["ident_bf"][:, :]), "pe")
            cT_prev = evac(p, "a", cmbT[0:8, qs], pools["psT"][0][0:8, 0:128],
                           deps=(mmt, pools["psT_free"][0]))
            pools["psT_free"][0] = cT_prev
        p.bar()
        subR.close()
        subH = contextlib.ExitStack()
        hbuf = p.sb("hbuf", (128, 16, T), BF16, stack=subH)
        ob2 = [p.sb("ob2", (128, D), F32, stack=subH) for _ in range(2)]
        b1f = p.sb("b1f", (128, 16), F32, stack=subH)
        ob2_free = [None, None]
        w1_ap = ctx.W("L1_ffn_w1").rearrange("e (k p) m -> e p k m", p=128)
        w2_ap = ctx.W("L1_ffn_w2").rearrange("e (k p) m -> e p k m", p=128)
        b1_ap = ctx.W("L1_ffn_b1_f")
        sw_state = {}
        for e in range(8):
            subW = contextlib.ExitStack()
            w1 = p.sb("w1", (128, KD, 2048), BF16, stack=subW)
            dw1 = p.dma(("g", "w1"), w1[:, :, :], w1_ap[e])
            db1 = p.dma(("s", "r0"), b1f[:, :], b1_ap[e], deps=(p.tip("a"),))
            proj_fm(ctx, hbuf, w1, 0, mlaf, (dw1, db1, p.tip("v")), pools["psA"],
                    pools["psA_free"], bias_fm=b1f, act=AF.Gelu)
            p.bar()
            subW.close()
            subW2 = contextlib.ExitStack()
            w2 = p.sb("w2", (128, 16, D), BF16, stack=subW2)
            dw2 = p.dma(("g", "w1"), w2[:, :, :], w2_ap[e])
            psB_free = None
            for q in range(NT):
                c = q % 2
                qs = slice(q * 128, (q + 1) * 128)
                mmd = None
                for h0 in range(2):
                    pairs = [(hbuf[:, k2, qs], w2[:, k2, h0 * 512:(h0 + 1) * 512])
                             for k2 in range(16)]
                    mmd = acc_mm(p, psC[:, h0 * 512:(h0 + 1) * 512], pairs,
                                 deps=(dw2, psB_free, p.tip("a")))
                p.need("v", mmd, ob2_free[c])
                d_ob = p.mark(v.tensor_scalar_mul(ob2[c][:, :], psC[:, :],
                                                  cmb[:, q, e:e + 1]), "v")
                psB_free = d_ob
                if e == 0:
                    ob2_free[c] = p.dma(("s", f"o{c}"), accd[qs, :], ob2[c][:, :],
                                        deps=(d_ob,))
                else:
                    ob2_free[c] = p.dma(("g", f"x{c}"), accd[qs, :], ob2[c][:, :],
                                        deps=(d_ob,), accum_op=ALU.add)
            p.bar()
            subW2.close()
        p.bar()
        subH.close()
        b2w = p.sb("b2w", (8, D), BF16)
        db2 = p.dma(("g", "w1"), b2w[:, :], ctx.W("L1_ffn_b2")[:, :])
        xt = [p.sb("xt", (128, D), F32) for _ in range(2)]
        at = [p.sb("at", (128, D), F32) for _ in range(2)]
        fr = [None, None]
        psF = None
        for q in range(NT):
            c = q % 2
            qs = slice(q * 128, (q + 1) * 128)
            d1 = p.dma(("s", f"a{c}"), xt[c][:, :], xn_f[qs, :], deps=(fr[c],))
            d2 = p.dma(("s", f"b{c}"), at[c][:, :], accd[qs, :], deps=(fr[c],))
            mmb = None
            for h0 in range(2):
                mmb = acc_mm(p, psC[:, h0 * 512:(h0 + 1) * 512],
                             [(cmbT[0:8, qs], b2w[0:8, h0 * 512:(h0 + 1) * 512])],
                             deps=(db2, psF, p.tip("a")))
            p.need("v", mmb, d1, d2)
            d3 = p.mark(v.tensor_tensor(at[c][:, :], at[c][:, :], psC[:, :],
                                        ALU.add), "v")
            psF = d3
            d4 = p.mark(v.scalar_tensor_tensor(xt[c][:, :], at[c][:, :], 0.3,
                                               xt[c][:, :], ALU.mult, ALU.add), "v")
            fr[c] = p.dma(("s", f"o{c}"), brB[qs, :], xt[c][:, :], deps=(d4,))


F_STAGES[1] = s_ffn1

def rope_half_tm(ctx, orr, xr, cosb, sinb, scrA, scrB, H, deps=()):
    p = ctx.p
    v = p.eng["v"]
    x1 = xr[:, :, 0:16]
    x2 = xr[:, :, 16:32]
    c1 = cosb[:, 0:16].unsqueeze(1).broadcast_to((128, H, 16))
    c2 = cosb[:, 16:32].unsqueeze(1).broadcast_to((128, H, 16))
    s1 = sinb[:, 0:16].unsqueeze(1).broadcast_to((128, H, 16))
    s2 = sinb[:, 16:32].unsqueeze(1).broadcast_to((128, H, 16))
    sA = scrA[:, 0:H * 16].rearrange("p (h f) -> p h f", h=H)
    sB = scrB[:, 0:H * 16].rearrange("p (h f) -> p h f", h=H)
    p.need("v", *deps)
    v.tensor_tensor(sA, x1, c1, ALU.mult)
    v.tensor_tensor(sB, x2, s1, ALU.mult)
    d1 = p.mark(v.tensor_tensor(orr[:, :, 0:16], sA, sB, ALU.subtract), "v")
    v.tensor_tensor(sA, x2, c2, ALU.mult)
    v.tensor_tensor(sB, x1, s2, ALU.mult)
    d2 = p.mark(v.tensor_tensor(orr[:, :, 16:32], sA, sB, ALU.add), "v")
    return d2


def s_ffn2(ctx: Ctx, l: int):
    p = ctx.p
    v, a, pe = p.eng["v"], p.eng["a"], p.eng["pe"]
    xn_T = ctx.buf("xn2_T", (D, T), BF16)
    xn_f = ctx.buf("xn2_f", (T, D), F32)
    brB = ctx.buf("brB2", (T, D), F32, write=True)
    x1d = ctx.buf("l2f_x1", (T, D), F32, write=True)
    qcTd = ctx.buf("l2f_qcT", (D, T), BF16, write=True)
    with p.stage("ffn2"):
        pools = mha_pools(p)
        psC = p.ps("psC", (128, 1024), F32)
        xl2f = p.sb("xl2f", (128, KD, T), BF16)
        cmb = p.sb("cmb", (128, NT, 7), F32)
        subQ = contextlib.ExitStack()
        kf = p.sb("kf", (128, KD, T), BF16, stack=subQ)
        vb = p.sb("vb", (128, NT, D), BF16, stack=subQ)
        subX = contextlib.ExitStack()
        xl1f = p.sb("xl1f", (128, KD, T), BF16, stack=subX)
        kvf = p.sb("kvf", (128, 2, T), BF16, stack=subX)
        qlf = p.sb("qlf", (128, 2, T), BF16, stack=subX)
        subX2 = contextlib.ExitStack()
        ln1bc, dln1 = bcast_row(ctx, "L2_ffn_ln1_w", psC, stack=subX2)
        rope_c = p.sb("rope_c", (128, NT, 32), F32, stack=subX2)
        rope_s = p.sb("rope_s", (128, NT, 32), F32, stack=subX2)
        drc = p.dma(("s", "r0"), rope_c[:, :, :],
                    ctx.W("cst_rope2_cos").rearrange("(q p) f -> p q f", p=128))
        drs = p.dma(("s", "r1"), rope_s[:, :, :],
                    ctx.W("cst_rope2_sin").rearrange("(q p) f -> p q f", p=128))
        xt = [p.sb("xt", (128, D), F32, stack=subX2) for _ in range(2)]
        xb_ = p.sb("xb_", (128, D), BF16, stack=subX2)
        st4 = p.sb("st4", (128, 4), F32, stack=subX2)
        sqs = p.sb("sqs", (128, D), F32, stack=subX2)
        scrA = p.sb("scrA", (128, 512), F32, stack=subX2)
        scrB = p.sb("scrB", (128, 512), F32, stack=subX2)
        qtm = p.sb("qtm", (128, 512), F32, stack=subX2)
        qbt = p.sb("qbt", (128, D), BF16, stack=subX2)
        tbq = [p.sb("tbq", (128, KD, 128), BF16, stack=subX2) for _ in range(2)]
        xt_free = [None, None]
        cast_prev = None
        stat_prev = None
        for q in range(NT):
            c = q % 2
            qs = slice(q * 128, (q + 1) * 128)
            dxl = p.dma(("s", f"a{c}"), xt[c][:, :], xn_f[qs, :], deps=(xt_free[c],))
            d_st = ln_stats(ctx, xt[c][:, :], st4[:, 0:2], st4[:, 2:4], sqs, 1e-5,
                            deps=(dxl, stat_prev))
            p.need("v", *d_st, dln1, cast_prev)
            d_n = p.mark(v.tensor_scalar(xt[c][:, :], xt[c][:, :], st4[:, 0:1],
                                         st4[:, 2:3], ALU.subtract, ALU.mult), "v")
            stat_prev = d_n
            d_n2 = p.mark(v.tensor_tensor(xt[c][:, :], xt[c][:, :], ln1bc[:, :],
                                          ALU.mult), "v")
            p.need("a", d_n2, cast_prev)
            d_cb = p.mark(a.copy(xb_[:, :], xt[c][:, :]), "a")
            xt_free[c] = d_cb
            for k in range(KD):
                tc = k % 2
                p.need("pe", d_cb, pools["psT_free"][tc])
                mmt = p.mark(pe.transpose(pools["psT"][tc][:, 0:128],
                                          xb_[:, k * 128:(k + 1) * 128],
                                          ctx.cst["ident_bf"][:, :]), "pe")
                if k == KD - 1:
                    cast_prev = mmt
                ev = evac(p, "v" if tc == 0 else "a", xl1f[:, k, qs],
                          pools["psT"][tc][:, 0:128], deps=(mmt,))
                pools["psT_free"][tc] = ev
        if DBG.get("f2", 9) < 1:
            p.bar(); subX2.close(); subX.close(); subQ.close(); return
        dx1f = (p.tip("v"), p.tip("a"))
        subW = contextlib.ExitStack()
        wkvd, dwkvd = load_w(ctx, "L2_ffn_kv_d", 256, stream=("g", "w0"), stack=subW)
        wqd, dwqd = load_w(ctx, "L2_ffn_q_d", 256, stream=("g", "w1"), stack=subW)
        d_kv = proj_fm(ctx, kvf, wkvd, 0, xl1f, (dwkvd,) + dx1f, pools["psA"],
                       pools["psA_free"])
        d_ql = proj_fm(ctx, qlf, wqd, 0, xl1f, (dwqd,) + dx1f, pools["psA"],
                       pools["psA_free"])
        p.bar()
        subW.close()
        subW = contextlib.ExitStack()
        wvu, dwvu = load_w(ctx, "L2_ffn_v_u", D, stream=("g", "w1"), stack=subW)
        proj_tm(ctx, vb, wvu, 0, D, kvf, (dwvu,) + d_kv, pools["psA"],
                pools["psA_free"])
        p.bar()
        subW.close()
        if DBG.get("f2", 9) < 2:
            p.bar(); subX2.close(); subX.close(); subQ.close(); return
        qcT3 = qcTd.rearrange("(k p) t -> p k t", p=128)
        for pass_i in range(2):
            subW = contextlib.ExitStack()
            if pass_i == 0:
                wnr, dwnr = load_w(ctx, "L2_ffn_q_u", D, stream=("g", "w0"),
                                   stack=subW)
                wro, dwro = load_w(ctx, "L2_ffn_rope_q_w", 512, stream=("g", "w1"),
                                   stack=subW)
                srcn, nkn, dn_ = qlf, 2, d_ql
                srcr, nkr, dr_ = qlf, 2, d_ql
            else:
                wnr, dwnr = load_w(ctx, "L2_ffn_k_u", D, stream=("g", "w0"),
                                   stack=subW)
                wro, dwro = load_w(ctx, "L2_ffn_rope_k_w", 512, stream=("g", "w1"),
                                   stack=subW)
                srcn, nkn, dn_ = kvf, 2, d_kv
                srcr, nkr, dr_ = xl1f, KD, dx1f
            rp_prev = None
            q_prev = None
            nr_prev = None
            tb_free = [None, None]
            for q in range(NT):
                c = q % 2
                qs = slice(q * 128, (q + 1) * 128)
                mmn = None
                for h0 in range(2):
                    pairs = [(srcn[:, k, qs], wnr[:, k, h0 * 512:(h0 + 1) * 512])
                             for k in range(nkn)]
                    mmn = acc_mm(p, psC[:, h0 * 512:(h0 + 1) * 512], pairs,
                                 deps=(dwnr, nr_prev) + tuple(dn_))
                pairs = [(srcr[:, k, qs], wro[:, k, 0:512]) for k in range(nkr)]
                mmr = acc_mm(p, pools["psA"][0][:, :], pairs,
                             deps=(dwro, q_prev) + tuple(dr_) +
                                  tuple(pools["psA_free"][0]))
                p.need("v", mmr, drc, drs)
                d_q = p.mark(v.tensor_copy(qtm[:, :], pools["psA"][0][:, :]), "v")
                q_prev = d_q
                pools["psA_free"][0] = (d_q,)
                qbt3 = qbt.rearrange("p (h f) -> p h f", h=16)
                d_rp = rope_half_tm(ctx, qbt3[:, :, 0:32],
                                    qtm.rearrange("p (h f) -> p h f", h=16),
                                    rope_c[:, q, :], rope_s[:, q, :],
                                    scrA, scrB, 16, deps=(d_q, rp_prev))
                psC3 = psC.rearrange("p (h f) -> p h f", h=16)
                p.need("v", mmn, rp_prev)
                d_nr = p.mark(v.tensor_copy(qbt3[:, :, 32:64], psC3[:, :, 32:64]),
                              "v")
                nr_prev = d_nr
                last_cp = None
                for k in range(KD):
                    tc = k % 2
                    p.need("pe", d_rp, d_nr, pools["psT_free"][tc])
                    mmt = p.mark(pe.transpose(pools["psT"][tc][:, 0:128],
                                              qbt[:, k * 128:(k + 1) * 128],
                                              ctx.cst["ident_bf"][:, :]), "pe")
                    if k == KD - 1:
                        rp_prev = mmt
                    if pass_i == 0:
                        ev = evac(p, "a", tbq[c][:, k, :],
                                  pools["psT"][tc][:, 0:128],
                                  deps=(mmt, tb_free[c]))
                        last_cp = ev
                    else:
                        ev = evac(p, "v" if tc == 0 else "a", kf[:, k, qs],
                                  pools["psT"][tc][:, 0:128], deps=(mmt,))
                    pools["psT_free"][tc] = ev
                if pass_i == 0:
                    tb_free[c] = p.dma(("s", f"tc{c}"), qcT3[:, :, qs],
                                       tbq[c][:, :, :], deps=(last_cp,))
            p.bar()
            subW.close()
        p.bar()
        subX2.close()
        p.bar()
        subX.close()
        if DBG.get("f2", 9) < 3:
            p.bar(); subQ.close(); return
        subB = contextlib.ExitStack()
        wo, dwo = load_w(ctx, "L2_ffn_o_w", D, stream=("g", "w0"), stack=subB)
        ln2bc, dln2 = bcast_row(ctx, "L2_ffn_ln2_w", psC, stack=subB)
        ob = [p.sb("ob", (128, D), F32, stack=subB) for _ in range(2)]
        xnt = [p.sb("xnt", (128, D), F32, stack=subB) for _ in range(2)]
        otb = p.sb("otb", (128, D), BF16, stack=subB)
        x1b = p.sb("x1b", (128, D), BF16, stack=subB)
        lnfm = p.sb("lnfm", (128, KD, 128), BF16, stack=subB)
        st4b = p.sb("st4b", (128, 4), F32, stack=subB)
        sqb = p.sb("sqb", (128, D), F32, stack=subB)
        state = {"psC_free": dln2, "fm_mm_prev": None, "cp_prev": None,
                 "ob_free": [None, None]}
        xnt_free = [None, None]
        x1t_prev = [None]

        def x1_hook(q, c, d_ob):
            s = slice(q * 128, (q + 1) * 128)
            dxl = p.dma(("s", f"b{c}"), xnt[c][:, :], xn_f[s, :],
                        deps=(xnt_free[c],))
            p.need("v", d_ob, dxl)
            d1 = p.mark(v.tensor_tensor(ob[c][:, :], ob[c][:, :], xnt[c][:, :],
                                        ALU.add), "v")
            xnt_free[c] = d1
            d_st = ln_stats(ctx, ob[c][:, :], st4b[:, 0:2], st4b[:, 2:4], sqb,
                            1e-5, deps=(d1, x1t_prev[0], dln2))
            p.need("v", *d_st)
            d_n = p.mark(v.tensor_scalar(sqb[:, :], ob[c][:, :], st4b[:, 0:1],
                                         st4b[:, 2:3], ALU.subtract, ALU.mult), "v")
            x1t_prev[0] = d_n
            d_n2 = p.mark(v.tensor_tensor(sqb[:, :], sqb[:, :], ln2bc[:, :],
                                          ALU.mult), "v")
            p.need("a", d_n2)
            d_cb = p.mark(a.copy(x1b[:, :], sqb[:, :]), "a")
            for k in range(KD):
                tc = k % 2
                p.need("pe", d_cb, pools["psT_free"][tc])
                mmt = p.mark(pe.transpose(pools["psT"][tc][:, 0:128],
                                          x1b[:, k * 128:(k + 1) * 128],
                                          ctx.cst["ident_bf"][:, :]), "pe")
                ev = evac(p, "v" if tc == 0 else "a", xl2f[:, k, s],
                          pools["psT"][tc][:, 0:128], deps=(mmt,))
                pools["psT_free"][tc] = ev
            return (d1, d_cb, p.tip("v"))

        def emit_out(q, c, ot, d_o):
            return tm_to_fm_out(ctx, pools, wo, dwo, None, dln2, psC, otb, lnfm,
                                ob, x1d, state, q, c, ot, d_o,
                                extra_add=x1_hook)

        subM = contextlib.ExitStack()
        mha_core(ctx, None, kf, vb, pools, H=16, dk=64, dv=64, causal=True,
                 scale=0.125, in_deps=(p.tip("v"), p.tip("a")),
                 emit_out=emit_out, q_dram=qcTd, arena=subM)
        p.bar()
        subM.close()
        subB.close()
        subQ.close()
        if DBG.get("f2", 9) < 4:
            return
        subR = contextlib.ExitStack()
        rw = p.sb("rw", (128, KD, 7), BF16, stack=subR)
        drw = p.dma(("g", "w0"), rw[:, :, :],
                    ctx.W("L2_ffn_router_w").rearrange("(k p) m -> p k m", p=128))
        rbc, drb = bcast_row(ctx, "L2_ffn_routing_bias", psC, stack=subR)
        lg = p.sb("lg", (128, 16), F32, stack=subR)
        ev_prev = drb
        for q in range(NT):
            qs = slice(q * 128, (q + 1) * 128)
            pairs = [(xl2f[:, k, qs], rw[:, k, :]) for k in range(KD)]
            mm = acc_mm(p, psC[:, 0:7], pairs, deps=(drw, ev_prev, p.tip("a")))
            p.need("v", mm, drb)
            d_lg = p.mark(v.tensor_tensor(lg[:, 0:7], psC[:, 0:7], rbc[:, 0:7],
                                          ALU.add), "v")
            ev_prev = d_lg
            p.need("a", d_lg)
            d_sg = p.mark(a.activation(lg[:, 0:7], lg[:, 0:7], AF.Sigmoid), "a")
            p.need("v", d_sg)
            v.reduce_max(lg[:, 8:9], lg[:, 0:7], axis=AX.X)
            v.tensor_scalar(cmb[:, q, :], lg[:, 0:7], lg[:, 8:9], None, ALU.is_ge)
            v.scalar_tensor_tensor(lg[:, 0:7], cmb[:, q, :], -1.0e30, lg[:, 0:7],
                                   ALU.mult, ALU.add)
            v.reduce_max(lg[:, 9:10], lg[:, 0:7], axis=AX.X)
            v.tensor_scalar(lg[:, 0:7], lg[:, 0:7], lg[:, 9:10], None, ALU.is_ge)
            v.tensor_scalar_mul(lg[:, 0:7], lg[:, 0:7], lg[:, 9:10])
            v.scalar_tensor_tensor(cmb[:, q, :], cmb[:, q, :], lg[:, 8:9],
                                   lg[:, 0:7], ALU.mult, ALU.add)
            v.tensor_tensor(lg[:, 10:11], lg[:, 8:9], lg[:, 9:10], ALU.add)
            v.reciprocal(lg[:, 11:12], lg[:, 10:11])
            p.mark(v.tensor_scalar_mul(cmb[:, q, :], cmb[:, q, :],
                                       lg[:, 11:12]), "v")
        p.bar()
        subR.close()
        if DBG.get("f2", 9) < 5:
            return
        sw_state = {}
        ob_free = [None, None]
        rg_ap = ctx.W("L2_ffn_routed_gate").rearrange("e (k p) m -> e p k m", p=128)
        ru_ap = ctx.W("L2_ffn_routed_up").rearrange("e (k p) m -> e p k m", p=128)
        rd_ap = ctx.W("L2_ffn_routed_down").rearrange("e (k p) m -> e p k m", p=128)
        for e in range(-1, 7):
            subH = contextlib.ExitStack()
            hbuf = p.sb("hbuf", (128, 16, T), BF16, stack=subH)
            obx = [p.sb("obx", (128, D), F32, stack=subH) for _ in range(2)]
            x1t = [p.sb("x1t", (128, D), F32, stack=subH) for _ in range(2)]
            subW = contextlib.ExitStack()
            if e < 0:
                wg, dwg = load_w(ctx, "L2_ffn_shared_gate", 2048,
                                 stream=("g", "w1"), stack=subW)
                wu, dwu = load_w(ctx, "L2_ffn_shared_up", 2048,
                                 stream=("g", "w2"), stack=subW)
            else:
                wg = p.sb("wg", (128, KD, 2048), BF16, stack=subW)
                wu = p.sb("wu", (128, KD, 2048), BF16, stack=subW)
                dwg = p.dma(("g", "w1"), wg[:, :, :], rg_ap[e])
                dwu = p.dma(("g", "w2"), wu[:, :, :], ru_ap[e])
            swiglu_fm(ctx, hbuf, wg, wu, xl2f, (dwg, dwu, p.tip("v"), p.tip("a")),
                      pools["psA"], pools["psA_free"], sw_state)
            p.bar()
            subW.close()
            subW2 = contextlib.ExitStack()
            if e < 0:
                wd, dwd = load_w(ctx, "L2_ffn_shared_down", D,
                                 stream=("g", "w1"), stack=subW2)
            else:
                wd = p.sb("wd", (128, 16, D), BF16, stack=subW2)
                dwd = p.dma(("g", "w1"), wd[:, :, :], rd_ap[e])
            psB_free = None
            for q in range(NT):
                c = q % 2
                qs = slice(q * 128, (q + 1) * 128)
                mmd = None
                for h0 in range(2):
                    pairs = [(hbuf[:, k2, qs], wd[:, k2, h0 * 512:(h0 + 1) * 512])
                             for k2 in range(16)]
                    mmd = acc_mm(p, psC[:, h0 * 512:(h0 + 1) * 512], pairs,
                                 deps=(dwd, psB_free, p.tip("v"), p.tip("a")))
                if e < 0:
                    dxt = p.dma(("s", f"a{c}"), x1t[c][:, :], x1d[qs, :],
                                deps=(ob_free[c],))
                    p.need("v", mmd, dxt, ob_free[c])
                    d_ob = p.mark(v.tensor_tensor(obx[c][:, :], psC[:, :],
                                                  x1t[c][:, :], ALU.add), "v")
                    psB_free = d_ob
                    ob_free[c] = p.dma(("s", f"o{c}"), brB[qs, :], obx[c][:, :],
                                       deps=(d_ob,))
                else:
                    p.need("v", mmd, ob_free[c])
                    d_ob = p.mark(v.tensor_scalar_mul(obx[c][:, :], psC[:, :],
                                                      cmb[:, q, e:e + 1]), "v")
                    psB_free = d_ob
                    ob_free[c] = p.dma(("g", f"x{c}"), brB[qs, :], obx[c][:, :],
                                       deps=(d_ob,), accum_op=ALU.add)
            p.bar()
            subW2.close()
            subH.close()


F_STAGES[2] = s_ffn2






def s_ffn3(ctx: Ctx, l: int):
    p = ctx.p
    v, a = p.eng["v"], p.eng["a"]
    xn_T = ctx.buf("xn3_T", (D, T), BF16)
    xn_lo = ctx.buf("xn3_loT", (D, T), BF16)
    xn_b = ctx.buf("xn3_b", (T, D), BF16)
    brB = ctx.buf("brB3", (T, D), F32, write=True)
    with p.stage("ffn3"):
        psA = [p.ps("psA", (128, 512), F32) for _ in range(2)]
        psB = p.ps("psB", (128, 1024), F32)
        psT = p.ps("psT", (128, 512), BF16)
        xnf = p.sb("xnf", (128, KD, T), BF16)
        dxn = p.dma(("s", "ax"), xnf[:, :, :],
                    xn_T.rearrange("(k p) t -> p k t", p=128))
        xsf = p.sb("xsf", (128, KD, T), BF16)
        cmb = p.sb("cmb", (128, NT, 8), F32)
        sc = p.sb("sc", (128, NT), F32)
        lg = p.sb("lg", (128, 8), F32)
        ob = [p.sb("ob", (128, D), F32) for _ in range(2)]
        subR = contextlib.ExitStack()
        xlof = p.sb("xlof", (128, KD, T), BF16, stack=subR)
        dlo = p.dma(("s", "ay"), xlof[:, :, :],
                    xn_lo.rearrange("(k p) t -> p k t", p=128))
        rw = p.sb("rw", (128, KD, 8), F32, stack=subR)
        rwh = p.sb("rwh", (128, KD, 8), BF16, stack=subR)
        rwl = p.sb("rwl", (128, KD, 8), BF16, stack=subR)
        drw = p.dma(("s", "w0"), rw[:, :, :],
                    ctx.W("L3_ffn_router_w").rearrange("(k p) m -> p k m", p=128))
        p.need("a", drw)
        d_h = p.mark(a.copy(rwh[:, :, :], rw[:, :, :]), "a")
        p.need("v", drw, d_h)
        d_l = p.mark(v.tensor_tensor(rwl[:, :, :], rw[:, :, :], rwh[:, :, :],
                                     ALU.subtract), "v")
        ev_prev = None
        for q in range(NT):
            qs = slice(q * 128, (q + 1) * 128)
            pairs = []
            for k in range(KD):
                pairs.append((xnf[:, k, qs], rwh[:, k, :]))
            for k in range(KD):
                pairs.append((xlof[:, k, qs], rwh[:, k, :]))
            for k in range(KD):
                pairs.append((xnf[:, k, qs], rwl[:, k, :]))
            mm = acc_mm(p, psB[:, 0:8], pairs, deps=(dxn, dlo, d_h, d_l, ev_prev))
            p.need("v", mm)
            d_lg = p.mark(v.tensor_copy(lg[:, :], psB[:, 0:8]), "v")
            ev_prev = d_lg
            d_rm = p.mark(v.reduce_max(sc[:, q:q + 1], lg[:, :], axis=AX.X), "v")
            p.need("v", d_rm)
            d_msk = p.mark(v.tensor_scalar(cmb[:, q, :], lg[:, :], sc[:, q:q + 1],
                                           None, ALU.is_ge), "v")
            p.need("a", d_msk)
            p.mark(a.activation(sc[:, q:q + 1], sc[:, q:q + 1], AF.Sigmoid), "a")
        p.bar()
        subR.close()
        subX = contextlib.ExitStack()
        xb = [p.sb("xb", (128, D), BF16, stack=subX) for _ in range(2)]
        p.need("v", p.tip("a"))
        xb_free = [None, None]
        cp_prev = None
        for q in range(NT):
            c = q % 2
            qs = slice(q * 128, (q + 1) * 128)
            dxb = p.dma(("s", f"a{c}"), xb[c][:, :], xn_b[qs, :], deps=(xb_free[c],))
            p.need("v", dxb)
            d_xs = p.mark(v.tensor_scalar_mul(xb[c][:, :], xb[c][:, :],
                                              sc[:, q:q + 1]), "v")
            xb_free[c] = d_xs
            for k in range(KD):
                p.need("pe", d_xs, cp_prev)
                mmt = p.mark(p.eng["pe"].transpose(
                    psT[:, 0:128], xb[c][:, k * 128:(k + 1) * 128],
                    ctx.cst["ident_bf"][:, :]), "pe")
                if k == KD - 1:
                    xb_free[c] = mmt
                cp_prev = evac(p, "a", xsf[:, k, qs], psT[:, 0:128], deps=(mmt,))
        p.bar()
        subX.close()
        sw_state = {}
        ob_free = [None, None]
        for e in range(-1, 8):
            subH = contextlib.ExitStack()
            hbuf = p.sb("hbuf", (128, 16, T), BF16, stack=subH)
            subW = contextlib.ExitStack()
            if e < 0:
                wg, dwg = load_w(ctx, "L3_ffn_shared_gate", 2048,
                                 stream=("g", "w1"), stack=subW)
                wu, dwu = load_w(ctx, "L3_ffn_shared_up", 2048,
                                 stream=("g", "w2"), stack=subW)
                xin = xnf
                dxi = dxn
            else:
                wga = ctx.W("L3_ffn_expert_gate").rearrange(
                    "e (k p) m -> e p k m", p=128)
                wua = ctx.W("L3_ffn_expert_up").rearrange(
                    "e (k p) m -> e p k m", p=128)
                wg = p.sb("wg", (128, KD, 2048), BF16, stack=subW)
                wu = p.sb("wu", (128, KD, 2048), BF16, stack=subW)
                dwg = p.dma(("g", "w1"), wg[:, :, :], wga[e])
                dwu = p.dma(("g", "w2"), wu[:, :, :], wua[e])
                xin = xsf
                dxi = p.tip("a")
            swiglu_fm(ctx, hbuf, wg, wu, xin, (dwg, dwu, dxi), psA, psA_free=[(), ()],
                      state=sw_state)
            p.bar()
            subW.close()
            subW2 = contextlib.ExitStack()
            if e < 0:
                wd, dwd = load_w(ctx, "L3_ffn_shared_down", D,
                                 stream=("g", "w1"), stack=subW2)
            else:
                wda = ctx.W("L3_ffn_expert_down").rearrange(
                    "e (k p) m -> e p k m", p=128)
                wd = p.sb("wd", (128, 16, D), BF16, stack=subW2)
                dwd = p.dma(("g", "w1"), wd[:, :, :], wda[e])
            psB_free = None
            for q in range(NT):
                c = q % 2
                qs = slice(q * 128, (q + 1) * 128)
                mmd = None
                for h0 in range(2):
                    pairs = [(hbuf[:, k2, qs], wd[:, k2, h0 * 512:(h0 + 1) * 512])
                             for k2 in range(16)]
                    mmd = acc_mm(p, psB[:, h0 * 512:(h0 + 1) * 512], pairs,
                                 deps=(dwd, psB_free, p.tip("v")))
                p.need("v", mmd, ob_free[c])
                if e < 0:
                    d_ob = p.mark(v.tensor_copy(ob[c][:, :], psB[:, :]), "v")
                else:
                    d_ob = p.mark(v.tensor_scalar_mul(ob[c][:, :], psB[:, :],
                                                      cmb[:, q, e:e + 1]), "v")
                psB_free = d_ob
                if e < 0:
                    ob_free[c] = p.dma(("s", f"o{c}"), brB[qs, :], ob[c][:, :],
                                       deps=(d_ob,))
                else:
                    ob_free[c] = p.dma(("g", f"x{c}"), brB[qs, :], ob[c][:, :],
                                       deps=(d_ob,), accum_op=ALU.add)
            p.bar()
            subW2.close()
            subH.close()


F_STAGES[3] = s_ffn3



STAGES = {}


def register_stages():
    global STAGES
    STAGES = {}
    for l in range(4):
        STAGES[f"ln{l}"] = (s_ln, l)
        STAGES[f"res{l}"] = (s_res, l)
    for l, fn in A_STAGES.items():
        STAGES[f"attn{l}"] = (fn, l)
    for l, fn in F_STAGES.items():
        STAGES[f"ffn{l}"] = (fn, l)


def build_group(stage_names, host_arrays, fused=False):
    p = Prog()
    ctx = Ctx(p, host_arrays, fused)
    with p.top:
        load_constants(ctx)
        for sn in stage_names:
            fn, l = STAGES[sn]
            fn(ctx, l)
    return p, ctx


def run_group(stage_names, host_arrays, bufs, fused=False, core_ids=None, trace=False):
    core_ids = core_ids or list(range(N_CORES))
    p, ctx = build_group(stage_names, host_arrays, fused=fused)
    in_maps = []
    for i in range(len(core_ids)):
        m = {}
        for name, kind in ctx.kind.items():
            if kind == "ExternalInput":
                if name in host_arrays:
                    m[name] = host_arrays[name]
                else:
                    m[name] = bufs[name][i]
        in_maps.append(m)
    res = run_bass_kernel_spmd(p.nc, in_maps, core_ids=core_ids, trace=trace)
    for name, kind in ctx.kind.items():
        if kind == "ExternalOutput":
            sh, dt = ctx.buf_shape[name]
            bufs[name] = [np.asarray(res.results[i][name]) for i in range(len(core_ids))]
    return res



GROUPS_UNFUSED = [
    ["ln0", "attn0", "ffn0", "res0"],
    ["ln1", "attn1", "ffn1", "res1"],
    ["ln2", "attn2", "ffn2", "res2"],
    ["ln3", "attn3", "ffn3", "res3"],
]

ALL_STAGES = [s for g in GROUPS_UNFUSED for s in g]

FUSED = True


def kernel(**inputs):
    host = prep_weights(inputs)
    host.update(host_constants())
    register_stages()
    x = np.ascontiguousarray(np.asarray(inputs["L0_attn_x"], dtype=np.float32))
    bufs = {"X0": [np.ascontiguousarray(x[i]) for i in range(N_CORES)]}
    if FUSED:
        run_group(ALL_STAGES, host, bufs, fused=True)
    else:
        for g in GROUPS_UNFUSED:
            run_group(g, host, bufs, fused=False)
    out = np.stack(bufs["X4"], axis=0).astype(np.float32)
    return out


# version 07
# speedup vs baseline: 1.2172x; 1.2172x over previous
import contextlib
import numpy as np
import ml_dtypes
import concourse.bass as bass
import concourse.mybir as mybir
from concourse.bass_utils import run_bass_kernel_spmd

F32 = mybir.dt.float32
BF16 = mybir.dt.bfloat16
AF = mybir.ActivationFunctionType
ALU = mybir.AluOpType
AX = mybir.AxisListType

T, D, NT, KD = 2048, 1024, 16, 8
NEG = -1.0e30
N_CORES = 8
DBG = {}



_CHAINED = {
    "tensor_copy", "tensor_tensor", "tensor_add", "tensor_sub", "tensor_mul",
    "tensor_max", "tensor_scalar", "tensor_single_scalar", "tensor_scalar_add",
    "tensor_scalar_sub", "tensor_scalar_mul", "tensor_scalar_max",
    "tensor_scalar_min", "tensor_relu", "tensor_reduce", "reduce_max",
    "reduce_sum", "scalar_tensor_tensor", "tensor_tensor_scan", "memset",
    "memzero", "reciprocal", "select", "copy_predicated", "bn_stats", "bn_aggr",
    "pool", "pool_avg", "pool_max", "stream_shuffle", "match_replace",
    "activation", "sign", "square", "sqrt", "mul", "copy", "add", "iota",
    "tensor_tensor_reduce", "max", "max_index",
}


_HAZARD = {
    "tensor_reduce", "reduce_max", "reduce_sum", "reciprocal",
    "tensor_tensor_scan", "bn_stats", "bn_aggr", "max", "max_index",
    "pool", "pool_avg", "pool_max", "tensor_tensor_reduce",
}


class EngProxy:
    def __init__(self, p, ename, raw):
        self._p = p
        self._ename = ename
        self._raw = raw

    def __getattr__(self, name):
        real = getattr(self._raw, name)
        if name not in _CHAINED:
            return real
        p, en = self._p, self._ename
        hazard = name in _HAZARD
        is_act = name == "activation"

        def wrapped(*a, **k):
            if p._hz.get(en) or hazard or p._strict_chain:
                p.need(en, p.tip(en))
            ins = real(*a, **k)
            p.mark(ins, en)
            small = False
            try:
                small = a[0].free_size() <= 16
            except Exception:
                small = True
            p._hz[en] = (hazard or small or
                         (is_act and k.get("accum_out") is not None))
            return ins
        return wrapped


class Prog:
    ENGS = ("pe", "v", "a", "g", "s")

    def __init__(self):
        self.nc = bass.Bass("TRN2", target_bir_lowering=False, debug=False)
        raw = {"pe": self.nc.tensor, "v": self.nc.vector, "a": self.nc.scalar,
               "g": self.nc.gpsimd, "s": self.nc.sync}
        self.eng = {k: EngProxy(self, k, v) for k, v in raw.items()}
        self.top = contextlib.ExitStack()
        self.sems = {}
        self.cnt = {}
        self.hwm = {}
        self.stage_stack = None
        self._uid = 0
        self._consts = {}
        self._stage_id = 0
        self._marked = {}
        self._keep = []
        self._hz = {}
        self._strict_chain = False

    def constf(self, val, dtype=F32):
        key = (self._stage_id, float(val), dtype)
        if key not in self._consts:
            stack = self.stage_stack if self.stage_stack is not None else self.top
            tsb = stack.enter_context(
                self.nc.sbuf_tensor(f"cst{self.uid()}", [128, 1], dtype))
            d = self.mark(self.eng["v"].memset(tsb[:, :], float(val)), "v")
            for en in self.ENGS:
                self.need(en, d)
            self._consts[key] = tsb
        return self._consts[key][:, 0:1]

    def uid(self):
        self._uid += 1
        return self._uid

    def sem(self, name):
        if name not in self.sems:
            self.sems[name] = self.top.enter_context(self.nc.semaphore(name))
            self.cnt[name] = 0
        return self.sems[name]

    def mark(self, ins, ename, n=1):
        key = id(ins)
        if key in self._marked:
            return self._marked[key]
        sname = "m_" + ename
        ins.then_inc(self.sem(sname), n)
        self.cnt[sname] += n
        dep = (sname, self.cnt[sname])
        self._marked[key] = dep
        self._keep.append(ins)
        return dep

    def tip(self, ename):
        sname = "m_" + ename
        self.sem(sname)
        return (sname, self.cnt[sname])

    def need(self, ename, *deps):
        stack = list(deps)
        while stack:
            dep = stack.pop()
            if dep is None or len(dep) == 0:
                continue
            if not isinstance(dep[0], str):
                stack.extend(dep)
                continue
            sname, v = dep
            if v <= 0:
                continue
            key = (ename, sname)
            if self.hwm.get(key, 0) < v:
                self.eng[ename].wait_ge(self.sems[sname], v)
                self.hwm[key] = v

    def dma(self, stream, out, in_, deps=(), **kw):
        ename, suffix = stream
        sname = "d_" + suffix
        self.sem(sname)
        if self.cnt[sname] > 0:
            self.need(ename, (sname, self.cnt[sname]))
        self.need(ename, *deps)
        ins = self.eng[ename].dma_start(out, in_, **kw)
        ins.then_inc(self.sems[sname], 16)
        self.cnt[sname] += 16
        return (sname, self.cnt[sname])

    def bar(self):
        s = self.sem("bar")
        for en in self.ENGS:
            self.need(en, self.tip(en))
            self.eng[en].sem_inc(s, 1)
        self.cnt["bar"] += len(self.ENGS)
        tgt = self.cnt["bar"]
        dma_deps = [(sn, v) for sn, v in self.cnt.items() if sn.startswith("d_") and v > 0]
        for en in self.ENGS:
            self.need(en, ("bar", tgt), *dma_deps)

    @contextlib.contextmanager
    def stage(self, name):
        st = contextlib.ExitStack()
        prev = self.stage_stack
        self.stage_stack = st
        self._stage_id += 1
        try:
            yield st
        finally:
            self.bar()
            st.close()
            self.stage_stack = prev
            self._stage_id += 1

    def sb(self, name, shape, dtype, stack=None, side=None):
        right = stack is not None
        stack = stack or (self.stage_stack if self.stage_stack is not None else self.top)
        return stack.enter_context(
            self.nc.sbuf_tensor(f"{name}_{self.uid()}", list(shape), dtype,
                                side=("right" if right else None) if side is None else side))

    def ps(self, name, shape, dtype=F32):
        stack = self.stage_stack if self.stage_stack is not None else self.top
        esz = 2 if dtype == BF16 else 4
        free = 1
        for s in shape[1:]:
            free *= s
        pad_free = ((free * esz + 2047) // 2048) * 2048 // esz
        if pad_free != free:
            shape = [shape[0], pad_free]
        return stack.enter_context(
            self.nc.psum_tensor(f"{name}_{self.uid()}", list(shape), dtype))



class Ctx:

    def __init__(self, p: Prog, host_arrays: dict, fused: bool):
        self.p = p
        self.host = host_arrays
        self.fused = fused
        self.decl = {}
        self.kind = {}
        self.buf_shape = {}
        self.buf_mark = {}
        self.cst = {}

    def W(self, name):
        if name not in self.decl:
            arr = self.host[name]
            self.decl[name] = self.p.nc.dram_tensor(
                name, list(arr.shape), F32, kind="ExternalInput").ap()
            self.kind[name] = "ExternalInput"
        return self.decl[name]

    def buf(self, name, shape=None, dtype=F32, write=False, final=False, external_in=False):
        if name not in self.decl:
            assert shape is not None, f"first access to {name} needs shape"
            if self.fused:
                kind = ("ExternalOutput" if final else
                        ("ExternalInput" if external_in else "Internal"))
            else:
                kind = "ExternalOutput" if write else "ExternalInput"
            self.decl[name] = self.p.nc.dram_tensor(
                name, list(shape), dtype, kind=kind).ap()
            self.kind[name] = kind
            self.buf_shape[name] = (tuple(shape), dtype)
        return self.decl[name]

    def wrote(self, name, dep):
        self.buf_mark[name] = dep

    def rdep(self, name):
        return self.buf_mark.get(name)



def host_constants():
    c = {}
    ii = np.arange(128)
    c["cst_ident"] = np.eye(128, dtype=np.float32)
    c["cst_causal_add"] = np.where(ii[None, :] <= ii[:, None], 0.0, NEG).astype(np.float32)
    c["cst_lower_ji"] = (ii[None, :] >= ii[:, None]).astype(np.float32)
    c["cst_ones_row"] = np.ones((1, 128), np.float32)
    c["cst_ones_col"] = np.ones((128, 1), np.float32)
    dh = 64
    inv = 1.0 / 10000.0 ** (np.arange(0, dh, 2, dtype=np.float64) / dh)
    ang = np.arange(T, dtype=np.float64)[:, None] * inv[None, :]
    c["cst_rope0_sin"] = np.sin(ang).astype(np.float32)
    c["cst_rope0_cos"] = np.cos(ang).astype(np.float32)
    rd = 32
    inv2 = 1.0 / 10000.0 ** (np.arange(0, rd, 2, dtype=np.float64) / rd)
    f2 = np.arange(T, dtype=np.float64)[:, None] * inv2[None, :]
    emb = np.concatenate([f2, f2], axis=1)
    c["cst_rope2_sin"] = np.sin(emb).astype(np.float32)
    c["cst_rope2_cos"] = np.cos(emb).astype(np.float32)
    c["cst_egroup"] = (np.arange(8)[None, :] // 4).astype(np.float32)
    return c


def load_constants(ctx: Ctx):
    p = ctx.p
    cst = ctx.cst
    cst["ident_bf"] = p.sb("ident_bf", (128, 128), BF16)
    cst["causal_add"] = p.sb("causal_add", (128, 128), F32)
    cst["lower_ji_bf"] = p.sb("lower_ji_bf", (128, 128), BF16)
    cst["lower_ji_f32"] = p.sb("lower_ji_f32", (128, 128), F32)
    cst["ones_row_f32"] = p.sb("ones_row_f32", (1, 128), F32)
    cst["ones_row_bf"] = p.sb("ones_row_bf", (1, 128), BF16)
    cst["ones_col_bf"] = p.sb("ones_col_bf", (128, 1), BF16)
    cst["ones_col_f32"] = p.sb("ones_col_f32", (128, 1), F32)
    deps = [
        p.dma(("g", "cload"), cst["ident_bf"][:, :], ctx.W("cst_ident")[:, :]),
        p.dma(("g", "cload"), cst["causal_add"][:, :], ctx.W("cst_causal_add")[:, :]),
        p.dma(("g", "cload"), cst["lower_ji_bf"][:, :], ctx.W("cst_lower_ji")[:, :]),
        p.dma(("g", "cload"), cst["lower_ji_f32"][:, :], ctx.W("cst_lower_ji")[:, :]),
        p.dma(("g", "cload"), cst["ones_row_f32"][:, :], ctx.W("cst_ones_row")[:, :]),
        p.dma(("g", "cload"), cst["ones_row_bf"][:, :], ctx.W("cst_ones_row")[:, :]),
        p.dma(("g", "cload"), cst["ones_col_bf"][:, :], ctx.W("cst_ones_col")[:, :]),
        p.dma(("g", "cload"), cst["ones_col_f32"][:, :], ctx.W("cst_ones_col")[:, :]),
    ]
    for en in Prog.ENGS:
        p.need(en, deps[-1])
    p.bar()



def row_bcast(ctx, dst, row_ap, n, psum, deps=(), dtype_bf=False):
    p = ctx.p
    ones = ctx.cst["ones_row_f32"]
    mm = None
    for off in range(0, n, 512):
        w = min(512, n - off)
        first = off == 0
        p.need("pe", *deps)
        mm = p.mark(p.eng["pe"].matmul(psum[:, off:off + w], ones[0:1, :],
                                       row_ap[0:1, off:off + w],
                                       start=True, stop=True), "pe")
    dep = None
    p.need("v", mm)
    dep = p.mark(p.eng["v"].tensor_copy(dst[:, 0:n], psum[:, 0:n]), "v")
    return dep


def transpose_to(ctx, dst_ap, src_ap, psum_bf, deps=(), engine="v"):
    p = ctx.p
    p.need("pe", *deps)
    mm = p.mark(p.eng["pe"].transpose(psum_bf[:, :], src_ap, ctx.cst["ident_bf"][:, :]), "pe")
    p.need(engine, mm)
    return p.mark(p.eng[engine].tensor_copy(dst_ap, psum_bf[:, :]), engine)


def ln_stats(ctx, x_tile, mean, rstd, scratch, eps, deps=()):
    p = ctx.p
    v, a = p.eng["v"], p.eng["a"]
    n = x_tile.shape[-1]
    p.need("a", *deps)
    d_sq = p.mark(a.activation(scratch[:, 0:n], x_tile, AF.Square,
                               accum_out=mean[:, 1:2]), "a")
    p.need("v", *deps)
    d = p.mark(v.reduce_sum(mean[:, 0:1], x_tile, axis=AX.X), "v")
    p.need("v", d, d_sq)
    d = p.mark(v.tensor_scalar_mul(mean[:, 0:1], mean[:, 0:1], 1.0 / n), "v")
    p.need("v", d)
    d2 = p.mark(v.tensor_tensor(rstd[:, 1:2], mean[:, 0:1], mean[:, 0:1], ALU.mult), "v")
    d3 = p.mark(v.tensor_scalar(rstd[:, 0:1], mean[:, 1:2], 1.0 / n, None, ALU.mult), "v")
    p.need("v", d2, d3)
    d = p.mark(v.tensor_tensor(rstd[:, 0:1], rstd[:, 0:1], rstd[:, 1:2], ALU.subtract), "v")
    p.need("a", d)
    a.activation(rstd[:, 1:2], rstd[:, 0:1], AF.Ln, bias=p.constf(eps))
    d_a = p.mark(a.activation(rstd[:, 0:1], rstd[:, 1:2], AF.Exp, scale=p.constf(-0.5)), "a")
    return (d_a, d)


def softplus(ctx, dst, src, tmp, deps=(), mark_last=True):
    p = ctx.p
    a, v = p.eng["a"], p.eng["v"]
    p.need("a", *deps)
    a.activation(tmp, src, AF.Abs)
    a.activation(tmp, tmp, AF.Exp, scale=p.constf(-1.0))
    d_l = p.mark(a.activation(tmp, tmp, AF.Ln, bias=p.constf(1.0)), "a")
    p.need("v", *deps)
    v.tensor_relu(dst, src)
    p.need("v", d_l)
    d = p.mark(v.tensor_tensor(dst, dst, tmp, ALU.add), "v")
    return d


def load_w(ctx, name, M, stream=("g", "wload"), deps=(), dtype=BF16, stack=None):
    p = ctx.p
    ap = ctx.W(name)
    Kdim = ap.shape[0]
    assert ap.shape[-1] == M and Kdim % 128 == 0
    wt = p.sb("w_" + name, (128, Kdim // 128, M), dtype, stack=stack)
    dep = p.dma(stream, wt[:, :, :], ap.rearrange("(k p) m -> p k m", p=128), deps=deps)
    return wt, dep


def load_row(ctx, name, stream=("s", "rload"), deps=(), stack=None):
    p = ctx.p
    ap = ctx.W(name)
    n = ap.shape[-1]
    rt = p.sb("r_" + name, (1, n), F32, stack=stack)
    dep = p.dma(stream, rt[0:1, :], ap[0:1, :], deps=deps)
    return rt, dep


def bcast_row(ctx, name, psum, deps=(), stack=None, dtype=F32):
    p = ctx.p
    rt, d = load_row(ctx, name, deps=deps, stack=stack)
    n = rt.shape[-1]
    bc = p.sb("bc_" + name, (128, n), dtype, stack=stack)
    dep = row_bcast(ctx, bc, rt, n, psum, deps=(d,) + tuple(deps))
    return bc, dep


def acc_mm(p, ps_ap, pairs, deps=(), tpos=None):
    n = len(pairs)
    p.need("pe", *deps)
    ins = None
    kw = {} if tpos is None else {"tile_position": tpos}
    for i, (lh, rh) in enumerate(pairs):
        ins = p.eng["pe"].matmul(ps_ap, lh, rh, start=(i == 0), stop=(i == n - 1),
                                 **kw)
    return p.mark(ins, "pe")



def prep_weights(inputs: dict) -> dict:
    w = {}
    tr = lambda x: np.ascontiguousarray(np.asarray(x).T)
    idn = lambda x: np.ascontiguousarray(np.asarray(x))
    row = lambda x: np.ascontiguousarray(np.asarray(x).reshape(1, -1))

    for n in ("Wq", "Wk", "Wv", "Wfc"):
        w[f"L0_attn_{n}"] = idn(inputs[f"L0_attn_{n}"])
    w["L0_attn_bfc"] = row(inputs["L0_attn_bfc"])
    w["L0_attn_gamma"] = row(inputs["L0_attn_gamma"])
    w["L0_attn_beta"] = row(inputs["L0_attn_beta"])
    for n in ("Wq", "Wk_c", "Wv_c", "Wk", "Wv", "Wo", "W_shared_in", "W_shared_out",
              "W1_shared", "W2_experts", "Wg_gate", "We_gate"):
        w[f"L0_ffn_{n}"] = idn(inputs[f"L0_ffn_{n}"])
    w["L0_ffn_group_bias"] = row(inputs["L0_ffn_group_bias"])
    w["L0_ffn_expert_bias"] = row(inputs["L0_ffn_expert_bias"])
    w["L0_ffn_norm1_w"] = row(inputs["L0_ffn_norm1_w"])
    w["L0_ffn_norm2_w"] = row(inputs["L0_ffn_norm2_w"])
    w["L1_attn_w_qkv"] = idn(inputs["L1_attn_w_qkv"])
    w["L1_attn_w_out"] = idn(inputs["L1_attn_w_out"])
    w["L1_attn_b_out"] = row(inputs["L1_attn_b_out"])
    for n in ("k_w", "v_w", "router_w", "w1", "w2"):
        w[f"L1_ffn_{n}"] = idn(inputs[f"L1_ffn_{n}"])
    for n in ("o_b", "router_b"):
        w[f"L1_ffn_{n}"] = row(inputs[f"L1_ffn_{n}"])
    qw = np.asarray(inputs["L1_ffn_q_w"])
    w["L1_ffn_q_w_c"] = np.ascontiguousarray(np.concatenate(
        [qw[:, 64 * h:64 * h + 32] for h in range(16)], axis=1))
    qb = np.asarray(inputs["L1_ffn_q_b"])
    qbc = np.concatenate([qb[64 * h:64 * h + 32] for h in range(16)])
    w["L1_ffn_q_b_f"] = np.ascontiguousarray(qbc.reshape(4, 128).T)
    w["L1_ffn_k_b_f"] = np.ascontiguousarray(
        np.asarray(inputs["L1_ffn_k_b"]).reshape(4, 128).T)
    w["L1_ffn_v_b"] = row(inputs["L1_ffn_v_b"])
    w["L1_ffn_o_w_c"] = idn(np.asarray(inputs["L1_ffn_o_w"])[0:512, :])
    b1 = np.asarray(inputs["L1_ffn_b1"])
    w["L1_ffn_b1_f"] = np.ascontiguousarray(
        b1.reshape(8, 16, 128).transpose(0, 2, 1))
    w["L1_ffn_b2"] = idn(inputs["L1_ffn_b2"])
    w["L2_attn_norm_w"] = row(inputs["L2_attn_norm_w"])
    w["L2_attn_norm_b"] = row(inputs["L2_attn_norm_b"])
    w["L2_attn_W_in"] = tr(inputs["L2_attn_W_in"])
    cw = np.asarray(inputs["L2_attn_conv_w"]).reshape(16, 128, 4)
    w["L2_attn_conv_w"] = np.ascontiguousarray(cw.transpose(1, 0, 2))
    w["L2_attn_conv_b"] = np.ascontiguousarray(
        np.asarray(inputs["L2_attn_conv_b"]).reshape(16, 128).T)
    w["L2_attn_W_dt"] = tr(inputs["L2_attn_W_dt"])
    w["L2_attn_b_dt"] = np.ascontiguousarray(
        np.asarray(inputs["L2_attn_b_dt"]).reshape(16, 128).T)
    w["L2_attn_W_B"] = tr(inputs["L2_attn_W_B"])
    w["L2_attn_W_C"] = tr(inputs["L2_attn_W_C"])
    w["L2_attn_D"] = np.ascontiguousarray(
        np.asarray(inputs["L2_attn_D"]).reshape(16, 128).T)
    al = np.asarray(inputs["L2_attn_A_log"]).reshape(16, 128, 16)
    w["L2_attn_A_log"] = np.ascontiguousarray(al.transpose(1, 0, 2))
    w["L2_attn_W_out"] = tr(inputs["L2_attn_W_out"])
    w["L2_ffn_ln1_w"] = row(inputs["L2_ffn_ln1_w"])
    w["L2_ffn_ln2_w"] = row(inputs["L2_ffn_ln2_w"])
    for n in ("kv_d", "q_d", "k_u", "q_u", "v_u", "rope_q_w", "rope_k_w", "o_w"):
        w[f"L2_ffn_{n}"] = tr(inputs[f"L2_ffn_{n}"])
    w["L2_ffn_router_w"] = tr(inputs["L2_ffn_router_w"])
    w["L2_ffn_routing_bias"] = row(inputs["L2_ffn_routing_bias"])
    w["L2_ffn_shared_gate"] = tr(np.asarray(inputs["L2_ffn_shared_gate"])[0])
    w["L2_ffn_shared_up"] = tr(np.asarray(inputs["L2_ffn_shared_up"])[0])
    w["L2_ffn_shared_down"] = tr(np.asarray(inputs["L2_ffn_shared_down"])[0])
    w["L2_ffn_routed_gate"] = np.ascontiguousarray(
        np.asarray(inputs["L2_ffn_routed_gate"]).transpose(0, 2, 1))
    w["L2_ffn_routed_up"] = np.ascontiguousarray(
        np.asarray(inputs["L2_ffn_routed_up"]).transpose(0, 2, 1))
    w["L2_ffn_routed_down"] = np.ascontiguousarray(
        np.asarray(inputs["L2_ffn_routed_down"]).transpose(0, 2, 1))
    for n in ("Wq", "Wk", "Wv", "Wg1", "Wg2", "Wg", "Wo"):
        w[f"L3_attn_{n}"] = idn(inputs[f"L3_attn_{n}"])
    w["L3_attn_bg2"] = row(inputs["L3_attn_bg2"])
    w["L3_attn_g_norm_w"] = row(inputs["L3_attn_g_norm_w"])
    w["L3_ffn_router_w"] = tr(inputs["L3_ffn_router_w"])
    for n in ("expert_gate", "expert_up", "expert_down", "shared_gate",
              "shared_up", "shared_down"):
        w[f"L3_ffn_{n}"] = idn(inputs[f"L3_ffn_{n}"])
    return w



A_STAGES = {}
F_STAGES = {}

def s_ln(ctx: Ctx, l: int):
    p = ctx.p
    X = ctx.buf(f"X{l}", (T, D), F32, external_in=(l == 0 and ctx.fused))
    xn_f = ctx.buf(f"xn{l}_f", (T, D), F32, write=True)
    xn_b = ctx.buf(f"xn{l}_b", (T, D), BF16, write=True)
    xn_T = ctx.buf(f"xn{l}_T", (D, T), BF16, write=True)
    xn_T3 = xn_T.rearrange("(k p) t -> p k t", p=128)
    want_lo = (l == 3)
    if want_lo:
        xn_lo = ctx.buf(f"xn{l}_loT", (D, T), BF16, write=True)
        xn_lo3 = xn_lo.rearrange("(k p) t -> p k t", p=128)
    with p.stage(f"ln{l}"):
        v = p.eng["v"]
        xt = [p.sb("xt", (128, D), F32) for _ in range(2)]
        nf = [p.sb("nf", (128, D), F32) for _ in range(2)]
        nb = [p.sb("nb", (128, D), BF16) for _ in range(2)]
        tb = [p.sb("tb", (128, KD, 128), BF16) for _ in range(2)]
        lo = [p.sb("lo", (128, D), BF16) for _ in range(2)]
        tl = [p.sb("tl", (128, KD, 128), BF16) for _ in range(2)]
        sq = p.sb("sq", (128, D), F32)
        st = p.sb("st", (128, 4), F32)
        ps_t = [p.ps("ps_t", (128, 128), BF16) for _ in range(2)]
        xt_free = [None, None]
        nb_free = [None, None]
        tb_free = [None, None]
        cp_prev = [None, None]
        nf_st = [None, None]
        nb_st = [None, None]
        lo_free = [None, None]
        tl_free = [None, None]
        prev_nf = None
        for q in range(NT):
            c = q % 2
            s = slice(q * 128, (q + 1) * 128)
            dl = p.dma(("s", f"a{c}"), xt[c][:, :], X[s, :], deps=(xt_free[c],))
            d_st = ln_stats(ctx, xt[c][:, :], st[:, 0:2], st[:, 2:4], sq, 1e-5,
                            deps=(dl, prev_nf))
            p.need("v", *d_st, nb_free[c], nf_st[c], nb_st[c])
            d_nf = p.mark(v.tensor_scalar(nf[c][:, :], xt[c][:, :], st[:, 0:1], st[:, 2:3],
                                          ALU.subtract, ALU.mult), "v")
            xt_free[c] = d_nf
            prev_nf = d_nf
            d_nb = p.mark(v.tensor_copy(nb[c][:, :], nf[c][:, :]), "v")
            nf_st[c] = p.dma(("s", f"sn{c}"), xn_f[s, :], nf[c][:, :], deps=(d_nf,))
            nb_st[c] = p.dma(("s", f"sb{c}"), xn_b[s, :], nb[c][:, :], deps=(d_nb,))
            if want_lo:
                p.need("v", lo_free[c])
                d_lo = p.mark(v.tensor_tensor(lo[c][:, :], nf[c][:, :], nb[c][:, :],
                                              ALU.subtract), "v")
            last_cp = None
            for k in range(KD):
                pc = k % 2
                p.need("pe", d_nb, cp_prev[pc])
                mm = p.mark(p.eng["pe"].transpose(ps_t[pc][:, 0:128],
                                                  nb[c][:, k * 128:(k + 1) * 128],
                                                  ctx.cst["ident_bf"][:, :]), "pe")
                if k == KD - 1:
                    nb_free[c] = mm
                p.need("a", mm, tb_free[c])
                last_cp = p.mark(p.eng["a"].copy(tb[c][:, k, :], ps_t[pc][:, 0:128]), "a")
                cp_prev[pc] = last_cp
            tb_free[c] = p.dma(("s", f"tc{c}"), xn_T3[:, :, s], tb[c][:, :, :],
                               deps=(last_cp,))
            if want_lo:
                last_cp2 = None
                for k in range(KD):
                    pc = k % 2
                    p.need("pe", d_lo, cp_prev[pc])
                    mm = p.mark(p.eng["pe"].transpose(ps_t[pc][:, 0:128],
                                                      lo[c][:, k * 128:(k + 1) * 128],
                                                      ctx.cst["ident_bf"][:, :]), "pe")
                    if k == KD - 1:
                        lo_free[c] = mm
                    p.need("a", mm, tl_free[c])
                    last_cp2 = p.mark(p.eng["a"].copy(tl[c][:, k, :], ps_t[pc][:, 0:128]), "a")
                    cp_prev[pc] = last_cp2
                tl_free[c] = p.dma(("s", f"tl{c}"), xn_lo3[:, :, s], tl[c][:, :, :],
                                   deps=(last_cp2,))



def s_res(ctx: Ctx, l: int):
    p = ctx.p
    X = ctx.buf(f"X{l}", (T, D), F32)
    A = ctx.buf(f"brA{l}", (T, D), F32)
    B = ctx.buf(f"brB{l}", (T, D), F32)
    Xn = ctx.buf(f"X{l + 1}", (T, D), F32, write=True,
                 final=(l == 3 and ctx.fused))
    deps = ()
    with p.stage(f"res{l}"):
        v = p.eng["v"]
        xt = [p.sb(f"x{i}", (128, D), F32) for i in range(2)]
        at = [p.sb(f"a{i}", (128, D), F32) for i in range(2)]
        bt = [p.sb(f"b{i}", (128, D), F32) for i in range(2)]
        ot = [p.sb(f"o{i}", (128, D), F32) for i in range(2)]
        last_use = [None, None]
        st_dep = [None, None]
        for q in range(NT):
            c = q % 2
            s = slice(q * 128, (q + 1) * 128)
            d1 = p.dma(("s", f"a{c}"), xt[c][:, :], X[s, :], deps=deps + (last_use[c],))
            d2 = p.dma(("s", f"b{c}"), at[c][:, :], A[s, :], deps=deps + (last_use[c],))
            d3 = p.dma(("g", f"c{c}"), bt[c][:, :], B[s, :], deps=deps + (last_use[c],))
            p.need("v", d1, d2, d3, st_dep[c])
            v.tensor_tensor(ot[c][:, :], xt[c][:, :], at[c][:, :], ALU.add)
            du = p.mark(v.tensor_tensor(ot[c][:, :], ot[c][:, :], bt[c][:, :], ALU.add), "v")
            last_use[c] = du
            st_dep[c] = p.dma(("s", f"o{c}"), Xn[s, :], ot[c][:, :], deps=(du,))


def evac(p, eng, out_ap, ps_ap, deps=()):
    p.need(eng, *deps)
    if eng == "v":
        return p.mark(p.eng["v"].tensor_copy(out_ap, ps_ap), "v")
    return p.mark(p.eng["a"].copy(out_ap, ps_ap), "a")


def proj_fm(ctx, out, wt, wcol0, xnf, deps, psA, psA_free, bias_fm=None,
            act=None):
    p = ctx.p
    nm = out.shape[1]
    nk = wt.shape[1]
    last = [None, None]
    for m in range(nm):
        for tsl in range(T // 512):
            t0 = tsl * 512
            b = tsl % 2
            pairs = [(wt[:, k, wcol0 + m * 128: wcol0 + (m + 1) * 128],
                      xnf[:, k, t0:t0 + 512]) for k in range(nk)]
            mm = acc_mm(p, psA[b][:, :], pairs, deps=tuple(deps) + tuple(psA_free[b]))
            if bias_fm is None and act is None:
                ev = evac(p, "v" if b == 0 else "a", out[:, m, t0:t0 + 512],
                          psA[b][:, :], deps=(mm,))
            elif act is None:
                p.need("v", mm)
                ev = p.mark(p.eng["v"].tensor_scalar_add(
                    out[:, m, t0:t0 + 512], psA[b][:, :], bias_fm[:, m:m + 1]), "v")
            else:
                p.need("a", mm)
                kw = {}
                if bias_fm is not None:
                    kw["bias"] = bias_fm[:, m:m + 1]
                ev = p.mark(p.eng["a"].activation(
                    out[:, m, t0:t0 + 512], psA[b][:, :], act, **kw), "a")
            psA_free[b] = (ev,)
            last[b] = ev
    return tuple(d for d in last if d is not None)


def proj_tm(ctx, out, wt, wcol0, N, xnf, deps, psA, psA_free, bias_bc=None):
    p = ctx.p
    v = p.eng["v"]
    nk = wt.shape[1]
    last = [None, None]
    for q in range(NT):
        qs = slice(q * 128, (q + 1) * 128)
        for h0 in range(0, N, 512):
            w_ = min(512, N - h0)
            b = (h0 // 512) % 2
            pairs = [(xnf[:, k, qs], wt[:, k, wcol0 + h0: wcol0 + h0 + w_])
                     for k in range(nk)]
            mm = acc_mm(p, psA[b][:, 0:w_], pairs, deps=tuple(deps) + tuple(psA_free[b]))
            if bias_bc is None:
                ev = evac(p, "v" if b == 0 else "a", out[:, q, h0:h0 + w_],
                          psA[b][:, 0:w_], deps=(mm,))
            else:
                p.need("v", mm)
                ev = p.mark(v.tensor_tensor(out[:, q, h0:h0 + w_], psA[b][:, 0:w_],
                                            bias_bc[:, h0:h0 + w_], ALU.add), "v")
            psA_free[b] = (ev,)
            last[b] = ev
    return tuple(d for d in last if d is not None)


def proj_fm_dram(ctx, dram3, wt, xnf, deps, psA, psA_free, stream):
    p = ctx.p
    nm = dram3.shape[1]
    nk = wt.shape[1]
    stg = [p.sb("stg", (128, 512), BF16) for _ in range(2)]
    stg_free = [None, None]
    for m in range(nm):
        for tsl in range(T // 512):
            t0 = tsl * 512
            b = tsl % 2
            pairs = [(wt[:, k, m * 128:(m + 1) * 128], xnf[:, k, t0:t0 + 512])
                     for k in range(nk)]
            mm = acc_mm(p, psA[b][:, :], pairs, deps=tuple(deps) + tuple(psA_free[b]))
            ev = evac(p, "v" if b == 0 else "a", stg[b][:, :], psA[b][:, :],
                      deps=(mm, stg_free[b]))
            psA_free[b] = (ev,)
            stg_free[b] = p.dma((stream[0], stream[1] + str(b)),
                                dram3[:, m, t0:t0 + 512], stg[b][:, :], deps=(ev,))
    return stg_free


def mha_pools(p):
    return {
        "psA": [p.ps("psA", (128, 512), F32) for _ in range(2)],
        "psT": [p.ps("psT", (128, 512), BF16) for _ in range(2)],
        "psO": [p.ps("psO", (128, 512), F32) for _ in range(2)],
        "psA_free": [(), ()],
        "psT_free": [None, None],
    }


def mha_core(ctx, qf, kf, vb, pools, *, H, dk, dv, causal, scale, in_deps,
             emit_out, mask_extra=None, q_dram=None, score_pairs=None,
             arena=None):
    p = ctx.p
    v, a, pe = p.eng["v"], p.eng["a"], p.eng["pe"]
    psA, psT, psO = pools["psA"], pools["psT"], pools["psO"]
    S = [p.sb("S", (128, T), F32, stack=arena) for _ in range(2)]
    P = [p.sb("P", (128, T), BF16, stack=arena) for _ in range(2)]
    PT = [p.sb("PT", (128, T), BF16, stack=arena) for _ in range(2)]
    stt = [p.sb("stt", (128, 4), F32, stack=arena) for _ in range(2)]
    ot = [p.sb("ot", (128, H * dv), F32, stack=arena) for _ in range(2)]
    if q_dram is not None:
        if not isinstance(q_dram, (list, tuple)):
            q_dram = [q_dram]
        nkqs = [ap.shape[0] // 128 for ap in q_dram]
        nkq = sum(nkqs)
        qtl = [p.sb("qtl", (128, nkq, 128), BF16, stack=arena) for _ in range(2)]
        qtl_free = [None, None]
    sc_ap = p.constf(scale)
    exp_prev = [None, None]
    rm_prev = [None, None]
    pt_mm_prev = [None, None]
    pt_rd_prev = [None, None]
    ot_free = [None, None]
    psO_free = [None, None]
    for q in range(NT):
        c = q % 2
        qs = slice(q * 128, (q + 1) * 128)
        if q_dram is not None:
            d_qt = ()
            k0 = 0
            for ap3, nk_i in zip(q_dram3 := [ap.rearrange("(k p) t -> p k t", p=128)
                                             for ap in q_dram], nkqs)                     if False else zip([ap.rearrange("(k p) t -> p k t", p=128)
                                       for ap in q_dram], nkqs):
                d_qt = d_qt + (p.dma(("s", f"q{c}{k0}"), qtl[c][:, k0:k0 + nk_i, :],
                                     ap3[:, :, qs], deps=(qtl_free[c],)),)
                k0 += nk_i
        kw = (q + 1) * 128 if causal else T
        nkt = kw // 128
        d_o = None
        for h in range(H):
            hc = h % 2
            bq, cq = (h * dk) % 128, (h * dk) // 128
            ev_S = None
            for g0 in range(0, kw, 512):
                w_ = min(512, kw - g0)
                b = (g0 // 512) % 2
                if score_pairs is not None:
                    prs = score_pairs(qtl[c] if q_dram is not None else qf, h,
                                      qs, g0, w_)
                else:
                    q_ap = (qtl[c][bq:bq + dk, cq, :] if q_dram is not None
                            else qf[bq:bq + dk, cq, qs])
                    prs = [(q_ap, kf[bq:bq + dk, cq, g0:g0 + w_], (bq, 0))]
                p.need("pe", *(tuple(in_deps) + tuple(pools["psA_free"][b]) +
                               (d_qt if q_dram is not None else ())))
                mm = None
                for i, (lh, rh, tp) in enumerate(prs):
                    mm = p.mark(pe.matmul(psA[b][:, 0:w_], lh, rh,
                                          start=(i == 0), stop=(i == len(prs) - 1),
                                          tile_position=tp), "pe")
                if q_dram is not None:
                    qtl_free[c] = mm
                ev_S = evac(p, "v" if b == 0 else "a", S[hc][:, g0:g0 + w_],
                            psA[b][:, 0:w_], deps=(mm, exp_prev[hc]))
                pools["psA_free"][b] = (ev_S,)
            p.need("v", ev_S, p.tip("a") if kw > 512 else None)
            if causal:
                v.tensor_tensor(S[hc][:, qs], S[hc][:, qs],
                                ctx.cst["causal_add"][:, :], ALU.add)
            if mask_extra is not None:
                mask_extra(q, h, S[hc], kw)
            p.need("v", rm_prev[hc])
            d_rm = p.mark(v.reduce_max(stt[hc][:, 0:1], S[hc][:, 0:kw], axis=AX.X,
                                       negate=True), "v")
            p.need("v", d_rm)
            d_nm = p.mark(v.tensor_scalar_mul(stt[hc][:, 1:2], stt[hc][:, 0:1],
                                              scale), "v")
            p.need("a", d_nm, pt_mm_prev[hc])
            d_exp = p.mark(a.activation(P[hc][:, 0:kw], S[hc][:, 0:kw], AF.Exp,
                                        bias=stt[hc][:, 1:2], scale=sc_ap,
                                        accum_out=stt[hc][:, 2:3]), "a")
            exp_prev[hc] = d_exp
            p.need("v", d_exp)
            d_rd = p.mark(v.reciprocal(stt[hc][:, 3:4], stt[hc][:, 2:3]), "v")
            rm_prev[hc] = d_rd
            ev_pt = []
            for g0 in range(0, nkt, 4):
                gn = min(4, nkt - g0)
                tc = (g0 // 4) % 2
                mmt = None
                for j in range(gn):
                    p.need("pe", d_exp, pools["psT_free"][tc], pt_rd_prev[hc])
                    mmt = p.mark(pe.transpose(psT[tc][:, j * 128:(j + 1) * 128],
                                              P[hc][:, (g0 + j) * 128:(g0 + j + 1) * 128],
                                              ctx.cst["ident_bf"][:, :]), "pe")
                pt_mm_prev[hc] = mmt
                evp = evac(p, "v" if tc == 0 else "a",
                           PT[hc][:, g0 * 128:(g0 + gn) * 128],
                           psT[tc][:, 0:gn * 128], deps=(mmt,))
                pools["psT_free"][tc] = evp
                ev_pt.append(evp)
            pairs = [(PT[hc][:, kt * 128:(kt + 1) * 128],
                      vb[:, kt, h * dv:(h + 1) * dv]) for kt in range(nkt)]
            mmo = acc_mm(p, psO[hc][:, 0:dv], pairs,
                         deps=tuple(ev_pt) + (psO_free[hc],) + tuple(in_deps))
            pt_rd_prev[hc] = mmo
            p.need("v", mmo, d_rd, ot_free[c] if h < 2 else None)
            d_o = p.mark(v.tensor_scalar_mul(ot[c][:, h * dv:(h + 1) * dv],
                                             psO[hc][:, 0:dv], stt[hc][:, 3:4]), "v")
            psO_free[hc] = d_o
        ot_free[c] = emit_out(q, c, ot[c], d_o)


def tm_to_fm_out(ctx, pools, wo, dwo, bias_bc, dbias, psC, otb, lnfm, ob, dst,
                 state, q, c, ot, d_o, nko=KD, ncol=D, extra_add=None, store=True):
    p = ctx.p
    v, a, pe = p.eng["v"], p.eng["a"], p.eng["pe"]
    s = slice(q * 128, (q + 1) * 128)
    p.need("a", d_o, state["fm_mm_prev"])
    d_cast = p.mark(a.copy(otb[:, 0:nko * 128], ot[:, 0:nko * 128]), "a")
    ev_fm = [None, None]
    for k in range(nko):
        tc = k % 2
        p.need("pe", d_cast, pools["psT_free"][tc])
        mmt = p.mark(pe.transpose(pools["psT"][tc][:, 0:128],
                                  otb[:, k * 128:(k + 1) * 128],
                                  ctx.cst["ident_bf"][:, :]), "pe")
        ev_fm[tc] = evac(p, "v" if tc == 0 else "a", lnfm[:, k, :],
                         pools["psT"][tc][:, 0:128], deps=(mmt, state["fm_mm_prev"]))
        pools["psT_free"][tc] = ev_fm[tc]
    mm_f = None
    for h0 in range(0, ncol, 512):
        w_ = min(512, ncol - h0)
        pairs = [(lnfm[:, k, :], wo[:, k, h0:h0 + w_]) for k in range(nko)]
        mm_f = acc_mm(p, psC[:, h0:h0 + w_], pairs,
                      deps=(ev_fm[0], ev_fm[1], dwo, state["psC_free"]))
    state["fm_mm_prev"] = mm_f
    p.need("v", mm_f, dbias, state["ob_free"][c])
    if bias_bc is not None:
        d_ob = p.mark(v.tensor_tensor(ob[c][:, 0:ncol], psC[:, 0:ncol],
                                      bias_bc[:, 0:ncol], ALU.add), "v")
    else:
        d_ob = p.mark(v.tensor_copy(ob[c][:, 0:ncol], psC[:, 0:ncol]), "v")
    if extra_add is not None:
        d_ob = extra_add(q, c, d_ob)
    state["psC_free"] = d_ob
    if store:
        dep = p.dma(("s", f"o{c}"), dst[s, :], ob[c][:, 0:ncol], deps=(d_ob,))
        state["ob_free"][c] = dep
    else:
        state["ob_free"][c] = d_ob
    return d_cast


def s_attn1(ctx: Ctx, l: int):
    p = ctx.p
    xn_T = ctx.buf("xn1_T", (D, T), BF16)
    brA = ctx.buf("brA1", (T, D), F32, write=True)
    with p.stage("attn1"):
        pools = mha_pools(p)
        psC = p.ps("psC", (128, 1024), F32)
        qf = p.sb("qf", (128, KD, T), BF16)
        kf = p.sb("kf", (128, KD, T), BF16)
        vb = p.sb("vb", (128, NT, D), BF16)
        subA = contextlib.ExitStack()
        wqkv_ap = ctx.W("L1_attn_w_qkv").rearrange("(k p) m -> p k m", p=128)
        wq = p.sb("wq", (128, KD, D), BF16, stack=subA)
        wk = p.sb("wk", (128, KD, D), BF16, stack=subA)
        wv = p.sb("wv", (128, KD, D), BF16, stack=subA)
        dwq = p.dma(("g", "w0"), wq[:, :, :], wqkv_ap[:, :, 0:D])
        dwk = p.dma(("g", "w1"), wk[:, :, :], wqkv_ap[:, :, D:2 * D])
        dwv = p.dma(("g", "w2"), wv[:, :, :], wqkv_ap[:, :, 2 * D:3 * D])
        xnf = p.sb("xnf", (128, KD, T), BF16, stack=subA)
        dxn = p.dma(("s", "ax"), xnf[:, :, :],
                    xn_T.rearrange("(k p) t -> p k t", p=128))
        dq = proj_fm(ctx, qf, wq, 0, xnf, (dwq, dxn), pools["psA"], pools["psA_free"])
        dk_ = proj_fm(ctx, kf, wk, 0, xnf, (dwk, dxn), pools["psA"], pools["psA_free"])
        dv_ = proj_tm(ctx, vb, wv, 0, D, xnf, (dwv, dxn), pools["psA"],
                      pools["psA_free"])
        p.bar()
        subA.close()
        subB = contextlib.ExitStack()
        wo, dwo = load_w(ctx, "L1_attn_w_out", D, stream=("g", "w0"), stack=subB)
        bo, dbo = bcast_row(ctx, "L1_attn_b_out", psC, stack=subB)
        ob = [p.sb("ob", (128, D), F32) for _ in range(2)]
        otb = p.sb("otb", (128, D), BF16)
        lnfm = p.sb("lnfm", (128, KD, 128), BF16)
        state = {"psC_free": dbo, "fm_mm_prev": None, "cp_prev": None,
                 "ob_free": [None, None]}

        def emit_out(q, c, ot, d_o):
            return tm_to_fm_out(ctx, pools, wo, dwo, bo, dbo, psC, otb, lnfm,
                                ob, brA, state, q, c, ot, d_o)

        subM = contextlib.ExitStack()
        mha_core(ctx, qf, kf, vb, pools, H=16, dk=64, dv=64, causal=True,
                 scale=0.125, in_deps=dq + dk_ + dv_, emit_out=emit_out,
                 arena=subM)
        p.bar()
        subM.close()
        subB.close()


A_STAGES[1] = s_attn1

def proj_tm_dram(ctx, dram, wt, xnf, deps, psA, psA_free, stream, bias_bc=None):
    p = ctx.p
    N = wt.shape[2]
    nk = wt.shape[1]
    stg = [p.sb("stg", (128, N), BF16) for _ in range(2)]
    stg_free = [None, None]
    for q in range(NT):
        c = q % 2
        qs = slice(q * 128, (q + 1) * 128)
        evs = []
        for h0 in range(0, N, 512):
            w_ = min(512, N - h0)
            b = (h0 // 512) % 2
            pairs = [(xnf[:, k, qs], wt[:, k, h0:h0 + w_]) for k in range(nk)]
            mm = acc_mm(p, psA[b][:, 0:w_], pairs, deps=tuple(deps) + tuple(psA_free[b]))
            ev = evac(p, "v" if b == 0 else "a", stg[c][:, h0:h0 + w_],
                      psA[b][:, 0:w_], deps=(mm, stg_free[c]))
            psA_free[b] = (ev,)
            evs.append(ev)
        stg_free[c] = p.dma((stream[0], stream[1] + str(c)), dram[qs, :],
                            stg[c][:, :], deps=tuple(evs))
    return stg_free


def s_attn3(ctx: Ctx, l: int):
    p = ctx.p
    v, a, pe = p.eng["v"], p.eng["a"], p.eng["pe"]
    xn_T = ctx.buf("xn3_T", (D, T), BF16)
    brA = ctx.buf("brA3", (T, D), F32, write=True)
    qd = ctx.buf("l3a_q", (T, D), BF16, write=True)
    rowd = ctx.buf("l3a_row", (1, D), F32, write=True)
    kd = ctx.buf("l3a_k", (T, D), BF16, write=True)
    vd = ctx.buf("l3a_v", (T, D), BF16, write=True)
    gd = ctx.buf("l3a_g", (T, D), BF16, write=True)
    H, DH = 16, 64
    SCALE = DH ** -0.5
    with p.stage("attn3"):
        psA = [p.ps("psA", (128, 512), F32) for _ in range(2)]
        psT = [p.ps("psT", (128, 512), BF16) for _ in range(2)]
        psO = p.ps("psO", (128, 512), F32)
        psC = p.ps("psC", (128, 1024), F32)
        pools = {"psA": psA, "psT": psT, "psA_free": [(), ()],
                 "psT_free": [None, None]}
        subA = contextlib.ExitStack()
        xnf = p.sb("xnf", (128, KD, T), BF16, stack=subA)
        dxn = p.dma(("s", "ax"), xnf[:, :, :],
                    xn_T.rearrange("(k p) t -> p k t", p=128))
        u_fm = p.sb("u_fm", (16, T), BF16)
        for (nm, dst) in (("Wq", qd), ("Wk", kd), ("Wv", vd), ("Wg", gd)):
            subW = contextlib.ExitStack()
            wt, dwt = load_w(ctx, f"L3_attn_{nm}", D, stream=("g", "w0"), stack=subW)
            proj_tm_dram(ctx, dst, wt, xnf, (dwt, dxn), psA, pools["psA_free"],
                         ("s", "t" + nm))
            p.bar()
            subW.close()
        subW = contextlib.ExitStack()
        wg1, dwg1 = load_w(ctx, "L3_attn_Wg1", 16, stream=("g", "w0"), stack=subW)
        ut = p.sb("ut", (128, 16), BF16, stack=subW)
        uf = p.sb("uf", (128, 16), F32, stack=subW)
        ev_prev = None
        for q in range(NT):
            qs = slice(q * 128, (q + 1) * 128)
            pairs = [(xnf[:, k, qs], wg1[:, k, :]) for k in range(KD)]
            mm = acc_mm(p, psA[0][:, 0:16], pairs,
                        deps=(dwg1, dxn, ev_prev) + tuple(pools["psA_free"][0]))
            p.need("v", mm)
            d_u = p.mark(v.tensor_copy(ut[:, :], psA[0][:, 0:16]), "v")
            p.need("pe", d_u, ev_prev)
            mmt = p.mark(pe.transpose(psT[0][0:16, 0:128], ut[:, :],
                                      ctx.cst["ident_bf"][:, :]), "pe")
            ev_prev = evac(p, "a", u_fm[0:16, qs], psT[0][0:16, 0:128], deps=(mmt,))
            pools["psA_free"][0] = (d_u,)
            pools["psT_free"][0] = ev_prev
        p.bar()
        subW.close()
        subA.close()
        wg2 = p.sb("wg2", (16, 1, D), BF16)
        dwg2 = p.dma(("g", "w0"), wg2[:, 0, :], ctx.W("L3_attn_Wg2")[:, :])
        wo, dwo = load_w(ctx, "L3_attn_Wo", D, stream=("g", "w1"))
        bg2bc, dbg = bcast_row(ctx, "L3_attn_bg2", psC)
        gnwbc, dgn = bcast_row(ctx, "L3_attn_g_norm_w", psC, deps=(dbg,))
        Sst = p.sb("Sst", (128, KD, DH), F32)
        Ssb = p.sb("Ssb", (128, KD, DH), BF16)
        v.memset(Sst[:, :, :], 0.0)
        d_z0 = p.mark(v.memset(Ssb[:, :, :], 0.0), "v")
        qt = [p.sb("qt", (128, D), BF16) for _ in range(2)]
        kt = [p.sb("kt", (128, D), BF16) for _ in range(2)]
        vt = [p.sb("vt", (128, D), BF16) for _ in range(2)]
        gt = [p.sb("gt", (128, D), BF16) for _ in range(2)]
        zf = p.sb("zf", (128, D), F32)
        scr1 = p.sb("scr1", (128, D), F32)
        gkf = p.sb("gkf", (128, D), F32)
        Af = p.sb("Af", (128, D), F32)
        rbc = p.sb("rbc", (128, D), F32)
        e1 = p.sb("e1", (128, D), F32)
        e2 = p.sb("e2", (128, D), F32)
        e3 = p.sb("e3", (128, D), F32)
        at_row = p.sb("at_row", (1, D), F32)
        rcol = p.sb("rcol", (128, KD), F32)
        ercol = p.sb("ercol", (128, KD), F32)
        dtcol = p.sb("dtcol", (128, KD), F32)
        qp = p.sb("qp", (128, D), BF16)
        kp = p.sb("kp", (128, D), BF16)
        kpp = p.sb("kpp", (128, D), BF16)
        qp_fm = p.sb("qp_fm", (128, KD, 128), BF16)
        kp_fm = p.sb("kp_fm", (128, KD, 128), BF16)
        qpp_fm = p.sb("qpp_fm", (128, KD, 128), BF16)
        St_sb = p.sb("St_sb", (128, 128), BF16)
        ot = p.sb("ot", (128, D), F32)
        ssq = p.sb("ssq", (128, 16), F32)
        obt = p.sb("obt", (128, D), BF16)
        obfm = p.sb("obfm", (128, KD, 128), BF16)
        ob = [p.sb("ob", (128, D), F32) for _ in range(2)]
        ob_free = [None, None]
        ld_free = [None, None]
        lnS = p.constf(float(np.log(SCALE)))
        NCH = T // 128
        prev = {"psC": (dgn,), "fm": None, "St": None, "ost": None, "kv": None,
                "o_ev": None, "gk": None}
        for ch in range(NCH):
            c = ch % 2
            qs = slice(ch * 128, (ch + 1) * 128)
            dq_ = p.dma(("s", f"a{c}"), qt[c][:, :], qd[qs, :], deps=(ld_free[c],))
            dk_ = p.dma(("s", f"b{c}"), kt[c][:, :], kd[qs, :], deps=(ld_free[c],))
            dv_ = p.dma(("s", f"c{c}"), vt[c][:, :], vd[qs, :], deps=(ld_free[c],))
            dg_ = p.dma(("s", f"d{c}"), gt[c][:, :], gd[qs, :], deps=(ld_free[c],))
            mmz = None
            for h0 in range(2):
                mmz = acc_mm(p, psC[:, h0 * 512:(h0 + 1) * 512],
                             [(u_fm[0:16, qs], wg2[0:16, 0, h0 * 512:(h0 + 1) * 512])],
                             deps=(dwg2, p.tip("v")) + tuple(prev["psC"]))
            p.need("v", mmz, dbg)
            d_z = p.mark(v.tensor_tensor(zf[:, :], psC[:, :], bg2bc[:, :],
                                         ALU.add), "v")
            p.need("a", d_z)
            a.activation(scr1[:, :], zf[:, :], AF.Abs)
            a.activation(scr1[:, :], scr1[:, :], AF.Exp, scale=p.constf(-1.0))
            d_l = p.mark(a.activation(scr1[:, :], scr1[:, :], AF.Ln,
                                      bias=p.constf(1.0)), "a")
            p.need("v", d_l)
            v.tensor_scalar_min(zf[:, :], zf[:, :], 0.0)
            v.tensor_tensor(zf[:, :], zf[:, :], scr1[:, :], ALU.subtract)
            d_gk = p.mark(v.tensor_scalar_mul(gkf[:, :], zf[:, :], 1.0 / 16.0), "v")
            mmA = None
            for h0 in range(2):
                mmA = acc_mm(p, psC[:, h0 * 512:(h0 + 1) * 512],
                             [(ctx.cst["lower_ji_f32"][:, :],
                               gkf[:, h0 * 512:(h0 + 1) * 512])],
                             deps=(d_gk, d_z))
            mmT = None
            for h0 in range(2):
                mmT = acc_mm(p, psA[0][0:1, 0:512],
                             [(ctx.cst["ones_col_f32"][:, :],
                               gkf[:, h0 * 512:(h0 + 1) * 512])],
                             deps=(d_gk,) + tuple(pools["psA_free"][0]))                     if h0 == 0 else                     acc_mm(p, psA[1][0:1, 0:512],
                           [(ctx.cst["ones_col_f32"][:, :],
                             gkf[:, h0 * 512:(h0 + 1) * 512])],
                           deps=(d_gk,) + tuple(pools["psA_free"][1]))
            p.need("v", mmA, mmT)
            d_A = p.mark(v.tensor_copy(Af[:, :], psC[:, :]), "v")
            prev["psC"] = (d_A,)
            v.tensor_scalar_mul(at_row[0:1, 0:512], psA[0][0:1, 0:512], 0.5)
            d_at = p.mark(v.tensor_scalar_mul(at_row[0:1, 512:1024],
                                              psA[1][0:1, 0:512], 0.5), "v")
            pools["psA_free"][0] = (d_at,)
            pools["psA_free"][1] = (d_at,)
            mmb = None
            for h0 in range(2):
                mmb = acc_mm(p, psC[:, h0 * 512:(h0 + 1) * 512],
                             [(ctx.cst["ones_row_f32"][0:1, :],
                               at_row[0:1, h0 * 512:(h0 + 1) * 512])],
                             deps=(d_at, d_A))
            p.need("v", mmb)
            d_rb = p.mark(v.tensor_copy(rbc[:, :], psC[:, :]), "v")
            prev["psC"] = (d_rb,)
            d_r1 = p.dma(("s", "rc"), rowd[0:1, :], at_row[0:1, :],
                         deps=(d_at,))
            d_rc = p.dma(("s", "rc2"), rcol[:, :],
                         rowd.rearrange("o (k p) -> p (k o)", p=128),
                         deps=(d_r1, p.tip("a")),
                         allow_slow_non_contiguous=True)
            p.need("a", d_rc, d_rb)
            a.activation(ercol[:, :], rcol[:, :], AF.Exp)
            d_dt = p.mark(a.activation(dtcol[:, :], rcol[:, :], AF.Exp,
                                       scale=p.constf(2.0)), "a")
            p.need("v", d_rb)
            d_s1 = p.mark(v.tensor_tensor(e1[:, :], Af[:, :], rbc[:, :],
                                          ALU.subtract), "v")
            p.need("a", d_s1)
            d_e1 = p.mark(a.activation(e1[:, :], e1[:, :], AF.Exp, bias=lnS), "a")
            d_s2 = p.mark(v.tensor_tensor(e2[:, :], rbc[:, :], Af[:, :],
                                          ALU.subtract), "v")
            p.need("a", d_s2)
            d_e2 = p.mark(a.activation(e2[:, :], e2[:, :], AF.Exp), "a")
            v.tensor_scalar_mul(e3[:, :], rbc[:, :], 2.0)
            d_s3 = p.mark(v.tensor_tensor(e3[:, :], e3[:, :], Af[:, :],
                                          ALU.subtract), "v")
            p.need("a", d_s3)
            d_e3 = p.mark(a.activation(e3[:, :], e3[:, :], AF.Exp), "a")
            p.need("v", d_e1, dq_, prev["fm"])
            d_qp = p.mark(v.tensor_tensor(qp[:, :], qt[c][:, :], e1[:, :],
                                          ALU.mult), "v")
            p.need("v", d_e2, dk_)
            d_kp = p.mark(v.tensor_tensor(kp[:, :], kt[c][:, :], e2[:, :],
                                          ALU.mult), "v")
            p.need("v", d_e3, dk_)
            d_kpp = p.mark(v.tensor_tensor(kpp[:, :], kt[c][:, :], e3[:, :],
                                          ALU.mult), "v")
            p.need("v", d_dt)
            evq = [None, None]
            for (src_t, dst_t) in ((qp, qp_fm), (kp, kp_fm)):
                dep_rp = d_qp if src_t is qp else d_kp
                for k in range(KD):
                    tc = k % 2
                    p.need("pe", dep_rp, pools["psT_free"][tc])
                    mmt = p.mark(pe.transpose(psT[tc][:, 0:128],
                                              src_t[:, k * 128:(k + 1) * 128],
                                              ctx.cst["ident_bf"][:, :]), "pe")
                    ev = evac(p, "v" if tc == 0 else "a", dst_t[:, k, :],
                              psT[tc][:, 0:128], deps=(mmt,))
                    pools["psT_free"][tc] = ev
                    evq[tc] = ev
            d_qpp = None
            for k in range(KD):
                p.need("v", evq[0], evq[1])
                d_qpp = p.mark(v.tensor_scalar_mul(qpp_fm[:, k, :], qp_fm[:, k, :],
                                                   ercol[:, k:k + 1]), "v")
            d_o = None
            for h in range(H):
                rows = (h * DH) % 128
                chk = (h * DH) // 128
                hs = slice(h * DH, (h + 1) * DH)
                mmS = acc_mm(p, psA[0][:, 0:128],
                             [(kp_fm[rows:rows + DH, chk, :],
                               qp_fm[rows:rows + DH, chk, :], )],
                             deps=(evq[0], evq[1], d_qpp) + tuple(pools["psA_free"][0]),
                             tpos=(rows, 0))
                p.need("v", mmS)
                d_St = p.mark(v.tensor_tensor(St_sb[:, :], psA[0][:, 0:128],
                                              ctx.cst["lower_ji_bf"][:, :],
                                              ALU.mult), "v")
                pools["psA_free"][0] = (d_St,)
                mmo = acc_mm(p, psO[:, 0:DH],
                             [(St_sb[:, :], vt[c][:, hs])],
                             deps=(d_St, dv_, prev["o_ev"]))
                mst = acc_mm(p, psA[1][:, 0:DH],
                             [(qpp_fm[rows:rows + DH, chk, :],
                               Ssb[rows:rows + DH, chk, :])],
                             deps=(d_qpp, d_z0, prev["o_ev"], prev["St"]),
                             tpos=(rows, 0))
                p.need("v", mmo, mst)
                v.tensor_copy(ot[:, hs], psO[:, 0:DH])
                d_o = p.mark(v.tensor_tensor(ot[:, hs], ot[:, hs], psA[1][:, 0:DH],
                                             ALU.add), "v")
                prev["o_ev"] = d_o
                pools["psA_free"][1] = (d_o,)
                mkv = acc_mm(p, psC[rows:rows + DH, 0:DH],
                             [(kt[c][:, hs] if False else kpp[:, hs], vt[c][:, hs])],
                             deps=(d_kpp, dv_) + tuple(prev["psC"]),
                             tpos=(0, rows))
                p.need("v", mkv, d_dt)
                v.tensor_scalar_mul(Sst[rows:rows + DH, chk, :],
                                    Sst[rows:rows + DH, chk, :],
                                    dtcol[rows:rows + DH, chk:chk + 1])
                v.tensor_tensor(Sst[rows:rows + DH, chk, :],
                                Sst[rows:rows + DH, chk, :],
                                psC[rows:rows + DH, 0:DH], ALU.add)
                d_up = p.mark(v.tensor_copy(Ssb[rows:rows + DH, chk, :],
                                            Sst[rows:rows + DH, chk, :]), "v")
                prev["St"] = d_up
                prev["psC"] = (d_up,)
            ot3 = ot[:, :].rearrange("p (h f) -> p h f", h=H)
            p.need("v", d_o)
            v.tensor_tensor(scr1[:, :], ot[:, :], ot[:, :], ALU.mult)
            d_ss = p.mark(v.reduce_sum(ssq[:, :].unsqueeze(2),
                                       scr1[:, :].rearrange("p (h f) -> p h f", h=H),
                                       axis=AX.X), "v")
            p.need("a", d_ss)
            a.activation(ssq[:, :], ssq[:, :], AF.Ln, scale=p.constf(1.0 / DH),
                         bias=p.constf(1e-5))
            d_rs = p.mark(a.activation(ssq[:, :], ssq[:, :], AF.Exp,
                                       scale=p.constf(-0.5)), "a")
            p.need("v", d_rs, dg_, dgn)
            v.tensor_tensor(ot3, ot3,
                            ssq[:, :].unsqueeze(2).broadcast_to((128, H, DH)),
                            ALU.mult)
            d_on = p.mark(v.tensor_tensor(
                ot3, ot3, gnwbc[:, 0:DH].unsqueeze(1).broadcast_to((128, H, DH)),
                ALU.mult), "v")
            p.need("a", d_on, dg_)
            d_gs = p.mark(a.activation(scr1[:, :], gt[c][:, :], AF.Silu), "a")
            p.need("v", d_gs, prev["gk"])
            d_og = p.mark(v.tensor_tensor(obt[:, :], ot[:, :], scr1[:, :],
                                          ALU.mult), "v")
            ld_free[c] = (d_og, prev["St"])
            ev_fm = [None, None]
            for k in range(KD):
                tc = k % 2
                p.need("pe", d_og, pools["psT_free"][tc])
                mmt = p.mark(pe.transpose(psT[tc][:, 0:128],
                                          obt[:, k * 128:(k + 1) * 128],
                                          ctx.cst["ident_bf"][:, :]), "pe")
                if k == KD - 1:
                    prev["gk"] = mmt
                ev_fm[tc] = evac(p, "v" if tc == 0 else "a", obfm[:, k, :],
                                 psT[tc][:, 0:128], deps=(mmt, prev["fm"]))
                pools["psT_free"][tc] = ev_fm[tc]
            mmf = None
            for h0 in range(2):
                pairs = [(obfm[:, k, :], wo[:, k, h0 * 512:(h0 + 1) * 512])
                         for k in range(KD)]
                mmf = acc_mm(p, psC[:, h0 * 512:(h0 + 1) * 512], pairs,
                             deps=(dwo, ev_fm[0], ev_fm[1]) + tuple(prev["psC"]))
            prev["fm"] = mmf
            p.need("v", mmf, ob_free[c])
            d_ob = p.mark(v.tensor_copy(ob[c][:, :], psC[:, :]), "v")
            prev["psC"] = (d_ob,)
            ob_free[c] = p.dma(("s", f"o{c}"), brA[qs, :], ob[c][:, :],
                               deps=(d_ob,))


A_STAGES[3] = s_attn3

def s_attn2(ctx: Ctx, l: int):
    p = ctx.p
    v, a, pe = p.eng["v"], p.eng["a"], p.eng["pe"]
    xn_T = ctx.buf("xn2_T", (D, T), BF16)
    xn_f = ctx.buf("xn2_f", (T, D), F32)
    brA = ctx.buf("brA2", (T, D), F32, write=True)
    xcd = ctx.buf("l2a_xc", (2048, T), BF16, write=True)
    zd = ctx.buf("l2a_z", (2048, T), BF16, write=True)
    DI, NS = 2048, 16
    with p.stage("attn2"):
        psA = [p.ps("psA", (128, 512), F32) for _ in range(2)]
        psT = [p.ps("psT", (128, 512), BF16) for _ in range(2)]
        psC = p.ps("psC", (128, 1024), F32)
        pools = {"psA_free": [(), ()], "psT_free": [None, None]}
        subX = contextlib.ExitStack()
        xn2f = p.sb("xn2f", (128, KD, T), BF16, stack=subX)
        subX2 = contextlib.ExitStack()
        nwbc, dnw = bcast_row(ctx, "L2_attn_norm_w", psC, stack=subX2)
        nbbc, dnb = bcast_row(ctx, "L2_attn_norm_b", psC, deps=(dnw,), stack=subX2)
        xt = [p.sb("xt", (128, D), F32, stack=subX2) for _ in range(2)]
        xb_ = p.sb("xb_", (128, D), BF16, stack=subX2)
        st4 = p.sb("st4", (128, 4), F32, stack=subX2)
        sqs = p.sb("sqs", (128, D), F32, stack=subX2)
        xt_free = [None, None]
        cast_prev = None
        stat_prev = None
        for q in range(NT):
            c = q % 2
            qs = slice(q * 128, (q + 1) * 128)
            dxl = p.dma(("s", f"a{c}"), xt[c][:, :], xn_f[qs, :], deps=(xt_free[c],))
            d_st = ln_stats(ctx, xt[c][:, :], st4[:, 0:2], st4[:, 2:4], sqs, 1e-5,
                            deps=(dxl, stat_prev))
            p.need("v", *d_st, dnw, dnb, cast_prev)
            d_n = p.mark(v.tensor_scalar(xt[c][:, :], xt[c][:, :], st4[:, 0:1],
                                         st4[:, 2:3], ALU.subtract, ALU.mult), "v")
            stat_prev = d_n
            v.tensor_tensor(xt[c][:, :], xt[c][:, :], nwbc[:, :], ALU.mult)
            d_n2 = p.mark(v.tensor_tensor(xt[c][:, :], xt[c][:, :], nbbc[:, :],
                                          ALU.add), "v")
            p.need("a", d_n2, cast_prev)
            d_cb = p.mark(a.copy(xb_[:, :], xt[c][:, :]), "a")
            xt_free[c] = d_cb
            for k in range(KD):
                tc = k % 2
                p.need("pe", d_cb, pools["psT_free"][tc])
                mmt = p.mark(pe.transpose(psT[tc][:, 0:128],
                                          xb_[:, k * 128:(k + 1) * 128],
                                          ctx.cst["ident_bf"][:, :]), "pe")
                if k == KD - 1:
                    cast_prev = mmt
                ev = evac(p, "v" if tc == 0 else "a", xn2f[:, k, qs],
                          psT[tc][:, 0:128], deps=(mmt,))
                pools["psT_free"][tc] = ev
        p.bar()
        subX2.close()
        cw = p.sb("cw", (128, 16, 4), F32)
        cb = p.sb("cb", (128, 16), F32)
        dcw = p.dma(("s", "r0"), cw[:, :, :], ctx.W("L2_attn_conv_w")[:, :, :])
        dcb = p.dma(("s", "r1"), cb[:, :], ctx.W("L2_attn_conv_b")[:, :])
        win_ap = ctx.W("L2_attn_W_in").rearrange("(k p) m -> p k m", p=128)
        for half in range(2):
            subW = contextlib.ExitStack()
            win = p.sb("win", (128, KD, 2048), BF16, stack=subW)
            dwi = p.dma(("g", "w0"), win[:, :, :],
                        win_ap[:, :, half * 2048:(half + 1) * 2048])
            xrow = p.sb("xrow", (128, T + 4), F32, stack=subW)
            crow = p.sb("crow", (128, T), F32, stack=subW)
            stg = [p.sb("stg", (128, T), BF16, stack=subW) for _ in range(2)]
            stg_free = [None, None]
            d_z0 = p.mark(v.memset(xrow[:, 0:4], 0.0), "v")
            ev_prev = None
            for m in range(16):
                c = m % 2
                evs = []
                for tsl in range(4):
                    t0 = tsl * 512
                    b = tsl % 2
                    pairs = [(win[:, k, m * 128:(m + 1) * 128],
                              xn2f[:, k, t0:t0 + 512]) for k in range(KD)]
                    mm = acc_mm(p, psA[b][:, :], pairs,
                                deps=(dwi,) + tuple(pools["psA_free"][b]))
                    if half == 0:
                        ev = evac(p, "v", xrow[:, 4 + t0:4 + t0 + 512],
                                  psA[b][:, :], deps=(mm, ev_prev))
                    else:
                        ev = evac(p, "v" if b == 0 else "a",
                                  stg[c][:, t0:t0 + 512], psA[b][:, :],
                                  deps=(mm, stg_free[c]))
                    pools["psA_free"][b] = (ev,)
                    evs.append(ev)
                if half == 0:
                    p.need("v", *(evs + [d_z0, dcw, dcb, ev_prev]))
                    v.tensor_scalar(crow[:, :], xrow[:, 1:1 + T], cw[:, m, 0:1],
                                    cb[:, m:m + 1], ALU.mult, ALU.add)
                    for j in range(1, 4):
                        v.scalar_tensor_tensor(crow[:, :], xrow[:, 1 + j:1 + j + T],
                                               cw[:, m, j:j + 1], crow[:, :],
                                               ALU.mult, ALU.add)
                    d_cv = p.mark(v.tensor_copy(crow[:, :], crow[:, :]), "v") \
                        if False else p.tip("v")
                    p.need("a", d_cv, stg_free[c])
                    d_si = p.mark(a.activation(stg[c][:, :], crow[:, :],
                                               AF.Silu), "a")
                    ev_prev = d_si
                    stg_free[c] = p.dma(("s", f"tc{c}"),
                                        xcd[m * 128:(m + 1) * 128, :],
                                        stg[c][:, :], deps=(d_si,))
                else:
                    stg_free[c] = p.dma(("s", f"tc{c}"),
                                        zd[m * 128:(m + 1) * 128, :],
                                        stg[c][:, :], deps=tuple(evs))
            p.bar()
            subW.close()
        p.bar()
        subX.close()
        TC = 256
        NTC = T // TC
        af = p.sb("af", (128, 16, NS), F32)
        dal = p.dma(("s", "r0"), af[:, :, :], ctx.W("L2_attn_A_log")[:, :, :])
        p.need("a", dal)
        d_af = p.mark(a.activation(af[:, :, :], af[:, :, :], AF.Exp), "a")
        p.need("v", d_af)
        d_af = p.mark(v.tensor_scalar_mul(af[:, :, :], af[:, :, :], -1.0), "v")
        dcol = p.sb("dcol", (128, 16), F32)
        ddc = p.dma(("s", "r1"), dcol[:, :], ctx.W("L2_attn_D")[:, :])
        bdt = p.sb("bdt", (128, 16), F32)
        dbd = p.dma(("s", "r2"), bdt[:, :], ctx.W("L2_attn_b_dt")[:, :])
        wbc_ = p.sb("wbc_", (128, 2, 16, NS), BF16)
        dwb = p.dma(("g", "w0"), wbc_[:, 0, :, :],
                    ctx.W("L2_attn_W_B").rearrange("(k p) m -> p k m", p=128))
        dwc = p.dma(("g", "w1"), wbc_[:, 1, :, :],
                    ctx.W("L2_attn_W_C").rearrange("(k p) m -> p k m", p=128))
        wout, dwo = load_w(ctx, "L2_attn_W_out", D, stream=("g", "w2"))
        hcarry = p.sb("hcarry", (128, 16 * NS), F32)
        d_hc0 = p.mark(v.memset(hcarry[:, :], 0.0), "v")
        xcc = p.sb("xcc", (128, 16, TC), BF16)
        zcc = p.sb("zcc", (128, 16, TC), BF16)
        dtc = p.sb("dtc", (128, 16, TC), BF16)
        abuf = p.sb("abuf", (128, 16, TC), BF16)
        bbuf = p.sb("bbuf", (128, 16, TC), BF16)
        hbuf = p.sb("hbuf", (128, 16, TC), BF16)
        ybuf = p.sb("ybuf", (128, 16, TC), F32)
        ubuf = p.sb("ubuf", (128, 16, TC), BF16)
        ygb = p.sb("ygb", (128, 16, TC), BF16)
        Bt = p.sb("Bt", (128, 2 * NS), F32)
        Bf = p.sb("Bf", (1, 32, TC), BF16)
        stg32 = p.sb("stg32", (32, 128), BF16)
        Bbc = p.sb("Bbc", (128, TC), F32)
        Cbc = p.sb("Cbc", (128, TC), F32)
        wdt_ap = ctx.W("L2_attn_W_dt").rearrange("(k p) m -> p k m", p=128)
        wdtm = [p.sb("wdtm", (128, 16, 128), BF16) for _ in range(2)]
        xnt = [p.sb("xnt", (128, D), F32) for _ in range(2)]
        obx = [p.sb("obx", (128, D), F32) for _ in range(2)]
        scn = p.sb("scn", (128, 3 * TC), F32)
        prev = {"xc": None, "z": None, "dt": None, "ab": None, "h": None,
                "y": None, "Bt": None, "Bf": None, "bc": (), "u": None,
                "yg": None, "ob": [None, None], "xn": [None, None],
                "wd": [None, None], "sc": None, "pcread": None}
        for tci in range(NTC):
            ts0 = tci * TC
            tsl = slice(ts0, ts0 + TC)
            dxc = p.dma(("s", "a0"), xcc[:, :, :],
                        xcd.rearrange("(k p) t -> p k t", p=128)[:, :, tsl],
                        deps=(prev["u"], prev["yg"]))
            for m in range(16):
                c = m % 2
                b = m % 2
                dwm = p.dma(("g", f"wd{c}"), wdtm[c][:, :, :],
                            wdt_ap[:, :, m * 128:(m + 1) * 128],
                            deps=(prev["wd"][c],))
                pairs = [(wdtm[c][:, k, :], xcc[:, k, :]) for k in range(16)]
                mm = acc_mm(p, psA[b][:, 0:TC], pairs,
                            deps=(dwm, dxc) + tuple(pools["psA_free"][b]))
                prev["wd"][c] = mm
                p.need("v", mm, dbd, prev["sc"])
                d_t = p.mark(v.tensor_scalar_add(scn[:, 0:TC], psA[b][:, 0:TC],
                                                 bdt[:, m:m + 1]), "v")
                pools["psA_free"][b] = (d_t,)
                d_sp = softplus(ctx, scn[:, TC:2 * TC], scn[:, 0:TC],
                                scn[:, 2 * TC:3 * TC], deps=(d_t,))
                p.need("v", d_sp)
                d_dt = p.mark(v.tensor_copy(dtc[:, m, :], scn[:, TC:2 * TC]), "v")
                prev["sc"] = d_dt
            for b4 in range(TC // 128):
                t4 = slice(b4 * 128, (b4 + 1) * 128)
                mm = None
                for ci in range(2):
                    pairs = [(xcc[:, k, t4], wbc_[:, ci, k, :]) for k in range(16)]
                    mm = acc_mm(p, psA[ci][:, 0:NS], pairs,
                                deps=(dwb, dwc, dxc) + tuple(pools["psA_free"][ci]))
                    ev = evac(p, "v" if ci == 0 else "a",
                              Bt[:, ci * NS:(ci + 1) * NS], psA[ci][:, 0:NS],
                              deps=(mm, prev["Bt"]))
                    pools["psA_free"][ci] = (ev,)
                p.need("v", p.tip("a"))
                d_bt = p.mark(v.tensor_copy(scn.bitcast(BF16)[:, 0:32],
                                            Bt[:, :]), "v")
                p.need("pe", d_bt, pools["psT_free"][0])
                mmt = p.mark(pe.transpose(psT[0][0:32, 0:128],
                                          scn.bitcast(BF16)[:, 0:32],
                                          ctx.cst["ident_bf"][:, :]), "pe")
                prev["Bt"] = mmt
                ev = evac(p, "a", stg32[:, :], psT[0][0:32, 0:128],
                          deps=(mmt, prev["Bf"]))
                pools["psT_free"][0] = ev
                prev["Bf"] = p.dma(("s", "bf"), Bf[0:1, :, t4], stg32[:, :],
                                   deps=(ev,))
            p.need("v", d_hc0)
            dum = None
            p.need("v", dxc, prev["yg"])
            d_u = p.mark(v.tensor_tensor(ubuf[:, :, :], xcc[:, :, :],
                                         dtc[:, :, :], ALU.mult), "v")
            prev["u"] = d_u
            d_y0 = p.mark(v.memset(ybuf[:, :, :], 0.0), "v")
            for n in range(NS):
                d_ab = None
                for k in range(16):
                    p.need("a", prev["ab"], prev["h"],
                           p.tip("v") if k == 0 and n == 0 else None)
                    d_ab = p.mark(a.activation(abuf[:, k, :], dtc[:, k, :], AF.Exp,
                                               scale=af[:, k, n:n + 1]), "a")
                prev["ab"] = d_ab
                mmb = acc_mm(p, psC[:, 0:TC],
                             [(ctx.cst["ones_row_bf"][0:1, :],
                               Bf[0:1, n, :])],
                             deps=(prev["Bf"], prev["bc"]) + tuple(
                                 d for d in prev["ob"] if d is not None))
                mmc = acc_mm(p, psC[:, 512:512 + TC],
                             [(ctx.cst["ones_row_bf"][0:1, :],
                               Bf[0:1, NS + n, :])],
                             deps=(prev["Bf"],) + tuple(
                                 d for d in prev["ob"] if d is not None))
                p.need("v", mmb, mmc)
                d_bb = p.mark(v.tensor_copy(Bbc[:, :], psC[:, 0:TC]), "v")
                d_cc = p.mark(v.tensor_copy(Cbc[:, :], psC[:, 512:512 + TC]), "v")
                prev["bc"] = (d_bb, d_cc)
                p.need("v", d_u, d_bb)
                d_b = p.mark(v.tensor_tensor(
                    bbuf[:, :, :], ubuf[:, :, :],
                    Bbc[:, :].unsqueeze(1).broadcast_to((128, 16, TC)),
                    ALU.mult), "v")
                d_h = None
                p.need("v", d_ab, d_b, prev["h"])
                for k in range(16):
                    d_h = p.mark(v.tensor_tensor_scan(
                        hbuf[:, k, :], abuf[:, k, :], bbuf[:, k, :],
                        hcarry[:, k * NS + n:k * NS + n + 1],
                        ALU.mult, ALU.add), "v")
                p.need("v", d_h)
                for k in range(16):
                    v.tensor_copy(hcarry[:, k * NS + n:k * NS + n + 1],
                                  hbuf[:, k, TC - 1:TC])
                d_hm = p.mark(v.tensor_tensor(
                    hbuf[:, :, :], hbuf[:, :, :],
                    Cbc[:, :].unsqueeze(1).broadcast_to((128, 16, TC)),
                    ALU.mult), "v")
                p.need("v", d_hm, d_y0)
                d_y = p.mark(v.tensor_tensor(ybuf[:, :, :], ybuf[:, :, :],
                                             hbuf[:, :, :], ALU.add), "v")
                prev["h"] = d_y
            dzc = p.dma(("s", "b0"), zcc[:, :, :],
                        zd.rearrange("(k p) t -> p k t", p=128)[:, :, tsl],
                        deps=(prev["z"],))
            p.need("v", ddc)
            d_g = None
            for k in range(16):
                d_g = p.mark(v.scalar_tensor_tensor(ybuf[:, k, :], xcc[:, k, :],
                                                    dcol[:, k:k + 1], ybuf[:, k, :],
                                                    ALU.mult, ALU.add), "v")
            p.need("a", dzc, prev["yg"])
            d_sz = p.mark(a.activation(ygb[:, :, :], zcc[:, :, :], AF.Silu), "a")
            prev["z"] = d_sz
            p.need("v", d_g, d_sz)
            d_yg = p.mark(v.tensor_tensor(ygb[:, :, :], ybuf[:, :, :],
                                          ygb[:, :, :], ALU.mult), "v")
            prev["yg"] = d_yg
            for b4 in range(TC // 128):
                c = b4 % 2
                q = tci * (TC // 128) + b4
                qs = slice(q * 128, (q + 1) * 128)
                t4 = slice(b4 * 128, (b4 + 1) * 128)
                dxn_ = p.dma(("s", f"c{c}"), xnt[c][:, :], xn_f[qs, :],
                             deps=(prev["xn"][c],))
                mmf = None
                for h0 in range(2):
                    pairs = [(ygb[:, k2, t4], wout[:, k2, h0 * 512:(h0 + 1) * 512])
                             for k2 in range(16)]
                    mmf = acc_mm(p, psC[:, h0 * 512:(h0 + 1) * 512], pairs,
                                 deps=(dwo, d_yg, prev["bc"], prev["pcread"]) +
                                      (() if prev["ob"][c] is None
                                       else (prev["ob"][c],)))
                prev["bc"] = ()
                p.need("v", mmf, dxn_, prev["ob"][c])
                v.tensor_tensor(obx[c][:, 0:512], psC[:, 0:512],
                                xnt[c][:, 0:512], ALU.add)
                d_ob = p.mark(v.tensor_tensor(obx[c][:, 512:1024], psC[:, 512:1024],
                                              xnt[c][:, 512:1024], ALU.add), "v")
                prev["xn"][c] = d_ob
                prev["pcread"] = d_ob
                prev["ob"][c] = p.dma(("s", f"o{c}"), brA[qs, :], obx[c][:, :],
                                      deps=(d_ob,))


A_STAGES[2] = s_attn2






def s_attn0(ctx: Ctx, l: int):
    p = ctx.p
    v, a, pe = p.eng["v"], p.eng["a"], p.eng["pe"]
    xn_T = ctx.buf("xn0_T", (D, T), BF16)
    brA = ctx.buf("brA0", (T, D), F32, write=True)
    vfd = ctx.buf("l0a_vf", (T, D), F32, write=True)
    SC = 1.0 / 32.0
    with p.stage("attn0"):
        psA = [p.ps("psA", (128, 512), F32) for _ in range(2)]
        psB = p.ps("psB", (128, 1024), F32)
        psC = p.ps("psC", (128, 1024), F32)
        psT = [p.ps("psT", (128, 128), BF16) for _ in range(2)]
        qf = p.sb("qf", (128, KD, T), BF16)
        kf = p.sb("kf", (128, KD, T), BF16)
        vb = p.sb("vb", (128, NT, D), BF16)
        vtm = p.sb("vtm", (128, D), F32)
        subA = contextlib.ExitStack()
        wq, dwq = load_w(ctx, "L0_attn_Wq", D, stack=subA)
        wk, dwk = load_w(ctx, "L0_attn_Wk", D, stack=subA)
        wv, dwv = load_w(ctx, "L0_attn_Wv", D, stack=subA)
        xnf = p.sb("xnf", (128, KD, T), BF16, stack=subA)
        dxn = p.dma(("s", "ax"), xnf[:, :, :],
                    xn_T.rearrange("(k p) t -> p k t", p=128))
        psA_free = [(), ()]
        vf_saved = None
        vb_done = None
        for q in range(DBG.get("a0_v", NT)):
            pairs = [[], []]
            for h in range(2):
                for k in range(KD):
                    pairs[h].append((xnf[:, k, q * 128:(q + 1) * 128],
                                     wv[:, k, h * 512:(h + 1) * 512]))
            mms = []
            for h in range(2):
                mms.append(acc_mm(p, psA[h][:, :], pairs[h],
                                  deps=(dxn, dwv) + tuple(psA_free[h])))
            p.need("v", *mms, vf_saved)
            v.tensor_copy(vb[:, q, 0:512], psA[0][:, :])
            dvb = p.mark(v.tensor_copy(vtm[:, 0:512], psA[0][:, :]), "v")
            p.need("a", *mms)
            p.need("a", vf_saved)
            a.copy(vb[:, q, 512:1024], psA[1][:, :])
            dvt = p.mark(a.copy(vtm[:, 512:1024], psA[1][:, :]), "a")
            vf_saved = p.dma(("s", "vf"), vfd[q * 128:(q + 1) * 128, :], vtm[:, :],
                             deps=(dvb, dvt))
            psA_free = [(dvb,), (dvt,)]
            vb_done = (dvb, dvt)
        last_ev = {"q": None, "k": None}
        for (buf, wt, dw, nm) in ((qf, wq, dwq, "q"), (kf, wk, dwk, "k"))[:DBG.get("a0_qk", 2)]:
            for m in range(KD):
                for tsl in range(4):
                    t0 = tsl * 512
                    h = tsl % 2
                    pairs = [(wt[:, k, m * 128:(m + 1) * 128], xnf[:, k, t0:t0 + 512])
                             for k in range(KD)]
                    mm = acc_mm(p, psA[h][:, :], pairs,
                                deps=(dw, dxn) + tuple(psA_free[h]))
                    p.need("v", mm)
                    ev = p.mark(v.tensor_copy(buf[:, m, t0:t0 + 512], psA[h][:, :]), "v")
                    psA_free[h] = (ev,)
                    last_ev[nm] = ev
        p.bar()
        subA.close()
        subB = contextlib.ExitStack()
        if DBG.get("a0_skipB"):
            p.bar()
            subB.close()
            return
        wfc, dwfc = load_w(ctx, "L0_attn_Wfc", D, stack=subB)
        gm, dgm = bcast_row(ctx, "L0_attn_gamma", psC, stack=subB)
        bt, dbt = bcast_row(ctx, "L0_attn_beta", psC, deps=(dgm,), stack=subB)
        bf, dbf = bcast_row(ctx, "L0_attn_bfc", psC, deps=(dbt,), stack=subB)
        S = p.sb("S", (128, T), F32)
        P = p.sb("P", (128, T), BF16)
        PT = p.sb("PT", (128, T), BF16)
        o1 = p.sb("o1", (128, D), F32)
        lnf = p.sb("lnf", (128, D), F32)
        lnb = p.sb("lnb", (128, D), BF16)
        lnfm = p.sb("lnfm", (128, KD, 128), BF16)
        vft = [p.sb("vft", (128, D), F32) for _ in range(2)]
        ob = [p.sb("ob", (128, D), F32) for _ in range(2)]
        stt_ = p.sb("st2", (128, 4), F32)
        sq2 = p.sb("sq2", (128, D), F32)
        rden = p.sb("rden", (128, 2), F32)
        vft_free = [None, None]
        ob_free = [None, None]
        exp_prev = None
        pt_mm_prev = None
        pv_prev = None
        psB_free = None
        psC_free = dbf
        psT_free = [None, None]
        o1_sq_prev = None
        lnfm_mm_prev = None
        prev_nf = None
        for q in range(DBG.get("a0_q", NT)):
            c = q % 2
            kw = (q + 1) * 128
            s = slice(q * 128, (q + 1) * 128)
            dvf = p.dma(("s", f"a{c}"), vft[c][:, :], vfd[s, :],
                        deps=(vf_saved, vft_free[c]))
            ev_S = None
            for g0 in range(0, kw, 512):
                w_ = min(512, kw - g0)
                h = (g0 // 512) % 2
                pairs = [(qf[:, k, s], kf[:, k, g0:g0 + w_]) for k in range(KD)]
                mm = acc_mm(p, psA[h][:, 0:w_], pairs,
                            deps=(last_ev["q"], last_ev["k"]) + tuple(psA_free[h]))
                p.need("v", mm, exp_prev)
                ev_S = p.mark(v.tensor_copy(S[:, g0:g0 + w_], psA[h][:, 0:w_]), "v")
                psA_free[h] = (ev_S,)
            d_msk = p.mark(v.tensor_tensor(S[:, s], S[:, s], ctx.cst["causal_add"][:, :],
                                           ALU.add), "v")
            d_rm = p.mark(v.reduce_max(stt_[:, 0:1], S[:, 0:kw], axis=AX.X, negate=True), "v")
            p.need("v", d_rm)
            d_nm = p.mark(v.tensor_scalar_mul(stt_[:, 1:2], stt_[:, 0:1], SC), "v")
            p.need("a", d_nm, pt_mm_prev)
            d_exp = p.mark(a.activation(P[:, 0:kw], S[:, 0:kw], AF.Exp,
                                        bias=stt_[:, 1:2], scale=p.constf(SC),
                                        accum_out=rden[:, 1:2]), "a")
            exp_prev = d_exp
            p.need("v", d_exp)
            d_rd = p.mark(v.reciprocal(rden[:, 0:1], rden[:, 1:2]), "v")
            ev_pt = None
            for kt in range(q + 1):
                tc = kt % 2
                p.need("pe", d_exp, psT_free[tc], pv_prev)
                mm = p.mark(pe.transpose(psT[tc][:, 0:128], P[:, kt * 128:(kt + 1) * 128],
                                         ctx.cst["ident_bf"][:, :]), "pe")
                pt_mm_prev = mm
                p.need("a", mm)
                ev_pt = p.mark(a.copy(PT[:, kt * 128:(kt + 1) * 128], psT[tc][:, 0:128]), "a")
                psT_free[tc] = ev_pt
            for h in range(2):
                pairs = [(PT[:, kt * 128:(kt + 1) * 128], vb[:, kt, h * 512:(h + 1) * 512])
                         for kt in range(q + 1)]
                mm = acc_mm(p, psB[:, h * 512:(h + 1) * 512], pairs,
                            deps=(ev_pt, psB_free) + tuple(vb_done or ()))
                pv_prev = mm
            p.need("v", pv_prev, d_rd, dvf, o1_sq_prev)
            d_o1 = p.mark(v.scalar_tensor_tensor(o1[:, :], psB[:, :], rden[:, 0:1],
                                                 vft[c][:, :], ALU.mult, ALU.add), "v")
            psB_free = d_o1
            vft_free[c] = d_o1
            d_st = ln_stats(ctx, o1[:, :], stt_[:, 0:2], stt_[:, 2:4], sq2, 1e-5,
                            deps=(d_o1, prev_nf))
            o1_sq_prev = d_st[0]
            p.need("v", *d_st)
            d_nf = p.mark(v.tensor_scalar(lnf[:, :], o1[:, :], stt_[:, 0:1], stt_[:, 2:3],
                                          ALU.subtract, ALU.mult), "v")
            prev_nf = d_nf
            p.need("v", dgm, dbt)
            v.tensor_tensor(lnf[:, :], lnf[:, :], gm[:, :], ALU.mult)
            d_ln = p.mark(v.tensor_tensor(lnf[:, :], lnf[:, :], bt[:, :], ALU.add), "v")
            p.need("a", d_ln, lnfm_mm_prev)
            d_lb = p.mark(a.copy(lnb[:, :], lnf[:, :]), "a")
            ev_fm = None
            for k in range(KD):
                tc = k % 2
                p.need("pe", d_lb, psT_free[tc])
                mm = p.mark(pe.transpose(psT[tc][:, 0:128], lnb[:, k * 128:(k + 1) * 128],
                                         ctx.cst["ident_bf"][:, :]), "pe")
                p.need("a", mm)
                ev_fm = p.mark(a.copy(lnfm[:, k, :], psT[tc][:, 0:128]), "a")
                psT_free[tc] = ev_fm
            mm_f = None
            for h in range(2):
                pairs = [(lnfm[:, k, :], wfc[:, k, h * 512:(h + 1) * 512])
                         for k in range(KD)]
                mm_f = acc_mm(p, psC[:, h * 512:(h + 1) * 512], pairs,
                              deps=(ev_fm, dwfc, psC_free))
            lnfm_mm_prev = mm_f
            p.need("v", mm_f, dbf, ob_free[c])
            d_ob = p.mark(v.tensor_tensor(ob[c][:, :], psC[:, :], bf[:, :], ALU.add), "v")
            psC_free = d_ob
            ob_free[c] = p.dma(("s", f"o{c}"), brA[s, :], ob[c][:, :], deps=(d_ob,))
        p.bar()
        subB.close()


A_STAGES[0] = s_attn0


def swiglu_fm(ctx, h_out, wg, wu, xf, deps, psA, psA_free, state,
              bias_g=None, bias_u=None, act=AF.Silu, tlen=T):
    p = ctx.p
    v, a = p.eng["v"], p.eng["a"]
    nm = h_out.shape[1]
    nk = wg.shape[1]
    if "sg" not in state:
        state["sg"] = p.sb("sg", (128, 512), F32)
        state["sg_prev"] = None
    sg = state["sg"]
    last = None
    for m in range(nm):
        for t0 in range(0, tlen, 512):
            w_ = min(512, tlen - t0)
            pairs_g = [(wg[:, k, m * 128:(m + 1) * 128], xf[:, k, t0:t0 + w_])
                       for k in range(nk)]
            mm_g = acc_mm(p, psA[0][:, 0:w_], pairs_g,
                          deps=tuple(deps) + tuple(psA_free[0]))
            p.need("a", mm_g, state["sg_prev"])
            kw = {} if bias_g is None else {"bias": bias_g[:, m:m + 1]}
            d_sg = p.mark(a.activation(sg[:, 0:w_], psA[0][:, 0:w_], act, **kw), "a")
            psA_free[0] = (d_sg,)
            pairs_u = [(wu[:, k, m * 128:(m + 1) * 128], xf[:, k, t0:t0 + w_])
                       for k in range(nk)]
            mm_u = acc_mm(p, psA[1][:, 0:w_], pairs_u,
                          deps=tuple(deps) + tuple(psA_free[1]))
            p.need("v", mm_u, d_sg)
            if bias_u is None:
                d_h = p.mark(v.tensor_tensor(h_out[:, m, t0:t0 + w_], sg[:, 0:w_],
                                             psA[1][:, 0:w_], ALU.mult), "v")
            else:
                d_h = p.mark(v.scalar_tensor_tensor(
                    h_out[:, m, t0:t0 + w_], psA[1][:, 0:w_], bias_u[:, m:m + 1],
                    sg[:, 0:w_], ALU.add, ALU.mult), "v")
            psA_free[1] = (d_h,)
            state["sg_prev"] = d_h
            last = d_h
    return last


def rms_tile(ctx, x_ap, wbc, st4, scr, eps, deps=(), n=D):
    p = ctx.p
    v, a = p.eng["v"], p.eng["a"]
    p.need("a", *deps)
    d_sq = p.mark(a.activation(scr[:, 0:n], x_ap, AF.Square,
                               accum_out=st4[:, 0:1]), "a")
    a.activation(st4[:, 1:2], st4[:, 0:1], AF.Ln, bias=p.constf(eps * n))
    d_r = p.mark(a.activation(st4[:, 2:3], st4[:, 1:2], AF.Exp,
                              scale=p.constf(-0.5)), "a")
    p.need("v", d_r, *deps)
    d = p.mark(v.tensor_scalar(x_ap, x_ap, st4[:, 2:3], float(np.sqrt(n)),
                               ALU.mult, ALU.mult), "v")
    if wbc is not None:
        d = p.mark(v.tensor_tensor(x_ap, x_ap, wbc[:, 0:n], ALU.mult), "v")
    return d


def rope_interleaved_tm(ctx, out_bf, xin, cosb, sinb, scrA, scrB, H, deps=()):
    p = ctx.p
    v = p.eng["v"]
    n = H * 32
    xr = xin.rearrange("p (h f two) -> p h f two", h=H, two=2)
    orr = out_bf.rearrange("p (h f two) -> p h f two", h=H, two=2)
    x1 = xr[:, :, :, 0]
    x2 = xr[:, :, :, 1]
    cb = cosb.unsqueeze(1).broadcast_to((128, H, 32))
    sb_ = sinb.unsqueeze(1).broadcast_to((128, H, 32))
    sA = scrA[:, 0:n].rearrange("p (h f) -> p h f", h=H)
    sB = scrB[:, 0:n].rearrange("p (h f) -> p h f", h=H)
    p.need("v", *deps)
    v.tensor_tensor(sA, x1, cb, ALU.mult)
    v.tensor_tensor(sB, x2, sb_, ALU.mult)
    d1 = p.mark(v.tensor_tensor(orr[:, :, :, 0], sA, sB, ALU.subtract), "v")
    v.tensor_tensor(sA, x1, sb_, ALU.mult)
    v.tensor_tensor(sB, x2, cb, ALU.mult)
    d2 = p.mark(v.tensor_tensor(orr[:, :, :, 1], sA, sB, ALU.add), "v")
    return d2


def s_ffn0(ctx: Ctx, l: int):
    p = ctx.p
    v, a, pe = p.eng["v"], p.eng["a"], p.eng["pe"]
    xn_T = ctx.buf("xn0_T", (D, T), BF16)
    xn_f = ctx.buf("xn0_f", (T, D), F32)
    brB = ctx.buf("brB0", (T, D), F32, write=True)
    x1d = ctx.buf("l0f_x1", (T, D), F32, write=True)
    accd = ctx.buf("l0f_acc", (T, D), F32, write=True)
    with p.stage("ffn0"):
        pools = mha_pools(p)
        psC = p.ps("psC", (128, 1024), F32)
        x1f = p.sb("x1f", (128, KD, T), BF16)
        rope_c = p.sb("rope_c", (128, NT, 32), F32)
        rope_s = p.sb("rope_s", (128, NT, 32), F32)
        drc = p.dma(("s", "r0"), rope_c[:, :, :],
                    ctx.W("cst_rope0_cos").rearrange("(q p) f -> p q f", p=128))
        drs = p.dma(("s", "r1"), rope_s[:, :, :],
                    ctx.W("cst_rope0_sin").rearrange("(q p) f -> p q f", p=128))
        qTd = ctx.buf("l0f_qT", (D, T), BF16, write=True)
        subQ = contextlib.ExitStack()
        kf = p.sb("kf", (128, KD, T), BF16, stack=subQ)
        vb = p.sb("vb", (128, NT, D), BF16, stack=subQ)
        subQ2 = contextlib.ExitStack()
        kcf = p.sb("kcf", (128, 2, T), BF16, stack=subQ2)
        vcf = p.sb("vcf", (128, 2, T), BF16, stack=subQ2)
        subX = contextlib.ExitStack()
        xnf = p.sb("xnf", (128, KD, T), BF16, stack=subX)
        dxn = p.dma(("s", "ax"), xnf[:, :, :],
                    xn_T.rearrange("(k p) t -> p k t", p=128))
        rope_c = p.sb("rope_c", (128, NT, 32), F32, stack=subX)
        rope_s = p.sb("rope_s", (128, NT, 32), F32, stack=subX)
        drc = p.dma(("s", "r0"), rope_c[:, :, :],
                    ctx.W("cst_rope0_cos").rearrange("(q p) f -> p q f", p=128))
        drs = p.dma(("s", "r1"), rope_s[:, :, :],
                    ctx.W("cst_rope0_sin").rearrange("(q p) f -> p q f", p=128))
        qtm = p.sb("qtm", (128, D), F32, stack=subX)
        qbt = p.sb("qbt", (128, D), BF16, stack=subX)
        scrA = p.sb("scrA", (128, 512), F32, stack=subX)
        scrB = p.sb("scrB", (128, 512), F32, stack=subX)
        tbq = [p.sb("tbq", (128, KD, 128), BF16, stack=subX) for _ in range(2)]
        qT3 = qTd.rearrange("(k p) t -> p k t", p=128)
        subW = contextlib.ExitStack()
        wkc, dwkc = load_w(ctx, "L0_ffn_Wk_c", 256, stream=("g", "w1"), stack=subW)
        wvc, dwvc = load_w(ctx, "L0_ffn_Wv_c", 256, stream=("g", "w2"), stack=subW)
        d_kc = proj_fm(ctx, kcf, wkc, 0, xnf, (dwkc, dxn), pools["psA"],
                       pools["psA_free"])
        d_vc = proj_fm(ctx, vcf, wvc, 0, xnf, (dwvc, dxn), pools["psA"],
                       pools["psA_free"])
        p.bar()
        subW.close()
        subW = contextlib.ExitStack()
        wv_, dwv = load_w(ctx, "L0_ffn_Wv", D, stream=("g", "w2"), stack=subW)
        d_vb = proj_tm(ctx, vb, wv_, 0, D, vcf, (dwv,) + d_vc, pools["psA"],
                       pools["psA_free"])
        p.bar()
        subW.close()
        for pass_i in range(2):
            subW = contextlib.ExitStack()
            if pass_i == 0:
                wt, dwt = load_w(ctx, "L0_ffn_Wq", D, stream=("g", "w0"), stack=subW)
                srcf, nk = xnf, KD
            else:
                wt, dwt = load_w(ctx, "L0_ffn_Wk", D, stream=("g", "w1"), stack=subW)
                srcf, nk = kcf, 2
            rp_prev = None
            q_prev = None
            tb_free = [None, None]
            for q in range(NT):
                c = q % 2
                qs = slice(q * 128, (q + 1) * 128)
                mmq = None
                for h0 in range(2):
                    pairs = [(srcf[:, k, qs], wt[:, k, h0 * 512:(h0 + 1) * 512])
                             for k in range(nk)]
                    mmq = acc_mm(p, psC[:, h0 * 512:(h0 + 1) * 512], pairs,
                                 deps=(dwt, dxn, q_prev) + d_kc)
                p.need("v", mmq, drc, drs)
                d_q = p.mark(v.tensor_copy(qtm[:, :], psC[:, :]), "v")
                q_prev = d_q
                d_rp = rope_interleaved_tm(ctx, qbt, qtm[:, :], rope_c[:, q, :],
                                           rope_s[:, q, :], scrA, scrB, 16,
                                           deps=(d_q, rp_prev))
                last_cp = None
                for k in range(KD):
                    tc = k % 2
                    p.need("pe", d_rp, pools["psT_free"][tc])
                    mmt = p.mark(pe.transpose(pools["psT"][tc][:, 0:128],
                                              qbt[:, k * 128:(k + 1) * 128],
                                              ctx.cst["ident_bf"][:, :]), "pe")
                    if k == KD - 1:
                        rp_prev = mmt
                    if pass_i == 0:
                        ev = evac(p, "a", tbq[c][:, k, :], pools["psT"][tc][:, 0:128],
                                  deps=(mmt, tb_free[c]))
                        last_cp = ev
                    else:
                        ev = evac(p, "v" if tc == 0 else "a", kf[:, k, qs],
                                  pools["psT"][tc][:, 0:128], deps=(mmt,))
                    pools["psT_free"][tc] = ev
                if pass_i == 0:
                    tb_free[c] = p.dma(("s", f"tc{c}"), qT3[:, :, qs],
                                       tbq[c][:, :, :], deps=(last_cp,))
            p.bar()
            subW.close()
        p.bar()
        subX.close()
        subQ2.close()
        subB = contextlib.ExitStack()
        wo, dwo = load_w(ctx, "L0_ffn_Wo", D, stream=("g", "w0"), stack=subB)
        n1bc, dn1 = bcast_row(ctx, "L0_ffn_norm1_w", psC, stack=subB)
        ob = [p.sb("ob", (128, D), F32, stack=subB) for _ in range(2)]
        xnt = [p.sb("xnt", (128, D), F32, stack=subB) for _ in range(2)]
        otb = p.sb("otb", (128, D), BF16, stack=subB)
        x1b = p.sb("x1b", (128, D), BF16, stack=subB)
        lnfm = p.sb("lnfm", (128, KD, 128), BF16, stack=subB)
        st4 = p.sb("st4", (128, 4), F32, stack=subB)
        sqs = p.sb("sqs", (128, D), F32, stack=subB)
        state = {"psC_free": dn1, "fm_mm_prev": None, "cp_prev": None,
                 "ob_free": [None, None]}
        xnt_free = [None, None]
        x1t_prev = [None]

        def x1_hook(q, c, d_ob):
            s = slice(q * 128, (q + 1) * 128)
            dxl = p.dma(("s", f"b{c}"), xnt[c][:, :], xn_f[s, :],
                        deps=(xnt_free[c],))
            p.need("v", d_ob, dxl)
            d1 = p.mark(v.tensor_tensor(ob[c][:, :], ob[c][:, :], xnt[c][:, :],
                                        ALU.add), "v")
            xnt_free[c] = d1
            d_x1 = rms_tile(ctx, ob[c][:, :], n1bc, st4, sqs, 1e-6,
                            deps=(d1, x1t_prev[0], dn1))
            x1t_prev[0] = d_x1
            p.need("a", d_x1)
            d_cb = p.mark(a.copy(x1b[:, :], ob[c][:, :]), "a")
            for k in range(KD):
                tc = k % 2
                p.need("pe", d_cb, pools["psT_free"][tc])
                mmt = p.mark(pe.transpose(pools["psT"][tc][:, 0:128],
                                          x1b[:, k * 128:(k + 1) * 128],
                                          ctx.cst["ident_bf"][:, :]), "pe")
                ev = evac(p, "v" if tc == 0 else "a", x1f[:, k, s],
                          pools["psT"][tc][:, 0:128], deps=(mmt,))
                pools["psT_free"][tc] = ev
            return (d_x1, d_cb)

        def emit_out(q, c, ot, d_o):
            return tm_to_fm_out(ctx, pools, wo, dwo, None, dn1, psC, otb, lnfm,
                                ob, x1d, state, q, c, ot, d_o,
                                extra_add=x1_hook)

        subM = contextlib.ExitStack()
        mha_core(ctx, None, kf, vb, pools, H=16, dk=64, dv=64, causal=False,
                 scale=0.125, in_deps=(p.tip("v"), p.tip("a")),
                 emit_out=emit_out, q_dram=qTd, arena=subM)
        p.bar()
        subM.close()
        subB.close()
        subQ.close()
        wsi_ap = ctx.W("L0_ffn_W_shared_in").rearrange("(k p) m -> p k m", p=128)
        w1s_ap = ctx.W("L0_ffn_W1_shared").rearrange("(k p) m -> p k m", p=128)
        sw_state = {}
        subH = contextlib.ExitStack()
        hbuf = p.sb("hbuf", (128, 16, T), BF16, stack=subH)
        ob2 = [p.sb("ob2", (128, D), F32, stack=subH) for _ in range(2)]
        ob2_free = [None, None]
        for phase in range(2):
            subW = contextlib.ExitStack()
            wg = p.sb("wg", (128, KD, 2048), BF16, stack=subW)
            wu = p.sb("wu", (128, KD, 2048), BF16, stack=subW)
            src_ap = wsi_ap if phase == 0 else w1s_ap
            dwg = p.dma(("g", "w1"), wg[:, :, :], src_ap[:, :, 0:2048])
            dwu = p.dma(("g", "w2"), wu[:, :, :], src_ap[:, :, 2048:4096])
            swiglu_fm(ctx, hbuf, wg, wu, x1f, (dwg, dwu, p.tip("v"), p.tip("a")),
                      pools["psA"], pools["psA_free"], sw_state)
            p.bar()
            subW.close()
            if phase == 0:
                subW2 = contextlib.ExitStack()
                wso, dwso = load_w(ctx, "L0_ffn_W_shared_out", D,
                                   stream=("g", "w1"), stack=subW2)
                psB_free = None
                for q in range(NT):
                    c = q % 2
                    qs = slice(q * 128, (q + 1) * 128)
                    mmd = None
                    for h0 in range(2):
                        pairs = [(hbuf[:, k2, qs],
                                  wso[:, k2, h0 * 512:(h0 + 1) * 512])
                                 for k2 in range(16)]
                        mmd = acc_mm(p, psC[:, h0 * 512:(h0 + 1) * 512], pairs,
                                     deps=(dwso, psB_free, p.tip("v")))
                    p.need("v", mmd, ob2_free[c])
                    d_ob = p.mark(v.tensor_copy(ob2[c][:, :], psC[:, :]), "v")
                    psB_free = d_ob
                    ob2_free[c] = p.dma(("s", f"o{c}"), accd[qs, :], ob2[c][:, :],
                                        deps=(d_ob,))
                p.bar()
                subW2.close()
        subR = contextlib.ExitStack()
        wgg = p.sb("wgg", (128, KD, 2), BF16, stack=subR)
        weg = p.sb("weg", (128, KD, 8), BF16, stack=subR)
        dgg = p.dma(("g", "w1"), wgg[:, :, :],
                    ctx.W("L0_ffn_Wg_gate").rearrange("(k p) m -> p k m", p=128))
        deg = p.dma(("g", "w2"), weg[:, :, :],
                    ctx.W("L0_ffn_We_gate").rearrange("(k p) m -> p k m", p=128))
        gbias, dgb = bcast_row(ctx, "L0_ffn_group_bias", psC, stack=subR)
        ebias, deb = bcast_row(ctx, "L0_ffn_expert_bias", psC, deps=(dgb,), stack=subR)
        egrp, dge = bcast_row(ctx, "cst_egroup", psC, deps=(deb,), stack=subR)
        cmb = p.sb("cmb", (128, NT, 8), F32)
        lg = p.sb("lg", (128, 16), F32)
        ev_prev = dge
        for q in range(NT):
            qs = slice(q * 128, (q + 1) * 128)
            pairs = [(x1f[:, k, qs], wgg[:, k, :]) for k in range(KD)]
            mm1 = acc_mm(p, psC[:, 0:2], pairs, deps=(dgg, ev_prev, p.tip("a")))
            pairs = [(x1f[:, k, qs], weg[:, k, :]) for k in range(KD)]
            mm2 = acc_mm(p, psC[:, 512:520], pairs, deps=(deg,))
            p.need("v", mm1, mm2, dgb, deb, dge)
            v.tensor_tensor(lg[:, 0:2], psC[:, 0:2], gbias[:, 0:2], ALU.add)
            d_lg = p.mark(v.tensor_tensor(lg[:, 2:10], psC[:, 512:520],
                                          ebias[:, 0:8], ALU.add), "v")
            ev_prev = d_lg
            d_gm = p.mark(v.reduce_max(lg[:, 10:11], lg[:, 0:2], axis=AX.X,
                                       negate=True), "v")
            p.need("a", d_gm)
            d_eg = p.mark(a.activation(lg[:, 12:14], lg[:, 0:2], AF.Exp,
                                       bias=lg[:, 10:11],
                                       accum_out=lg[:, 11:12]), "a")
            p.need("v", d_eg)
            v.reciprocal(lg[:, 14:15], lg[:, 11:12])
            v.reduce_max(lg[:, 15:16], lg[:, 12:14], axis=AX.X)
            v.tensor_tensor(lg[:, 10:11], lg[:, 15:16], lg[:, 14:15], ALU.mult)
            v.tensor_scalar(lg[:, 11:12], lg[:, 1:2], lg[:, 0:1], None, ALU.is_gt)
            v.tensor_scalar(cmb[:, q, :], egrp[:, 0:8], lg[:, 11:12], None,
                            ALU.is_equal)
            v.tensor_scalar(cmb[:, q, :], cmb[:, q, :], 1.0e30, -1.0e30,
                            ALU.mult, ALU.add)
            v.tensor_tensor(lg[:, 2:10], lg[:, 2:10], cmb[:, q, :], ALU.add)
            d_m8 = p.mark(v.reduce_max(lg[:, 12:13], lg[:, 2:10], axis=AX.X,
                                       negate=True), "v")
            p.need("a", d_m8)
            d_e8 = p.mark(a.activation(lg[:, 2:10], lg[:, 2:10], AF.Exp,
                                       bias=lg[:, 12:13],
                                       accum_out=lg[:, 13:14]), "a")
            p.need("v", d_e8)
            v.reciprocal(lg[:, 14:15], lg[:, 13:14])
            v.tensor_scalar(lg[:, 2:10], lg[:, 2:10], lg[:, 14:15], lg[:, 10:11],
                            ALU.mult, ALU.mult)
            v.reduce_max(lg[:, 12:13], lg[:, 2:10], axis=AX.X)
            v.tensor_scalar(cmb[:, q, :], lg[:, 2:10], lg[:, 12:13], None,
                            ALU.is_ge)
            v.scalar_tensor_tensor(lg[:, 2:10], cmb[:, q, :], -1.0e30, lg[:, 2:10],
                                   ALU.mult, ALU.add)
            v.reduce_max(lg[:, 13:14], lg[:, 2:10], axis=AX.X)
            v.tensor_scalar(lg[:, 2:10], lg[:, 2:10], lg[:, 13:14], None,
                            ALU.is_ge)
            v.tensor_scalar_mul(lg[:, 2:10], lg[:, 2:10], lg[:, 13:14])
            p.mark(v.scalar_tensor_tensor(cmb[:, q, :], cmb[:, q, :],
                                          lg[:, 12:13], lg[:, 2:10],
                                          ALU.mult, ALU.add), "v")
        if DBG.get("dump_cmb"):
            dcmb = ctx.buf("l0f_cmb", (128, NT * 8), F32, write=True)
            p.dma(("s", "o0"), dcmb[:, :], cmb[:, :, :].rearrange("p q e -> p (q e)"),
                  deps=(p.tip("v"),))
        p.bar()
        subR.close()
        w2_ap = ctx.W("L0_ffn_W2_experts").rearrange("e (k p) m -> e p k m", p=128)
        for e in range(8):
            subW2 = contextlib.ExitStack()
            w2 = p.sb("w2", (128, 16, D), BF16, stack=subW2)
            dw2 = p.dma(("g", "w1"), w2[:, :, :], w2_ap[e])
            psB_free = None
            for q in range(NT):
                c = q % 2
                qs = slice(q * 128, (q + 1) * 128)
                mmd = None
                for h0 in range(2):
                    pairs = [(hbuf[:, k2, qs], w2[:, k2, h0 * 512:(h0 + 1) * 512])
                             for k2 in range(16)]
                    mmd = acc_mm(p, psC[:, h0 * 512:(h0 + 1) * 512], pairs,
                                 deps=(dw2, psB_free, p.tip("v")))
                p.need("v", mmd, ob2_free[c])
                d_ob = p.mark(v.tensor_scalar_mul(ob2[c][:, :], psC[:, :],
                                                  cmb[:, q, e:e + 1]), "v")
                psB_free = d_ob
                ob2_free[c] = p.dma(("g", f"x{c}"), accd[qs, :], ob2[c][:, :],
                                    deps=(d_ob,), accum_op=ALU.add)
            p.bar()
            subW2.close()
        p.bar()
        subH.close()
        n2bc, dn2 = bcast_row(ctx, "L0_ffn_norm2_w", psC)
        xt = [p.sb("xt", (128, D), F32) for _ in range(2)]
        at = [p.sb("at", (128, D), F32) for _ in range(2)]
        st4b = p.sb("st4b", (128, 4), F32)
        sqb = p.sb("sqb", (128, D), F32)
        fr = [None, None]
        rprev = None
        for q in range(NT):
            c = q % 2
            qs = slice(q * 128, (q + 1) * 128)
            d1 = p.dma(("s", f"a{c}"), xt[c][:, :], x1d[qs, :], deps=(fr[c],))
            d2 = p.dma(("s", f"b{c}"), at[c][:, :], accd[qs, :], deps=(fr[c],))
            p.need("v", d1, d2)
            d3 = p.mark(v.tensor_tensor(xt[c][:, :], xt[c][:, :], at[c][:, :],
                                        ALU.add), "v")
            d4 = rms_tile(ctx, xt[c][:, :], n2bc, st4b, sqb, 1e-6,
                          deps=(d3, rprev, dn2))
            rprev = d4
            fr[c] = p.dma(("s", f"o{c}"), brB[qs, :], xt[c][:, :], deps=(d4,))


F_STAGES[0] = s_ffn0

def s_ffn1(ctx: Ctx, l: int):
    p = ctx.p
    v, a, pe = p.eng["v"], p.eng["a"], p.eng["pe"]
    xn_T = ctx.buf("xn1_T", (D, T), BF16)
    xn_f = ctx.buf("xn1_f", (T, D), F32)
    brB = ctx.buf("brB1", (T, D), F32, write=True)
    accd = ctx.buf("l1f_acc", (T, D), F32, write=True)
    with p.stage("ffn1"):
        pools = mha_pools(p)
        psC = p.ps("psC", (128, 1024), F32)
        mlaf = p.sb("mlaf", (128, KD, T), BF16)
        cmb = p.sb("cmb", (128, NT, 8), F32)
        cmbT = p.sb("cmbT", (8, NT * 128), BF16)
        cmb_b = p.sb("cmb_b", (128, 8), BF16)
        subQ = contextlib.ExitStack()
        qf = p.sb("qf", (128, 4, T), BF16, stack=subQ)
        kf = p.sb("kf", (128, 4, T), BF16, stack=subQ)
        vb = p.sb("vb", (128, NT, 512), BF16, stack=subQ)
        subX = contextlib.ExitStack()
        xnf = p.sb("xnf", (128, KD, T), BF16, stack=subX)
        dxn = p.dma(("s", "ax"), xnf[:, :, :],
                    xn_T.rearrange("(k p) t -> p k t", p=128))
        wq, dwq = load_w(ctx, "L1_ffn_q_w_c", 512, stream=("g", "w0"), stack=subX)
        wk, dwk = load_w(ctx, "L1_ffn_k_w", 512, stream=("g", "w1"), stack=subX)
        wv, dwv = load_w(ctx, "L1_ffn_v_w", 512, stream=("g", "w2"), stack=subX)
        qbf = p.sb("qbf", (128, 4), F32, stack=subX)
        kbf = p.sb("kbf", (128, 4), F32, stack=subX)
        dqb = p.dma(("s", "r0"), qbf[:, :], ctx.W("L1_ffn_q_b_f")[:, :])
        dkb = p.dma(("s", "r1"), kbf[:, :], ctx.W("L1_ffn_k_b_f")[:, :])
        vbc, dvb0 = bcast_row(ctx, "L1_ffn_v_b", psC, stack=subX)
        dq = proj_fm(ctx, qf, wq, 0, xnf, (dwq, dxn, dqb), pools["psA"],
                     pools["psA_free"], bias_fm=qbf)
        dk_ = proj_fm(ctx, kf, wk, 0, xnf, (dwk, dxn, dkb), pools["psA"],
                      pools["psA_free"], bias_fm=kbf)
        dv_ = proj_tm(ctx, vb, wv, 0, 512, xnf, (dwv, dxn, dvb0), pools["psA"],
                      pools["psA_free"], bias_bc=vbc)
        p.bar()
        subX.close()
        subB = contextlib.ExitStack()
        wo, dwo = load_w(ctx, "L1_ffn_o_w_c", D, stream=("g", "w0"), stack=subB)
        obc, dob = bcast_row(ctx, "L1_ffn_o_b", psC, stack=subB)
        ob = [p.sb("ob", (128, D), F32, stack=subB) for _ in range(2)]
        otb = p.sb("otb", (128, 512), BF16, stack=subB)
        mlb = p.sb("mlb", (128, D), BF16, stack=subB)
        lnfm = p.sb("lnfm", (128, 4, 128), BF16, stack=subB)
        state = {"psC_free": dob, "fm_mm_prev": None, "cp_prev": None,
                 "ob_free": [None, None]}

        def mla_hook(q, c, d_ob):
            s = slice(q * 128, (q + 1) * 128)
            p.need("a", d_ob)
            d_cb = p.mark(a.copy(mlb[:, :], ob[c][:, :]), "a")
            for k in range(KD):
                tc = k % 2
                p.need("pe", d_cb, pools["psT_free"][tc])
                mmt = p.mark(pe.transpose(pools["psT"][tc][:, 0:128],
                                          mlb[:, k * 128:(k + 1) * 128],
                                          ctx.cst["ident_bf"][:, :]), "pe")
                ev = evac(p, "v" if tc == 0 else "a", mlaf[:, k, s],
                          pools["psT"][tc][:, 0:128], deps=(mmt,))
                pools["psT_free"][tc] = ev
            return (d_ob, d_cb)

        def emit_out(q, c, ot, d_o):
            return tm_to_fm_out(ctx, pools, wo, dwo, obc, dob, psC, otb, lnfm,
                                ob, None, state, q, c, ot, d_o, nko=4, ncol=D,
                                extra_add=mla_hook, store=False)

        subM = contextlib.ExitStack()
        mha_core(ctx, qf, kf, vb, pools, H=16, dk=32, dv=32, causal=False,
                 scale=1.0 / float(np.sqrt(32.0)),
                 in_deps=dq + dk_ + dv_, emit_out=emit_out, arena=subM)
        p.bar()
        subM.close()
        subB.close()
        subQ.close()
        subR = contextlib.ExitStack()
        rw = p.sb("rw", (128, KD, 8), BF16, stack=subR)
        drw = p.dma(("g", "w0"), rw[:, :, :],
                    ctx.W("L1_ffn_router_w").rearrange("(k p) m -> p k m", p=128))
        rbc, drb = bcast_row(ctx, "L1_ffn_router_b", psC, stack=subR)
        lg = p.sb("lg", (128, 16), F32, stack=subR)
        ev_prev = drb
        cT_prev = None
        for q in range(NT):
            qs = slice(q * 128, (q + 1) * 128)
            pairs = [(mlaf[:, k, qs], rw[:, k, :]) for k in range(KD)]
            mm = acc_mm(p, psC[:, 0:8], pairs, deps=(drw, ev_prev, p.tip("a")))
            p.need("v", mm, drb)
            d_lg = p.mark(v.tensor_tensor(lg[:, 0:8], psC[:, 0:8], rbc[:, 0:8],
                                          ALU.add), "v")
            ev_prev = d_lg
            d_m = p.mark(v.reduce_max(lg[:, 8:9], lg[:, 0:8], axis=AX.X,
                                      negate=True), "v")
            p.need("a", d_m)
            d_e = p.mark(a.activation(lg[:, 0:8], lg[:, 0:8], AF.Exp,
                                      bias=lg[:, 8:9], accum_out=lg[:, 9:10]), "a")
            p.need("v", d_e)
            v.reciprocal(lg[:, 10:11], lg[:, 9:10])
            v.tensor_scalar_mul(lg[:, 0:8], lg[:, 0:8], lg[:, 10:11])
            v.reduce_max(lg[:, 8:9], lg[:, 0:8], axis=AX.X)
            v.tensor_scalar(cmb[:, q, :], lg[:, 0:8], lg[:, 8:9], None, ALU.is_ge)
            v.scalar_tensor_tensor(lg[:, 0:8], cmb[:, q, :], -1.0e30, lg[:, 0:8],
                                   ALU.mult, ALU.add)
            v.reduce_max(lg[:, 9:10], lg[:, 0:8], axis=AX.X)
            v.tensor_scalar(lg[:, 0:8], lg[:, 0:8], lg[:, 9:10], None, ALU.is_ge)
            v.tensor_scalar_mul(lg[:, 0:8], lg[:, 0:8], lg[:, 9:10])
            v.scalar_tensor_tensor(cmb[:, q, :], cmb[:, q, :], lg[:, 8:9],
                                   lg[:, 0:8], ALU.mult, ALU.add)
            v.tensor_tensor(lg[:, 11:12], lg[:, 8:9], lg[:, 9:10], ALU.add)
            v.tensor_scalar_add(lg[:, 11:12], lg[:, 11:12], 1.0e-8)
            v.reciprocal(lg[:, 12:13], lg[:, 11:12])
            d_w = p.mark(v.tensor_scalar_mul(cmb[:, q, :], cmb[:, q, :],
                                             lg[:, 12:13]), "v")
            p.need("v", d_w)
            d_cb = p.mark(v.tensor_copy(cmb_b[:, :], cmb[:, q, :]), "v")
            p.need("pe", d_cb, cT_prev)
            mmt = p.mark(pe.transpose(pools["psT"][0][0:8, 0:128], cmb_b[:, :],
                                      ctx.cst["ident_bf"][:, :]), "pe")
            cT_prev = evac(p, "a", cmbT[0:8, qs], pools["psT"][0][0:8, 0:128],
                           deps=(mmt, pools["psT_free"][0]))
            pools["psT_free"][0] = cT_prev
        p.bar()
        subR.close()
        subH = contextlib.ExitStack()
        hbuf = p.sb("hbuf", (128, 16, T), BF16, stack=subH)
        ob2 = [p.sb("ob2", (128, D), F32, stack=subH) for _ in range(2)]
        b1f = p.sb("b1f", (128, 16), F32, stack=subH)
        ob2_free = [None, None]
        w1_ap = ctx.W("L1_ffn_w1").rearrange("e (k p) m -> e p k m", p=128)
        w2_ap = ctx.W("L1_ffn_w2").rearrange("e (k p) m -> e p k m", p=128)
        b1_ap = ctx.W("L1_ffn_b1_f")
        sw_state = {}
        for e in range(8):
            subW = contextlib.ExitStack()
            w1 = p.sb("w1", (128, KD, 2048), BF16, stack=subW)
            dw1 = p.dma(("g", "w1"), w1[:, :, :], w1_ap[e])
            db1 = p.dma(("s", "r0"), b1f[:, :], b1_ap[e], deps=(p.tip("a"),))
            proj_fm(ctx, hbuf, w1, 0, mlaf, (dw1, db1, p.tip("v")), pools["psA"],
                    pools["psA_free"], bias_fm=b1f, act=AF.Gelu)
            p.bar()
            subW.close()
            subW2 = contextlib.ExitStack()
            w2 = p.sb("w2", (128, 16, D), BF16, stack=subW2)
            dw2 = p.dma(("g", "w1"), w2[:, :, :], w2_ap[e])
            psB_free = None
            for q in range(NT):
                c = q % 2
                qs = slice(q * 128, (q + 1) * 128)
                mmd = None
                for h0 in range(2):
                    pairs = [(hbuf[:, k2, qs], w2[:, k2, h0 * 512:(h0 + 1) * 512])
                             for k2 in range(16)]
                    mmd = acc_mm(p, psC[:, h0 * 512:(h0 + 1) * 512], pairs,
                                 deps=(dw2, psB_free, p.tip("a")))
                p.need("v", mmd, ob2_free[c])
                d_ob = p.mark(v.tensor_scalar_mul(ob2[c][:, :], psC[:, :],
                                                  cmb[:, q, e:e + 1]), "v")
                psB_free = d_ob
                if e == 0:
                    ob2_free[c] = p.dma(("s", f"o{c}"), accd[qs, :], ob2[c][:, :],
                                        deps=(d_ob,))
                else:
                    ob2_free[c] = p.dma(("g", f"x{c}"), accd[qs, :], ob2[c][:, :],
                                        deps=(d_ob,), accum_op=ALU.add)
            p.bar()
            subW2.close()
        p.bar()
        subH.close()
        b2w = p.sb("b2w", (8, D), BF16)
        db2 = p.dma(("g", "w1"), b2w[:, :], ctx.W("L1_ffn_b2")[:, :])
        xt = [p.sb("xt", (128, D), F32) for _ in range(2)]
        at = [p.sb("at", (128, D), F32) for _ in range(2)]
        fr = [None, None]
        psF = None
        for q in range(NT):
            c = q % 2
            qs = slice(q * 128, (q + 1) * 128)
            d1 = p.dma(("s", f"a{c}"), xt[c][:, :], xn_f[qs, :], deps=(fr[c],))
            d2 = p.dma(("s", f"b{c}"), at[c][:, :], accd[qs, :], deps=(fr[c],))
            mmb = None
            for h0 in range(2):
                mmb = acc_mm(p, psC[:, h0 * 512:(h0 + 1) * 512],
                             [(cmbT[0:8, qs], b2w[0:8, h0 * 512:(h0 + 1) * 512])],
                             deps=(db2, psF, p.tip("a")))
            p.need("v", mmb, d1, d2)
            d3 = p.mark(v.tensor_tensor(at[c][:, :], at[c][:, :], psC[:, :],
                                        ALU.add), "v")
            psF = d3
            d4 = p.mark(v.scalar_tensor_tensor(xt[c][:, :], at[c][:, :], 0.3,
                                               xt[c][:, :], ALU.mult, ALU.add), "v")
            fr[c] = p.dma(("s", f"o{c}"), brB[qs, :], xt[c][:, :], deps=(d4,))


F_STAGES[1] = s_ffn1

def rope_half_tm(ctx, orr, xr, cosb, sinb, scrA, scrB, H, deps=()):
    p = ctx.p
    v = p.eng["v"]
    x1 = xr[:, :, 0:16]
    x2 = xr[:, :, 16:32]
    c1 = cosb[:, 0:16].unsqueeze(1).broadcast_to((128, H, 16))
    c2 = cosb[:, 16:32].unsqueeze(1).broadcast_to((128, H, 16))
    s1 = sinb[:, 0:16].unsqueeze(1).broadcast_to((128, H, 16))
    s2 = sinb[:, 16:32].unsqueeze(1).broadcast_to((128, H, 16))
    sA = scrA[:, 0:H * 16].rearrange("p (h f) -> p h f", h=H)
    sB = scrB[:, 0:H * 16].rearrange("p (h f) -> p h f", h=H)
    p.need("v", *deps)
    v.tensor_tensor(sA, x1, c1, ALU.mult)
    v.tensor_tensor(sB, x2, s1, ALU.mult)
    d1 = p.mark(v.tensor_tensor(orr[:, :, 0:16], sA, sB, ALU.subtract), "v")
    v.tensor_tensor(sA, x2, c2, ALU.mult)
    v.tensor_tensor(sB, x1, s2, ALU.mult)
    d2 = p.mark(v.tensor_tensor(orr[:, :, 16:32], sA, sB, ALU.add), "v")
    return d2


def s_ffn2(ctx: Ctx, l: int):
    p = ctx.p
    v, a, pe = p.eng["v"], p.eng["a"], p.eng["pe"]
    xn_T = ctx.buf("xn2_T", (D, T), BF16)
    xn_f = ctx.buf("xn2_f", (T, D), F32)
    brB = ctx.buf("brB2", (T, D), F32, write=True)
    x1d = ctx.buf("l2f_x1", (T, D), F32, write=True)
    qcTd = ctx.buf("l2f_qcT", (D, T), BF16, write=True)
    with p.stage("ffn2"):
        pools = mha_pools(p)
        psC = p.ps("psC", (128, 1024), F32)
        xl2f = p.sb("xl2f", (128, KD, T), BF16)
        cmb = p.sb("cmb", (128, NT, 7), F32)
        subQ = contextlib.ExitStack()
        kf = p.sb("kf", (128, KD, T), BF16, stack=subQ)
        vb = p.sb("vb", (128, NT, D), BF16, stack=subQ)
        subX = contextlib.ExitStack()
        xl1f = p.sb("xl1f", (128, KD, T), BF16, stack=subX)
        kvf = p.sb("kvf", (128, 2, T), BF16, stack=subX)
        qlf = p.sb("qlf", (128, 2, T), BF16, stack=subX)
        subX2 = contextlib.ExitStack()
        ln1bc, dln1 = bcast_row(ctx, "L2_ffn_ln1_w", psC, stack=subX2)
        rope_c = p.sb("rope_c", (128, NT, 32), F32, stack=subX2)
        rope_s = p.sb("rope_s", (128, NT, 32), F32, stack=subX2)
        drc = p.dma(("s", "r0"), rope_c[:, :, :],
                    ctx.W("cst_rope2_cos").rearrange("(q p) f -> p q f", p=128))
        drs = p.dma(("s", "r1"), rope_s[:, :, :],
                    ctx.W("cst_rope2_sin").rearrange("(q p) f -> p q f", p=128))
        xt = [p.sb("xt", (128, D), F32, stack=subX2) for _ in range(2)]
        xb_ = p.sb("xb_", (128, D), BF16, stack=subX2)
        st4 = p.sb("st4", (128, 4), F32, stack=subX2)
        sqs = p.sb("sqs", (128, D), F32, stack=subX2)
        scrA = p.sb("scrA", (128, 512), F32, stack=subX2)
        scrB = p.sb("scrB", (128, 512), F32, stack=subX2)
        qtm = p.sb("qtm", (128, 512), F32, stack=subX2)
        qbt = p.sb("qbt", (128, D), BF16, stack=subX2)
        tbq = [p.sb("tbq", (128, KD, 128), BF16, stack=subX2) for _ in range(2)]
        xt_free = [None, None]
        cast_prev = None
        stat_prev = None
        for q in range(NT):
            c = q % 2
            qs = slice(q * 128, (q + 1) * 128)
            dxl = p.dma(("s", f"a{c}"), xt[c][:, :], xn_f[qs, :], deps=(xt_free[c],))
            d_st = ln_stats(ctx, xt[c][:, :], st4[:, 0:2], st4[:, 2:4], sqs, 1e-5,
                            deps=(dxl, stat_prev))
            p.need("v", *d_st, dln1, cast_prev)
            d_n = p.mark(v.tensor_scalar(xt[c][:, :], xt[c][:, :], st4[:, 0:1],
                                         st4[:, 2:3], ALU.subtract, ALU.mult), "v")
            stat_prev = d_n
            d_n2 = p.mark(v.tensor_tensor(xt[c][:, :], xt[c][:, :], ln1bc[:, :],
                                          ALU.mult), "v")
            p.need("a", d_n2, cast_prev)
            d_cb = p.mark(a.copy(xb_[:, :], xt[c][:, :]), "a")
            xt_free[c] = d_cb
            for k in range(KD):
                tc = k % 2
                p.need("pe", d_cb, pools["psT_free"][tc])
                mmt = p.mark(pe.transpose(pools["psT"][tc][:, 0:128],
                                          xb_[:, k * 128:(k + 1) * 128],
                                          ctx.cst["ident_bf"][:, :]), "pe")
                if k == KD - 1:
                    cast_prev = mmt
                ev = evac(p, "v" if tc == 0 else "a", xl1f[:, k, qs],
                          pools["psT"][tc][:, 0:128], deps=(mmt,))
                pools["psT_free"][tc] = ev
        if DBG.get("f2", 9) < 1:
            p.bar(); subX2.close(); subX.close(); subQ.close(); return
        dx1f = (p.tip("v"), p.tip("a"))
        subW = contextlib.ExitStack()
        wkvd, dwkvd = load_w(ctx, "L2_ffn_kv_d", 256, stream=("g", "w0"), stack=subW)
        wqd, dwqd = load_w(ctx, "L2_ffn_q_d", 256, stream=("g", "w1"), stack=subW)
        d_kv = proj_fm(ctx, kvf, wkvd, 0, xl1f, (dwkvd,) + dx1f, pools["psA"],
                       pools["psA_free"])
        d_ql = proj_fm(ctx, qlf, wqd, 0, xl1f, (dwqd,) + dx1f, pools["psA"],
                       pools["psA_free"])
        p.bar()
        subW.close()
        subW = contextlib.ExitStack()
        wvu, dwvu = load_w(ctx, "L2_ffn_v_u", D, stream=("g", "w1"), stack=subW)
        proj_tm(ctx, vb, wvu, 0, D, kvf, (dwvu,) + d_kv, pools["psA"],
                pools["psA_free"])
        p.bar()
        subW.close()
        if DBG.get("f2", 9) < 2:
            p.bar(); subX2.close(); subX.close(); subQ.close(); return
        qcT3 = qcTd.rearrange("(k p) t -> p k t", p=128)
        for pass_i in range(2):
            subW = contextlib.ExitStack()
            if pass_i == 0:
                wnr, dwnr = load_w(ctx, "L2_ffn_q_u", D, stream=("g", "w0"),
                                   stack=subW)
                wro, dwro = load_w(ctx, "L2_ffn_rope_q_w", 512, stream=("g", "w1"),
                                   stack=subW)
                srcn, nkn, dn_ = qlf, 2, d_ql
                srcr, nkr, dr_ = qlf, 2, d_ql
            else:
                wnr, dwnr = load_w(ctx, "L2_ffn_k_u", D, stream=("g", "w0"),
                                   stack=subW)
                wro, dwro = load_w(ctx, "L2_ffn_rope_k_w", 512, stream=("g", "w1"),
                                   stack=subW)
                srcn, nkn, dn_ = kvf, 2, d_kv
                srcr, nkr, dr_ = xl1f, KD, dx1f
            rp_prev = None
            q_prev = None
            nr_prev = None
            tb_free = [None, None]
            for q in range(NT):
                c = q % 2
                qs = slice(q * 128, (q + 1) * 128)
                mmn = None
                for h0 in range(2):
                    pairs = [(srcn[:, k, qs], wnr[:, k, h0 * 512:(h0 + 1) * 512])
                             for k in range(nkn)]
                    mmn = acc_mm(p, psC[:, h0 * 512:(h0 + 1) * 512], pairs,
                                 deps=(dwnr, nr_prev) + tuple(dn_))
                pairs = [(srcr[:, k, qs], wro[:, k, 0:512]) for k in range(nkr)]
                mmr = acc_mm(p, pools["psA"][0][:, :], pairs,
                             deps=(dwro, q_prev) + tuple(dr_) +
                                  tuple(pools["psA_free"][0]))
                p.need("v", mmr, drc, drs)
                d_q = p.mark(v.tensor_copy(qtm[:, :], pools["psA"][0][:, :]), "v")
                q_prev = d_q
                pools["psA_free"][0] = (d_q,)
                qbt3 = qbt.rearrange("p (h f) -> p h f", h=16)
                d_rp = rope_half_tm(ctx, qbt3[:, :, 0:32],
                                    qtm.rearrange("p (h f) -> p h f", h=16),
                                    rope_c[:, q, :], rope_s[:, q, :],
                                    scrA, scrB, 16, deps=(d_q, rp_prev))
                psC3 = psC.rearrange("p (h f) -> p h f", h=16)
                p.need("v", mmn, rp_prev)
                d_nr = p.mark(v.tensor_copy(qbt3[:, :, 32:64], psC3[:, :, 32:64]),
                              "v")
                nr_prev = d_nr
                last_cp = None
                for k in range(KD):
                    tc = k % 2
                    p.need("pe", d_rp, d_nr, pools["psT_free"][tc])
                    mmt = p.mark(pe.transpose(pools["psT"][tc][:, 0:128],
                                              qbt[:, k * 128:(k + 1) * 128],
                                              ctx.cst["ident_bf"][:, :]), "pe")
                    if k == KD - 1:
                        rp_prev = mmt
                    if pass_i == 0:
                        ev = evac(p, "a", tbq[c][:, k, :],
                                  pools["psT"][tc][:, 0:128],
                                  deps=(mmt, tb_free[c]))
                        last_cp = ev
                    else:
                        ev = evac(p, "v" if tc == 0 else "a", kf[:, k, qs],
                                  pools["psT"][tc][:, 0:128], deps=(mmt,))
                    pools["psT_free"][tc] = ev
                if pass_i == 0:
                    tb_free[c] = p.dma(("s", f"tc{c}"), qcT3[:, :, qs],
                                       tbq[c][:, :, :], deps=(last_cp,))
            p.bar()
            subW.close()
        p.bar()
        subX2.close()
        p.bar()
        subX.close()
        if DBG.get("f2", 9) < 3:
            p.bar(); subQ.close(); return
        subB = contextlib.ExitStack()
        wo, dwo = load_w(ctx, "L2_ffn_o_w", D, stream=("g", "w0"), stack=subB)
        ln2bc, dln2 = bcast_row(ctx, "L2_ffn_ln2_w", psC, stack=subB)
        ob = [p.sb("ob", (128, D), F32, stack=subB) for _ in range(2)]
        xnt = [p.sb("xnt", (128, D), F32, stack=subB) for _ in range(2)]
        otb = p.sb("otb", (128, D), BF16, stack=subB)
        x1b = p.sb("x1b", (128, D), BF16, stack=subB)
        lnfm = p.sb("lnfm", (128, KD, 128), BF16, stack=subB)
        st4b = p.sb("st4b", (128, 4), F32, stack=subB)
        sqb = p.sb("sqb", (128, D), F32, stack=subB)
        state = {"psC_free": dln2, "fm_mm_prev": None, "cp_prev": None,
                 "ob_free": [None, None]}
        xnt_free = [None, None]
        x1t_prev = [None]

        def x1_hook(q, c, d_ob):
            s = slice(q * 128, (q + 1) * 128)
            dxl = p.dma(("s", f"b{c}"), xnt[c][:, :], xn_f[s, :],
                        deps=(xnt_free[c],))
            p.need("v", d_ob, dxl)
            d1 = p.mark(v.tensor_tensor(ob[c][:, :], ob[c][:, :], xnt[c][:, :],
                                        ALU.add), "v")
            xnt_free[c] = d1
            d_st = ln_stats(ctx, ob[c][:, :], st4b[:, 0:2], st4b[:, 2:4], sqb,
                            1e-5, deps=(d1, x1t_prev[0], dln2))
            p.need("v", *d_st)
            d_n = p.mark(v.tensor_scalar(sqb[:, :], ob[c][:, :], st4b[:, 0:1],
                                         st4b[:, 2:3], ALU.subtract, ALU.mult), "v")
            x1t_prev[0] = d_n
            d_n2 = p.mark(v.tensor_tensor(sqb[:, :], sqb[:, :], ln2bc[:, :],
                                          ALU.mult), "v")
            p.need("a", d_n2)
            d_cb = p.mark(a.copy(x1b[:, :], sqb[:, :]), "a")
            for k in range(KD):
                tc = k % 2
                p.need("pe", d_cb, pools["psT_free"][tc])
                mmt = p.mark(pe.transpose(pools["psT"][tc][:, 0:128],
                                          x1b[:, k * 128:(k + 1) * 128],
                                          ctx.cst["ident_bf"][:, :]), "pe")
                ev = evac(p, "v" if tc == 0 else "a", xl2f[:, k, s],
                          pools["psT"][tc][:, 0:128], deps=(mmt,))
                pools["psT_free"][tc] = ev
            return (d1, d_cb, p.tip("v"))

        def emit_out(q, c, ot, d_o):
            return tm_to_fm_out(ctx, pools, wo, dwo, None, dln2, psC, otb, lnfm,
                                ob, x1d, state, q, c, ot, d_o,
                                extra_add=x1_hook)

        subM = contextlib.ExitStack()
        mha_core(ctx, None, kf, vb, pools, H=16, dk=64, dv=64, causal=True,
                 scale=0.125, in_deps=(p.tip("v"), p.tip("a")),
                 emit_out=emit_out, q_dram=qcTd, arena=subM)
        p.bar()
        subM.close()
        subB.close()
        subQ.close()
        if DBG.get("f2", 9) < 4:
            return
        subR = contextlib.ExitStack()
        rw = p.sb("rw", (128, KD, 7), BF16, stack=subR)
        drw = p.dma(("g", "w0"), rw[:, :, :],
                    ctx.W("L2_ffn_router_w").rearrange("(k p) m -> p k m", p=128))
        rbc, drb = bcast_row(ctx, "L2_ffn_routing_bias", psC, stack=subR)
        lg = p.sb("lg", (128, 16), F32, stack=subR)
        ev_prev = drb
        for q in range(NT):
            qs = slice(q * 128, (q + 1) * 128)
            pairs = [(xl2f[:, k, qs], rw[:, k, :]) for k in range(KD)]
            mm = acc_mm(p, psC[:, 0:7], pairs, deps=(drw, ev_prev, p.tip("a")))
            p.need("v", mm, drb)
            d_lg = p.mark(v.tensor_tensor(lg[:, 0:7], psC[:, 0:7], rbc[:, 0:7],
                                          ALU.add), "v")
            ev_prev = d_lg
            p.need("a", d_lg)
            d_sg = p.mark(a.activation(lg[:, 0:7], lg[:, 0:7], AF.Sigmoid), "a")
            p.need("v", d_sg)
            v.reduce_max(lg[:, 8:9], lg[:, 0:7], axis=AX.X)
            v.tensor_scalar(cmb[:, q, :], lg[:, 0:7], lg[:, 8:9], None, ALU.is_ge)
            v.scalar_tensor_tensor(lg[:, 0:7], cmb[:, q, :], -1.0e30, lg[:, 0:7],
                                   ALU.mult, ALU.add)
            v.reduce_max(lg[:, 9:10], lg[:, 0:7], axis=AX.X)
            v.tensor_scalar(lg[:, 0:7], lg[:, 0:7], lg[:, 9:10], None, ALU.is_ge)
            v.tensor_scalar_mul(lg[:, 0:7], lg[:, 0:7], lg[:, 9:10])
            v.scalar_tensor_tensor(cmb[:, q, :], cmb[:, q, :], lg[:, 8:9],
                                   lg[:, 0:7], ALU.mult, ALU.add)
            v.tensor_tensor(lg[:, 10:11], lg[:, 8:9], lg[:, 9:10], ALU.add)
            v.reciprocal(lg[:, 11:12], lg[:, 10:11])
            p.mark(v.tensor_scalar_mul(cmb[:, q, :], cmb[:, q, :],
                                       lg[:, 11:12]), "v")
        p.bar()
        subR.close()
        if DBG.get("f2", 9) < 5:
            return
        sw_state = {}
        ob_free = [None, None]
        rg_ap = ctx.W("L2_ffn_routed_gate").rearrange("e (k p) m -> e p k m", p=128)
        ru_ap = ctx.W("L2_ffn_routed_up").rearrange("e (k p) m -> e p k m", p=128)
        rd_ap = ctx.W("L2_ffn_routed_down").rearrange("e (k p) m -> e p k m", p=128)
        for e in range(-1, 7):
            subH = contextlib.ExitStack()
            hbuf = p.sb("hbuf", (128, 16, T), BF16, stack=subH)
            obx = [p.sb("obx", (128, D), F32, stack=subH) for _ in range(2)]
            x1t = [p.sb("x1t", (128, D), F32, stack=subH) for _ in range(2)]
            subW = contextlib.ExitStack()
            if e < 0:
                wg, dwg = load_w(ctx, "L2_ffn_shared_gate", 2048,
                                 stream=("g", "w1"), stack=subW)
                wu, dwu = load_w(ctx, "L2_ffn_shared_up", 2048,
                                 stream=("g", "w2"), stack=subW)
            else:
                wg = p.sb("wg", (128, KD, 2048), BF16, stack=subW)
                wu = p.sb("wu", (128, KD, 2048), BF16, stack=subW)
                dwg = p.dma(("g", "w1"), wg[:, :, :], rg_ap[e])
                dwu = p.dma(("g", "w2"), wu[:, :, :], ru_ap[e])
            swiglu_fm(ctx, hbuf, wg, wu, xl2f, (dwg, dwu, p.tip("v"), p.tip("a")),
                      pools["psA"], pools["psA_free"], sw_state)
            p.bar()
            subW.close()
            subW2 = contextlib.ExitStack()
            if e < 0:
                wd, dwd = load_w(ctx, "L2_ffn_shared_down", D,
                                 stream=("g", "w1"), stack=subW2)
            else:
                wd = p.sb("wd", (128, 16, D), BF16, stack=subW2)
                dwd = p.dma(("g", "w1"), wd[:, :, :], rd_ap[e])
            psB_free = None
            for q in range(NT):
                c = q % 2
                qs = slice(q * 128, (q + 1) * 128)
                mmd = None
                for h0 in range(2):
                    pairs = [(hbuf[:, k2, qs], wd[:, k2, h0 * 512:(h0 + 1) * 512])
                             for k2 in range(16)]
                    mmd = acc_mm(p, psC[:, h0 * 512:(h0 + 1) * 512], pairs,
                                 deps=(dwd, psB_free, p.tip("v"), p.tip("a")))
                if e < 0:
                    dxt = p.dma(("s", f"a{c}"), x1t[c][:, :], x1d[qs, :],
                                deps=(ob_free[c],))
                    p.need("v", mmd, dxt, ob_free[c])
                    d_ob = p.mark(v.tensor_tensor(obx[c][:, :], psC[:, :],
                                                  x1t[c][:, :], ALU.add), "v")
                    psB_free = d_ob
                    ob_free[c] = p.dma(("s", f"o{c}"), brB[qs, :], obx[c][:, :],
                                       deps=(d_ob,))
                else:
                    p.need("v", mmd, ob_free[c])
                    d_ob = p.mark(v.tensor_scalar_mul(obx[c][:, :], psC[:, :],
                                                      cmb[:, q, e:e + 1]), "v")
                    psB_free = d_ob
                    ob_free[c] = p.dma(("g", f"x{c}"), brB[qs, :], obx[c][:, :],
                                       deps=(d_ob,), accum_op=ALU.add)
            p.bar()
            subW2.close()
            subH.close()


F_STAGES[2] = s_ffn2






def s_ffn3(ctx: Ctx, l: int):
    p = ctx.p
    v, a = p.eng["v"], p.eng["a"]
    xn_T = ctx.buf("xn3_T", (D, T), BF16)
    xn_lo = ctx.buf("xn3_loT", (D, T), BF16)
    xn_b = ctx.buf("xn3_b", (T, D), BF16)
    brB = ctx.buf("brB3", (T, D), F32, write=True)
    with p.stage("ffn3"):
        psA = [p.ps("psA", (128, 512), F32) for _ in range(2)]
        psB = p.ps("psB", (128, 1024), F32)
        psT = p.ps("psT", (128, 512), BF16)
        xnf = p.sb("xnf", (128, KD, T), BF16)
        dxn = p.dma(("s", "ax"), xnf[:, :, :],
                    xn_T.rearrange("(k p) t -> p k t", p=128))
        xsf = p.sb("xsf", (128, KD, T), BF16)
        cmb = p.sb("cmb", (128, NT, 8), F32)
        sc = p.sb("sc", (128, NT), F32)
        lg = p.sb("lg", (128, 8), F32)
        ob = [p.sb("ob", (128, D), F32) for _ in range(2)]
        subR = contextlib.ExitStack()
        xlof = p.sb("xlof", (128, KD, T), BF16, stack=subR)
        dlo = p.dma(("s", "ay"), xlof[:, :, :],
                    xn_lo.rearrange("(k p) t -> p k t", p=128))
        rw = p.sb("rw", (128, KD, 8), F32, stack=subR)
        rwh = p.sb("rwh", (128, KD, 8), BF16, stack=subR)
        rwl = p.sb("rwl", (128, KD, 8), BF16, stack=subR)
        drw = p.dma(("s", "w0"), rw[:, :, :],
                    ctx.W("L3_ffn_router_w").rearrange("(k p) m -> p k m", p=128))
        p.need("a", drw)
        d_h = p.mark(a.copy(rwh[:, :, :], rw[:, :, :]), "a")
        p.need("v", drw, d_h)
        d_l = p.mark(v.tensor_tensor(rwl[:, :, :], rw[:, :, :], rwh[:, :, :],
                                     ALU.subtract), "v")
        ev_prev = None
        for q in range(NT):
            qs = slice(q * 128, (q + 1) * 128)
            pairs = []
            for k in range(KD):
                pairs.append((xnf[:, k, qs], rwh[:, k, :]))
            for k in range(KD):
                pairs.append((xlof[:, k, qs], rwh[:, k, :]))
            for k in range(KD):
                pairs.append((xnf[:, k, qs], rwl[:, k, :]))
            mm = acc_mm(p, psB[:, 0:8], pairs, deps=(dxn, dlo, d_h, d_l, ev_prev))
            p.need("v", mm)
            d_lg = p.mark(v.tensor_copy(lg[:, :], psB[:, 0:8]), "v")
            ev_prev = d_lg
            d_rm = p.mark(v.reduce_max(sc[:, q:q + 1], lg[:, :], axis=AX.X), "v")
            p.need("v", d_rm)
            d_msk = p.mark(v.tensor_scalar(cmb[:, q, :], lg[:, :], sc[:, q:q + 1],
                                           None, ALU.is_ge), "v")
            p.need("a", d_msk)
            p.mark(a.activation(sc[:, q:q + 1], sc[:, q:q + 1], AF.Sigmoid), "a")
        p.bar()
        subR.close()
        subX = contextlib.ExitStack()
        xb = [p.sb("xb", (128, D), BF16, stack=subX) for _ in range(2)]
        p.need("v", p.tip("a"))
        xb_free = [None, None]
        cp_prev = None
        for q in range(NT):
            c = q % 2
            qs = slice(q * 128, (q + 1) * 128)
            dxb = p.dma(("s", f"a{c}"), xb[c][:, :], xn_b[qs, :], deps=(xb_free[c],))
            p.need("v", dxb)
            d_xs = p.mark(v.tensor_scalar_mul(xb[c][:, :], xb[c][:, :],
                                              sc[:, q:q + 1]), "v")
            xb_free[c] = d_xs
            for k in range(KD):
                p.need("pe", d_xs, cp_prev)
                mmt = p.mark(p.eng["pe"].transpose(
                    psT[:, 0:128], xb[c][:, k * 128:(k + 1) * 128],
                    ctx.cst["ident_bf"][:, :]), "pe")
                if k == KD - 1:
                    xb_free[c] = mmt
                cp_prev = evac(p, "a", xsf[:, k, qs], psT[:, 0:128], deps=(mmt,))
        p.bar()
        subX.close()
        sw_state = {}
        ob_free = [None, None]
        for e in range(-1, 8):
            subH = contextlib.ExitStack()
            hbuf = p.sb("hbuf", (128, 16, T), BF16, stack=subH)
            subW = contextlib.ExitStack()
            if e < 0:
                wg, dwg = load_w(ctx, "L3_ffn_shared_gate", 2048,
                                 stream=("g", "w1"), stack=subW)
                wu, dwu = load_w(ctx, "L3_ffn_shared_up", 2048,
                                 stream=("g", "w2"), stack=subW)
                xin = xnf
                dxi = dxn
            else:
                wga = ctx.W("L3_ffn_expert_gate").rearrange(
                    "e (k p) m -> e p k m", p=128)
                wua = ctx.W("L3_ffn_expert_up").rearrange(
                    "e (k p) m -> e p k m", p=128)
                wg = p.sb("wg", (128, KD, 2048), BF16, stack=subW)
                wu = p.sb("wu", (128, KD, 2048), BF16, stack=subW)
                dwg = p.dma(("g", "w1"), wg[:, :, :], wga[e])
                dwu = p.dma(("g", "w2"), wu[:, :, :], wua[e])
                xin = xsf
                dxi = p.tip("a")
            swiglu_fm(ctx, hbuf, wg, wu, xin, (dwg, dwu, dxi), psA, psA_free=[(), ()],
                      state=sw_state)
            p.bar()
            subW.close()
            subW2 = contextlib.ExitStack()
            if e < 0:
                wd, dwd = load_w(ctx, "L3_ffn_shared_down", D,
                                 stream=("g", "w1"), stack=subW2)
            else:
                wda = ctx.W("L3_ffn_expert_down").rearrange(
                    "e (k p) m -> e p k m", p=128)
                wd = p.sb("wd", (128, 16, D), BF16, stack=subW2)
                dwd = p.dma(("g", "w1"), wd[:, :, :], wda[e])
            psB_free = None
            for q in range(NT):
                c = q % 2
                qs = slice(q * 128, (q + 1) * 128)
                mmd = None
                for h0 in range(2):
                    pairs = [(hbuf[:, k2, qs], wd[:, k2, h0 * 512:(h0 + 1) * 512])
                             for k2 in range(16)]
                    mmd = acc_mm(p, psB[:, h0 * 512:(h0 + 1) * 512], pairs,
                                 deps=(dwd, psB_free, p.tip("v")))
                p.need("v", mmd, ob_free[c])
                if e < 0:
                    d_ob = p.mark(v.tensor_copy(ob[c][:, :], psB[:, :]), "v")
                else:
                    d_ob = p.mark(v.tensor_scalar_mul(ob[c][:, :], psB[:, :],
                                                      cmb[:, q, e:e + 1]), "v")
                psB_free = d_ob
                if e < 0:
                    ob_free[c] = p.dma(("s", f"o{c}"), brB[qs, :], ob[c][:, :],
                                       deps=(d_ob,))
                else:
                    ob_free[c] = p.dma(("g", f"x{c}"), brB[qs, :], ob[c][:, :],
                                       deps=(d_ob,), accum_op=ALU.add)
            p.bar()
            subW2.close()
            subH.close()


F_STAGES[3] = s_ffn3



STAGES = {}


def register_stages():
    global STAGES
    STAGES = {}
    for l in range(4):
        STAGES[f"ln{l}"] = (s_ln, l)
        STAGES[f"res{l}"] = (s_res, l)
    for l, fn in A_STAGES.items():
        STAGES[f"attn{l}"] = (fn, l)
    for l, fn in F_STAGES.items():
        STAGES[f"ffn{l}"] = (fn, l)


def build_group(stage_names, host_arrays, fused=False):
    p = Prog()
    ctx = Ctx(p, host_arrays, fused)
    with p.top:
        load_constants(ctx)
        for sn in stage_names:
            fn, l = STAGES[sn]
            fn(ctx, l)
    return p, ctx


def run_group(stage_names, host_arrays, bufs, fused=False, core_ids=None, trace=False):
    core_ids = core_ids or list(range(N_CORES))
    p, ctx = build_group(stage_names, host_arrays, fused=fused)
    in_maps = []
    for i in range(len(core_ids)):
        m = {}
        for name, kind in ctx.kind.items():
            if kind == "ExternalInput":
                if name in host_arrays:
                    m[name] = host_arrays[name]
                else:
                    m[name] = bufs[name][i]
        in_maps.append(m)
    res = run_bass_kernel_spmd(p.nc, in_maps, core_ids=core_ids, trace=trace)
    for name, kind in ctx.kind.items():
        if kind == "ExternalOutput":
            sh, dt = ctx.buf_shape[name]
            bufs[name] = [np.asarray(res.results[i][name]) for i in range(len(core_ids))]
    return res



GROUPS_UNFUSED = [
    ["ln0", "attn0", "ffn0", "res0"],
    ["ln1", "attn1", "ffn1", "res1"],
    ["ln2", "attn2", "ffn2", "res2"],
    ["ln3", "attn3", "ffn3", "res3"],
]

ALL_STAGES = [s for g in GROUPS_UNFUSED for s in g]

FUSED = True


def kernel(**inputs):
    host = prep_weights(inputs)
    host.update(host_constants())
    register_stages()
    x = np.ascontiguousarray(np.asarray(inputs["L0_attn_x"], dtype=np.float32))
    bufs = {"X0": [np.ascontiguousarray(x[i]) for i in range(N_CORES)]}
    if FUSED:
        run_group(ALL_STAGES, host, bufs, fused=True)
    else:
        for g in GROUPS_UNFUSED:
            run_group(g, host, bufs, fused=False)
    out = np.stack(bufs["X4"], axis=0).astype(np.float32)
    return out
